# version 06
# baseline (speedup 1.0000x reference)
import contextlib
import numpy as np
import ml_dtypes
import concourse.bass as bass
import concourse.mybir as mybir
from concourse.bass_utils import run_bass_kernel_spmd

F32 = mybir.dt.float32
BF16 = mybir.dt.bfloat16
I32 = mybir.dt.int32
AF = mybir.ActivationFunctionType
ALU = mybir.AluOpType
AX = mybir.AxisListType
BF = ml_dtypes.bfloat16

NCORES = 8
C = 1024
NH = 16
DH = 64
FH = 3072
TB = 2048
BPC = 2
TPC = TB * BPC
SEQ2 = 256
NSTEP = 16
EPS = 1e-5
NEGBIG = -1e9
RCP127 = float(np.float32(1.0 / 127.0))

ENGS = ("sync", "gpsimd", "tensor", "vector", "scalar")
DBG = set()


class Emitter:

    def __init__(self, nc):
        self.nc = nc
        self.ops = {e: [] for e in ENGS}
        self.sems = {}
        self.cnt = {}
        self.producer = {}
        self.dma_sem = set()
        self.hw = {e: {} for e in ENGS}

    def sem(self, name):
        if name not in self.sems:
            self.sems[name] = self.nc.alloc_semaphore(name)
            self.cnt[name] = 0
        return self.sems[name]

    def emit(self, eng, fn, wait=(), inc=None, amt=1):
        for s, _v in wait:
            self.sem(s)
        if inc is not None:
            self.sem(inc)
            self.cnt[inc] += amt
            if amt >= 16:
                self.dma_sem.add(inc)
            p = self.producer.get(inc)
            if p is None:
                self.producer[inc] = eng
            elif p != eng:
                self.producer[inc] = "*"
        sems = self.sems
        hw = self.hw[eng]
        kept = []
        for s, v in wait:
            if v <= 0:
                continue
            if v <= hw.get(s, 0):
                continue
            if (s not in self.dma_sem and self.producer.get(s) == eng):
                continue
            hw[s] = v
            kept.append((s, v))

        def g(b, fn=fn, kept=tuple(kept), inc=inc, amt=amt):
            for s, v in kept:
                b.wait_ge(sems[s], v)
            ins = fn(b)
            if inc is not None:
                assert ins is not None, "inc requires an instruction"
                ins.then_inc(sems[inc], amt)
        self.ops[eng].append(g)

    def val(self, sem):
        self.sem(sem)
        return self.cnt[sem]

    def barrier(self, waits, engines=ENGS):
        for e in engines:
            self.emit(e, lambda b: None, wait=waits)

    def finalize(self):
        nc = self.nc
        with nc.Block() as block:
            @block.sync
            def _(b):
                for g in self.ops["sync"]:
                    g(b)

            @block.gpsimd
            def _(b):
                for g in self.ops["gpsimd"]:
                    g(b)

            @block.tensor
            def _(b):
                for g in self.ops["tensor"]:
                    g(b)

            @block.vector
            def _(b):
                for g in self.ops["vector"]:
                    g(b)

            @block.scalar
            def _(b):
                for g in self.ops["scalar"]:
                    g(b)



def make_consts(nc):
    con = {}
    con["ident"] = nc.inline_tensor(np.eye(128, dtype=BF), name="ident_c")
    con["diagmask"] = nc.inline_tensor(
        (np.triu(np.ones((128, 128), np.float32), 1) * NEGBIG), name="diagmask_c")
    con["ones2"] = nc.inline_tensor(np.ones((2, 128), dtype=BF), name="ones2_c")
    return con


def make_alibi_consts(nc):
    kt = np.arange(TB, dtype=np.float64) - (TB - 1)
    rows = np.zeros((48, TB), np.float64)
    for h in range(NH):
        slope = 2.0 ** (-0.5 * (h + 1))
        v = slope * kt
        b1 = v.astype(BF).astype(np.float64)
        b2 = (v - b1).astype(BF).astype(np.float64)
        b3 = (v - b1 - b2).astype(BF).astype(np.float64)
        rows[3 * h + 0] = b1
        rows[3 * h + 1] = b2
        rows[3 * h + 2] = b3
    sel = np.zeros((48, NH * 128), np.float32)
    for h in range(NH):
        sel[3 * h:3 * h + 3, 128 * h:128 * (h + 1)] = 1.0
    b48 = nc.inline_tensor(rows.astype(BF), name="alibi_b48_c")
    selc = nc.inline_tensor(sel.astype(BF), name="alibi_sel_c")
    return b48, selc



def emit_act_warmup(E, funcs, scratch):
    for k in range(3):
        for f in funcs:
            E.emit("scalar", lambda b, f=f: b.activation(
                scratch[:, 0:256], scratch[:, 0:256], f), inc="S")


def emit_ln_pass(E, src_aps, ydram_ap, tag, capture=None):
    nc = E.nc
    nt = len(src_aps)
    st = contextlib.ExitStack()
    xb_all = st.enter_context(nc.sbuf_tensor(f"{tag}_xb", [128, nt, C], BF16))
    stt = st.enter_context(nc.sbuf_tensor(f"{tag}_st", [128, nt, 8], F32))
    scr = st.enter_context(nc.sbuf_tensor(f"{tag}_scr", [128, C], BF16))
    scr2 = st.enter_context(nc.sbuf_tensor(f"{tag}_scr2", [128, C], BF16))
    ytile = st.enter_context(nc.sbuf_tensor(f"{tag}_yt", [128, 2, C], BF16))
    ystage = st.enter_context(nc.sbuf_tensor(f"{tag}_ys", [128, 2, C], BF16))
    warm = st.enter_context(nc.sbuf_tensor(f"{tag}_warm", [128, 256], F32))
    ident_sb = st.enter_context(nc.sbuf_tensor(f"{tag}_id", [128, 128], BF16))
    pst2 = st.enter_context(nc.psum_tensor(f"{tag}_psT", [128, 2, 1024], BF16))

    eps_col = st.enter_context(nc.sbuf_tensor(f"{tag}_eps", [128, 1], F32))
    E.emit("vector", lambda b: b.memset(eps_col[:, :], EPS))
    E.emit("sync", lambda b: b.dma_start(out=ident_sb[:, :],
                                         in_=E.con["ident"].ap()),
           inc="Dc", amt=16)
    idv = E.val("Dc")
    emit_act_warmup(E, [AF.Ln, AF.Square, AF.Copy], warm)
    nt4 = (nt + 3) // 4
    gsem = {}
    for g in range(4):
        lo, hi = g * nt4, min((g + 1) * nt4, nt)
        if lo >= hi:
            continue
        name = f"Gx{g}"
        for i in range(lo, hi):
            E.emit("gpsimd", lambda b, i=i: b.dma_start(
                out=xb_all[:, i, :], in_=src_aps[i]), inc=name, amt=16)
        gv = E.val(name)
        for i in range(lo, hi):
            E.emit("vector", lambda b, i=i: b.tensor_scalar(
                scr[:, :], xb_all[:, i, :], 1.0, 0.0, ALU.mult, ALU.add,
                accum_out=stt[:, i, 0:1]), wait=[(name, gv)], inc="V")
            E.emit("scalar", lambda b, i=i: b.activation(
                scr2[:, :], xb_all[:, i, :], AF.Square,
                accum_out=stt[:, i, 1:2]), wait=[(name, gv)], inc="S")
    v_stats = E.val("V")
    s_stats = E.val("S")
    spacer = lambda b: b.tensor_copy(scr[:, 0:512], scr2[:, 0:512])
    E.emit("vector", lambda b: b.tensor_scalar(
        stt[:, :, 2:3], stt[:, :, 0:1], 1.0 / C, None, ALU.mult),
        wait=[("V", v_stats), ("S", s_stats)])
    E.emit("vector", spacer)
    E.emit("vector", lambda b: b.tensor_tensor(
        stt[:, :, 3:4], stt[:, :, 2:3], stt[:, :, 2:3], ALU.mult))
    E.emit("vector", spacer)
    E.emit("vector", lambda b: b.scalar_tensor_tensor(
        stt[:, :, 4:5], stt[:, :, 1:2], 1.0 / C, stt[:, :, 3:4],
        ALU.mult, ALU.subtract))
    E.emit("vector", spacer)
    E.emit("vector", spacer, inc="V")
    v_var = E.val("V")
    E.emit("scalar", lambda b: b.activation(
        stt[:, :, 5:6], stt[:, :, 4:5], AF.Ln, bias=eps_col[:, 0:1]),
        wait=[("V", v_var)])
    emit_act_warmup(E, [AF.Exp], warm)
    E.emit("scalar", lambda b: b.activation(
        stt[:, :, 6:7], stt[:, :, 5:6], AF.Exp, scale=-0.5))
    E.emit("scalar", lambda b: b.activation(warm[:, 0:256], warm[:, 0:256],
                                            AF.Copy), inc="S")
    s_rstd = E.val("S")
    E.emit("vector", lambda b: b.scalar_tensor_tensor(
        stt[:, :, 7:8], stt[:, :, 2:3], -1.0, stt[:, :, 6:7],
        ALU.mult, ALU.mult), wait=[("S", s_rstd)], inc="V")
    E.emit("vector", spacer)
    E.emit("vector", spacer)
    v_nmr = E.val("V")
    if capture is not None:
        E.emit("vector", lambda b: b.tensor_copy(capture[:, 0:8 * nt],
                                                 stt[:, :, :]), inc="V")
    ev = [0, 0]
    yw = [0, 0]
    sv = [0, 0]
    for i in range(nt):
        p = i % 2
        w = [("V", v_nmr)]
        if yw[p]:
            w.append(("T", yw[p]))
        E.emit("vector", lambda b, i=i, p=p: b.tensor_scalar(
            ytile[:, p, :], xb_all[:, i, :], stt[:, i, 6:7], stt[:, i, 7:8],
            ALU.mult, ALU.add), wait=w, inc="V")
        v_y = E.val("V")
        w = [("V", v_y), ("Dc", idv)]
        if ev[p]:
            w.append(("V", ev[p]))

        def tr(b, p=p):
            ins = None
            for c in range(8):
                ins = b.transpose(pst2[:, p, 128 * c:128 * (c + 1)],
                                  ytile[:, p, 128 * c:128 * (c + 1)],
                                  ident_sb[:, :])
            return ins
        E.emit("tensor", tr, wait=w, inc="T")
        yw[p] = E.val("T")
        ysem = f"{tag}y{p}"
        w = [("T", yw[p])]
        if sv[p]:
            w.append((ysem, sv[p]))
        E.emit("vector", lambda b, p=p: b.tensor_copy(
            ystage[:, p, :], pst2[:, p, :]), wait=w, inc="V")
        ev[p] = E.val("V")
        E.emit("sync", lambda b, p=p, i=i: b.dma_start(
            out=ydram_ap[:, :, 128 * i:128 * (i + 1)],
            in_=ystage[:, p, :].rearrange("x (a c) -> x a c", a=8)),
            wait=[("V", ev[p])], inc=ysem, amt=16)
        sv[p] = E.val(ysem)
    st.close()
    return [(f"{tag}y0", E.val(f"{tag}y0")), (f"{tag}y1", E.val(f"{tag}y1"))]



def emit_ffn_block(E, xin_ap, yin_ap, xout_ap, wg_ap, wu_ap, wd_ap, ctx,
                   ntok=TPC):
    nc = E.nc
    st = contextlib.ExitStack()
    wg_sb = st.enter_context(nc.sbuf_tensor("ffn_wg", [128, 8, FH], BF16))
    wu_sb = st.enter_context(nc.sbuf_tensor("ffn_wu", [128, 8, FH], BF16))
    wd_sb = st.enter_context(nc.sbuf_tensor("ffn_wd", [128, 24, C], BF16))
    warm_sb = st.enter_context(nc.sbuf_tensor("ffn_warm", [128, 256], F32))
    emit_act_warmup(E, [AF.Tanh], warm_sb)

    with nc.sbuf_tensor("ffn_stage", [128, 2, FH], I32) as stage:
        last_dq = [0, 0]
        srcs = [(wg_ap, wg_sb, 8, FH, RCP127), (wu_ap, wu_sb, 8, FH, RCP127),
                (wd_ap, wd_sb, 24, C, float(np.float32(0.5) * np.float32(RCP127)))]
        k = 0
        for (src, dst, nch, wid, dqs) in srcs:
            for cc in range(nch):
                p = k % 2
                w = [("V", last_dq[p])] if last_dq[p] else []
                gsem = f"Gs{p}"
                E.emit("gpsimd", lambda b, src=src, cc=cc, p=p, wid=wid:
                       b.dma_start(out=stage[:, p, 0:wid], in_=src[:, cc, :]),
                       wait=w, inc=gsem, amt=16)
                gv = E.val(gsem)
                E.emit("vector", lambda b, dst=dst, cc=cc, p=p, wid=wid, dqs=dqs:
                       b.tensor_scalar(dst[:, cc, :], stage[:, p, 0:wid],
                                       dqs, None, ALU.mult),
                       wait=[(gsem, gv)], inc="V")
                last_dq[p] = E.val("V")
                k += 1
    dq_done = E.val("V")
    E.barrier([("V", dq_done)])

    y_blk = st.enter_context(nc.sbuf_tensor("ffn_y", [128, 2, 8, 512], BF16))
    hid = st.enter_context(nc.sbuf_tensor("ffn_hid", [128, 24, 512], BF16))
    t_sb = st.enter_context(nc.sbuf_tensor("ffn_t", [128, 2, 512], BF16))
    g_sb = st.enter_context(nc.sbuf_tensor("ffn_g", [128, 2, 512], BF16))
    xres2 = st.enter_context(nc.sbuf_tensor("ffn_xr", [128, 2, C], F32))
    out2 = st.enter_context(nc.sbuf_tensor("ffn_out", [128, 2, C], F32))
    ps_s = st.enter_context(nc.psum_tensor("ffn_psA", [128, 2048], F32))
    ps_m = st.enter_context(nc.psum_tensor("ffn_psM", [128, 1024], F32))

    last_hmul = [0, 0]
    last_gu = 0
    prev_res = 0
    last_store = [0, 0]
    last_xload = [0, 0]
    last_yload = [0, 0]

    nblk = ntok // 512
    for blk in range(nblk):
        bp = blk % 2
        w = [("T", last_yload[bp])] if last_yload[bp] else []
        ybsem = f"Dyb{bp}"
        E.emit("sync", lambda b, blk=blk, bp=bp: b.dma_start(
            out=y_blk[:, bp, :, :],
            in_=yin_ap[:, :, 512 * blk:512 * (blk + 1)]),
            wait=w, inc=ybsem, amt=16)
        yv = (ybsem, E.val(ybsem))
        for ht in range(24):
            pp = ht % 2
            gslc = (1024 * pp, 1024 * pp + 512)
            uslc = (1024 * pp + 512, 1024 * pp + 1024)
            w = [yv]
            if last_hmul[pp]:
                w.append(("V", last_hmul[pp]))

            def gu(b, ht=ht, bp=bp, gslc=gslc, uslc=uslc):
                ins = None
                for cc in range(8):
                    b.matmul(ps_s[:, gslc[0]:gslc[1]],
                             wg_sb[:, cc, 128 * ht:128 * (ht + 1)],
                             y_blk[:, bp, cc, :], start=(cc == 0),
                             stop=(cc == 7))
                for cc in range(8):
                    ins = b.matmul(ps_s[:, uslc[0]:uslc[1]],
                                   wu_sb[:, cc, 128 * ht:128 * (ht + 1)],
                                   y_blk[:, bp, cc, :], start=(cc == 0),
                                   stop=(cc == 7))
                return ins
            E.emit("tensor", gu, wait=w, inc="T")
            last_gu = E.val("T")
            E.emit("scalar", lambda b, pp=pp, gslc=gslc: b.activation(
                t_sb[:, pp, :], ps_s[:, gslc[0]:gslc[1]], AF.Tanh, scale=0.5),
                wait=[("T", last_gu)], inc="S")
            sv = E.val("S")
            E.emit("vector", lambda b, pp=pp, gslc=gslc:
                   b.scalar_tensor_tensor(g_sb[:, pp, :], t_sb[:, pp, :], 1.0,
                                          ps_s[:, gslc[0]:gslc[1]],
                                          ALU.add, ALU.mult),
                   wait=[("S", sv)])
            E.emit("vector", lambda b, ht=ht, pp=pp, uslc=uslc:
                   b.tensor_tensor(hid[:, ht, :], ps_s[:, uslc[0]:uslc[1]],
                                   g_sb[:, pp, :], ALU.mult), inc="V")
            last_hmul[pp] = E.val("V")
        last_yload[bp] = last_gu
        hv = E.val("V")
        for tt in range(4):
            p = tt % 2
            w = [("V", hv)]
            if prev_res:
                w.append(("V", prev_res))

            def down(b, tt=tt):
                ins = None
                for mc in range(2):
                    for hc in range(24):
                        ins = b.matmul(ps_m[:, 512 * mc:512 * (mc + 1)],
                                       hid[:, hc, 128 * tt:128 * (tt + 1)],
                                       wd_sb[:, hc, 512 * mc:512 * (mc + 1)],
                                       start=(hc == 0), stop=(hc == 23))
                return ins
            E.emit("tensor", down, wait=w, inc="T")
            tv = E.val("T")
            row0 = 512 * blk + 128 * tt
            xsem, osem = f"Dx{p}", f"Do{p}"
            wl = [("V", last_xload[p])] if last_xload[p] else []
            E.emit("sync", lambda b, row0=row0, p=p: b.dma_start(
                out=xres2[:, p, :], in_=xin_ap[row0:row0 + 128, :]),
                wait=wl, inc=xsem, amt=16)
            dv = E.val(xsem)
            w = [("T", tv), (xsem, dv)]
            if last_store[p]:
                w.append((osem, last_store[p]))
            E.emit("vector", lambda b, p=p: b.tensor_tensor(
                out2[:, p, :], ps_m[:, :], xres2[:, p, :], ALU.add),
                wait=w, inc="V")
            prev_res = E.val("V")
            last_xload[p] = prev_res
            E.emit("sync", lambda b, row0=row0, p=p: b.dma_start(
                out=xout_ap[row0:row0 + 128, :], in_=out2[:, p, :]),
                wait=[("V", prev_res)], inc=osem, amt=16)
            last_store[p] = E.val(osem)
    st.close()
    return [("Do0", E.val("Do0")), ("Do1", E.val("Do1"))]



def emit_attn_block(E, xin_ap, yin_ap, xout_ap, wq_ap, wk_ap, wv_ap, wo_ap,
                    bvec_ap, bo_ap, layer, tag, dbg_nqt=16, dbg_nb=BPC,
                    out_row_fn=None):
    nc = E.nc
    st = contextlib.ExitStack()
    warm = st.enter_context(nc.sbuf_tensor(f"{tag}_warm", [128, 256], F32))
    Q2 = st.enter_context(nc.sbuf_tensor(f"{tag}_q2", [128, 8, TB], BF16))
    K2 = st.enter_context(nc.sbuf_tensor(f"{tag}_k2", [128, 8, TB], BF16))
    Vt = st.enter_context(nc.sbuf_tensor(f"{tag}_vt", [128, 16, C], BF16))
    bvec = st.enter_context(nc.sbuf_tensor(f"{tag}_bv", [128, 24], F32))
    bvbf = st.enter_context(nc.sbuf_tensor(f"{tag}_bvb", [128, 8], BF16))
    bo_sb = st.enter_context(nc.sbuf_tensor(f"{tag}_bo", [1, C], F32))
    bias2 = st.enter_context(nc.sbuf_tensor(f"{tag}_b2", [2, C], BF16))
    ones2 = st.enter_context(nc.sbuf_tensor(f"{tag}_on2", [2, 128], BF16))
    dmask = st.enter_context(nc.sbuf_tensor(f"{tag}_dm", [128, 128], F32))
    ident_sb = st.enter_context(nc.sbuf_tensor(f"{tag}_id", [128, 128], BF16))
    if layer == 0:
        sel_sb = st.enter_context(nc.sbuf_tensor(f"{tag}_sel", [48, NH * 128], BF16))
        b48_sb = st.enter_context(nc.sbuf_tensor(f"{tag}_b48", [48, TB], BF16))
    ast = st.enter_context(nc.sbuf_tensor(f"{tag}_ast", [128, 2, 8], F32))
    E.dbgP = st.enter_context(nc.sbuf_tensor(f"{tag}_dbgp", [128, 128], BF16))
    E.dbgA = st.enter_context(nc.sbuf_tensor(f"{tag}_dbga", [128, 8], F32))
    P_sb = st.enter_context(nc.sbuf_tensor(f"{tag}_p", [128, 2, TB], BF16))
    PT_sb = st.enter_context(nc.sbuf_tensor(f"{tag}_pt", [128, TB], BF16))
    zn2 = st.enter_context(nc.sbuf_tensor(f"{tag}_zn", [128, 128], BF16))
    qst = st.enter_context(nc.sbuf_tensor(f"{tag}_qst", [128, 2, 128], BF16))
    zT2 = st.enter_context(nc.sbuf_tensor(f"{tag}_zt", [128, 128], BF16))
    ps_s = st.enter_context(nc.psum_tensor(f"{tag}_pss", [128, 2048], F32))
    ps_t = st.enter_context(nc.psum_tensor(f"{tag}_pst", [128, 1024], BF16))
    ps_z = st.enter_context(nc.psum_tensor(f"{tag}_psz", [128, 128], F32))
    ps_m = st.enter_context(nc.psum_tensor(f"{tag}_psm", [128, 1024], F32))

    emit_act_warmup(E, [AF.Exp], warm)
    for dst, srcap in ((bvec, bvec_ap), (bo_sb, bo_ap),
                       (ones2, E.con["ones2"].ap()),
                       (dmask, E.con["diagmask"].ap()),
                       (ident_sb, E.con["ident"].ap())):
        E.emit("sync", lambda b, dst=dst, srcap=srcap: b.dma_start(
            out=dst[:, :], in_=srcap), inc="Dc", amt=16)
    if layer == 0:
        E.emit("sync", lambda b: b.dma_start(out=sel_sb[:, :],
                                             in_=E.con["alibi_sel"].ap()),
               inc="Dc", amt=16)
        E.emit("sync", lambda b: b.dma_start(out=b48_sb[:, :],
                                             in_=E.con["alibi_b48"].ap()),
               inc="Dc", amt=16)
    dc_v = E.val("Dc")
    E.emit("vector", lambda b: b.tensor_copy(bvbf[:, :], bvec[:, 16:24]),
           wait=[("Dc", dc_v)], inc="V")
    E.emit("vector", lambda b: b.memset(qst[:, :, :], 0.0), inc="V")
    E.barrier([("Dc", dc_v), ("V", E.val("V"))])

    osem = [f"{tag}o0", f"{tag}o1"]
    xsem = [f"{tag}x0", f"{tag}x1"]
    last_store = [0, 0]
    last_xload = [0, 0]
    bias_ready = False
    prev_merge_evac = [0]

    for bb in range(dbg_nb):
        tok0 = bb * TB
        E.barrier([("T", E.val("T"))], engines=("vector", "sync", "gpsimd"))
        with contextlib.ExitStack() as stA:
            wq_sb = stA.enter_context(nc.sbuf_tensor(f"{tag}_wq{bb}", [128, 8, C], BF16))
            wk_sb = stA.enter_context(nc.sbuf_tensor(f"{tag}_wk{bb}", [128, 8, C], BF16))
            wv_sb = stA.enter_context(nc.sbuf_tensor(f"{tag}_wv{bb}", [128, 8, C], BF16))
            y_win = stA.enter_context(nc.sbuf_tensor(f"{tag}_yw{bb}", [128, 2, 8, 512], BF16))
            wsems = {}
            for nm, dst, srcap in (("q", wq_sb, wq_ap), ("k", wk_sb, wk_ap),
                                   ("v", wv_sb, wv_ap)):
                sname = f"{tag}w{nm}{bb}"
                E.emit("gpsimd", lambda b, dst=dst, srcap=srcap: b.dma_start(
                    out=dst[:, :, :], in_=srcap), inc=sname, amt=16)
                wsems[nm] = (sname, E.val(sname))
            bank_evac = [0, 0, 0, 0]
            ywin_T = [0, 0]
            kbank = [0]
            for tw in range(4):
                p = tw % 2
                w = [("T", ywin_T[p])] if ywin_T[p] else []
                ysm = f"{tag}ya{p}"
                E.emit("sync", lambda b, tw=tw, p=p, tok0=tok0: b.dma_start(
                    out=y_win[:, p, :, :],
                    in_=yin_ap[:, :, tok0 + 512 * tw: tok0 + 512 * (tw + 1)]),
                    wait=w, inc=ysm, amt=16)
                yv = E.val(ysm)

                def proj(dst, w_sb, wwait, bcol, tw=tw, p=p, yv=yv, ysm=ysm):
                    for hp in range(8):
                        bk = kbank[0] % 4
                        kbank[0] += 1
                        w = [(ysm, yv), wwait]
                        if bank_evac[bk]:
                            w.append(("V", bank_evac[bk]))

                        def mm(b, hp=hp, bk=bk, w_sb=w_sb, p=p):
                            ins = None
                            for cc in range(8):
                                ins = b.matmul(
                                    ps_s[:, 512 * bk:512 * (bk + 1)],
                                    w_sb[:, cc, 128 * hp:128 * (hp + 1)],
                                    y_win[:, p, cc, :],
                                    start=(cc == 0), stop=(cc == 7))
                            return ins
                        E.emit("tensor", mm, wait=w, inc="T")
                        tv = E.val("T")
                        E.emit("vector", lambda b, hp=hp, bk=bk, dst=dst,
                               bcol=bcol, tw=tw: b.tensor_scalar(
                                   dst[:, hp, 512 * tw:512 * (tw + 1)],
                                   ps_s[:, 512 * bk:512 * (bk + 1)],
                                   bvec[:, bcol + hp:bcol + hp + 1], None,
                                   ALU.add), wait=[("T", tv)], inc="V")
                        bank_evac[bk] = E.val("V")
                proj(Q2, wq_sb, wsems["q"], 0)
                proj(K2, wk_sb, wsems["k"], 8)
                for tt in range(4):
                    for vc in range(2):
                        bk = kbank[0] % 4
                        kbank[0] += 1
                        w = [(ysm, yv), wsems["v"]]
                        if bank_evac[bk]:
                            w.append(("V", bank_evac[bk]))

                        def mmv(b, tt=tt, vc=vc, bk=bk, p=p):
                            ins = None
                            for cc in range(8):
                                ins = b.matmul(
                                    ps_s[:, 512 * bk:512 * (bk + 1)],
                                    y_win[:, p, cc, 128 * tt:128 * (tt + 1)],
                                    wv_sb[:, cc, 512 * vc:512 * (vc + 1)],
                                    start=(cc == 0), stop=(cc == 7))
                            return ins
                        E.emit("tensor", mmv, wait=w, inc="T")
                        tv = E.val("T")
                        E.emit("vector", lambda b, tt=tt, vc=vc, bk=bk, tw=tw:
                               b.tensor_copy(
                                   Vt[:, 4 * tw + tt, 512 * vc:512 * (vc + 1)],
                                   ps_s[:, 512 * bk:512 * (bk + 1)]),
                               wait=[("T", tv)], inc="V")
                        bank_evac[bk] = E.val("V")
                ywin_T[p] = E.val("T")
        E.barrier([("V", E.val("V"))])
        with contextlib.ExitStack() as stB:
            wo_sb = stB.enter_context(nc.sbuf_tensor(f"{tag}_wo{bb}", [128, 8, C], BF16))
            comb = stB.enter_context(nc.sbuf_tensor(f"{tag}_cmb{bb}", [1, C], F32))
            hi1 = stB.enter_context(nc.sbuf_tensor(f"{tag}_hi{bb}", [1, C], BF16))
            lo1 = stB.enter_context(nc.sbuf_tensor(f"{tag}_lo{bb}", [1, C], BF16))
            out2 = stB.enter_context(nc.sbuf_tensor(f"{tag}_o2{bb}", [128, 2, C], F32))
            xres2 = stB.enter_context(nc.sbuf_tensor(f"{tag}_xr{bb}", [128, 2, C], F32))
            wosem = f"{tag}wo{bb}"
            E.emit("gpsimd", lambda b: b.dma_start(out=wo_sb[:, :, :],
                                                   in_=wo_ap), inc=wosem, amt=16)
            wo_v = E.val(wosem)
            if not bias_ready:
                bias_ready = True

                def bmm(b):
                    ins = None
                    for j in range(8):
                        for mw in range(2):
                            ins = b.matmul(ps_m[0:1, 512 * mw:512 * (mw + 1)],
                                           bvbf[:, j:j + 1],
                                           wo_sb[:, j, 512 * mw:512 * (mw + 1)],
                                           start=(j == 0), stop=(j == 7))
                    return ins
                E.emit("tensor", bmm, wait=[(wosem, wo_v)], inc="T")
                tv = E.val("T")
                E.emit("vector", lambda b: b.tensor_tensor(
                    comb[:, :], ps_m[0:1, :], bo_sb[:, :], ALU.add),
                    wait=[("T", tv)], inc="V")
                E.emit("vector", lambda b: b.tensor_copy(hi1[:, :],
                                                         comb[:, :]))
                E.emit("vector", lambda b: b.tensor_tensor(
                    lo1[:, :], comb[:, :], hi1[:, :], ALU.subtract),
                    inc="V")
                prev_merge_evac[0] = E.val("V")
                E.emit("gpsimd", lambda b: b.dma_start(
                    out=bias2[0:1, :], in_=hi1[:, :]),
                    wait=[("V", E.val("V"))], inc=f"{tag}b2", amt=16)
                E.emit("gpsimd", lambda b: b.dma_start(
                    out=bias2[1:2, :], in_=lo1[:, :]),
                    inc=f"{tag}b2", amt=16)
                E.barrier([("V", E.val("V")), (f"{tag}b2", E.val(f"{tag}b2"))],
                          engines=("tensor",))

            pend = [None]
            qst_war = [0, 0]
            bank_exp = [0, 0, 0, 0]
            bank_red = [0, 0, 0, 0]
            wcnt = [0]
            psz_free = [0]
            pt_war = [0]
            psb_war = [0, 0]
            it = [0]

            def flush():
                if pend[0] is None or "noflush" in DBG:
                    pend[0] = None
                    return False
                (qt, hp, j, par, W, s_exp) = pend[0]
                pend[0] = None
                h = 2 * hp + j
                nkt = W // 128
                for rnd in range((nkt + 7) // 8):
                    k0, k1 = 8 * rnd, min(8 * rnd + 8, nkt)

                    def tr(b, k0=k0, k1=k1, par=par):
                        ins = None
                        for k in range(k0, k1):
                            ins = b.transpose(
                                ps_t[:, 128 * (k - k0):128 * (k - k0 + 1)],
                                P_sb[:, par, 128 * k:128 * (k + 1)],
                                ident_sb[:, :])
                        return ins
                    w = [("S", s_exp)]
                    if pt_war[0]:
                        w.append(pt_war[0])
                    E.emit("tensor", tr, wait=w, inc="T")
                    tv = E.val("T")
                    E.emit("scalar", lambda b, k0=k0, k1=k1: b.activation(
                        PT_sb[:, 128 * k0:128 * k1],
                        ps_t[:, 0:128 * (k1 - k0)], AF.Copy),
                        wait=[("T", tv)], inc="Sp")
                    pt_war[0] = ("Sp", E.val("Sp"))
                psb_war[par] = E.val("T")
                w = [pt_war[0]]
                if psz_free[0]:
                    w.append(("V", psz_free[0]))

                def pv(b, nkt=nkt, h=h):
                    ins = None
                    for k in range(nkt):
                        ins = b.matmul(ps_z[:, 0:DH],
                                       PT_sb[:, 128 * k:128 * (k + 1)],
                                       Vt[:, k, DH * h:DH * h + DH],
                                       start=(k == 0), stop=(k == nkt - 1))
                    return ins
                E.emit("tensor", pv, wait=w, inc="T")
                tv = E.val("T")
                E.emit("vector", lambda b, j=j, par=par: b.tensor_scalar(
                    zn2[:, DH * j:DH * (j + 1)], ps_z[:, 0:DH],
                    ast[:, par, 1:2], None, ALU.mult),
                    wait=[("T", tv)], inc="V")
                psz_free[0] = E.val("V")
                if j == 1:
                    w = [("V", psz_free[0])]
                    if pt_war[0]:
                        w.append(pt_war[0])
                    E.emit("tensor", lambda b: b.transpose(
                        ps_t[:, 0:128], zn2[:, :], ident_sb[:, :]),
                        wait=w, inc="T")
                    tv = E.val("T")
                    E.emit("vector", lambda b: b.tensor_copy(
                        zT2[:, :], ps_t[:, 0:128]), wait=[("T", tv)], inc="V")
                    pt_war[0] = ("V", E.val("V"))
                    zv = E.val("V")
                    w = [("V", zv), (wosem, wo_v)]
                    if hp == 0 and prev_merge_evac[0]:
                        w.append(("V", prev_merge_evac[0]))

                    def mrg(b, hp=hp):
                        ins = None
                        for mw in range(2):
                            ins = b.matmul(ps_m[:, 512 * mw:512 * (mw + 1)],
                                           zT2[:, :],
                                           wo_sb[:, hp, 512 * mw:512 * (mw + 1)],
                                           start=(hp == 0), stop=False)
                        return ins
                    E.emit("tensor", mrg, wait=w, inc="T")
                return True

            for qt in range(dbg_nqt):
                W = 128 * (qt + 1)
                for hp in range(8):
                    for j in range(2):
                        h = 2 * hp + j
                        it[0] += 1
                        par = it[0] % 2
                        alibi = (layer == 0) and ("noalibi" not in DBG)
                        wq = []
                        if qst_war[j]:
                            wq.append(("T", qst_war[j]))
                        E.emit("scalar", lambda b, qt=qt, hp=hp, j=j:
                               b.activation(
                                   qst[64 * j:64 * (j + 1), j, :],
                                   Q2[64 * j:64 * (j + 1), hp,
                                      128 * qt:128 * (qt + 1)],
                                   AF.Copy, scale=0.125),
                               wait=wq, inc="S")
                        sq = E.val("S")
                        nw = (W + 511) // 512
                        banks = []
                        for wi in range(nw):
                            wnd = 512 * wi
                            wid = min(512, W - wnd)
                            bk = wcnt[0] % 4
                            wcnt[0] += 1
                            banks.append((bk, wid))
                            w = [("S", sq)]
                            if bank_exp[bk]:
                                w.append(("S", bank_exp[bk]))
                            if bank_red[bk]:
                                w.append(("V", bank_red[bk]))

                            def smw(b, j=j, hp=hp, h=h, bk=bk, wnd=wnd,
                                    wid=wid, alibi=alibi):
                                ins = b.matmul(
                                    ps_s[:, 512 * bk:512 * bk + wid],
                                    qst[:, j, :],
                                    K2[:, hp, wnd:wnd + wid],
                                    start=True, stop=not alibi)
                                if alibi:
                                    ins = b.matmul(
                                        ps_s[:, 512 * bk:512 * bk + wid],
                                        sel_sb[:, 128 * h:128 * (h + 1)],
                                        b48_sb[:, wnd:wnd + wid],
                                        start=False, stop=True)
                                return ins
                            E.emit("tensor", smw, wait=w, inc="T")
                            t_w = E.val("T")
                            if wi == nw - 1:
                                E.emit("vector", lambda b, bk=bk, wid=wid:
                                       b.tensor_tensor(
                                           ps_s[:, 512 * bk + wid - 128:
                                                512 * bk + wid],
                                           ps_s[:, 512 * bk + wid - 128:
                                                512 * bk + wid],
                                           dmask[:, :], ALU.add),
                                       wait=[("T", t_w)], inc="V")
                            E.emit("vector", lambda b, bk=bk, wid=wid, par=par,
                                   wi=wi: b.tensor_reduce(
                                       ast[:, par, 4 + wi:5 + wi],
                                       ps_s[:, 512 * bk:512 * bk + wid],
                                       AX.X, ALU.max, negate=True),
                                   wait=[("T", t_w)], inc="V")
                            bank_red[bk] = E.val("V")
                        if not flush():
                            E.emit("vector", lambda b: b.tensor_copy(
                                warm[:, 0:256], warm[:, 0:256]))
                        if nw > 1:
                            E.emit("vector", lambda b, par=par, nw=nw:
                                   b.tensor_reduce(
                                       ast[:, par, 3:4],
                                       ast[:, par, 4:4 + nw],
                                       AX.X, ALU.min), inc="V")
                            bcol = 3
                        else:
                            bcol = 4
                        v_bias = E.val("V")
                        w0 = [("V", v_bias)]
                        if psb_war[par]:
                            w0.append(("T", psb_war[par]))
                        for wi, (bk, wid) in enumerate(banks):
                            zc = 4 + wi if nw > 1 else 5
                            E.emit("scalar", lambda b, par=par, bk=bk, wid=wid,
                                   wi=wi, bcol=bcol, zc=zc: b.activation(
                                       P_sb[:, par, 512 * wi:512 * wi + wid],
                                       ps_s[:, 512 * bk:512 * bk + wid],
                                       AF.Exp, bias=ast[:, par, bcol:bcol + 1],
                                       scale=1.0,
                                       accum_out=ast[:, par, zc:zc + 1]),
                                   wait=(w0 if wi == 0 else []), inc="S")
                            bank_exp[bk] = E.val("S")
                        zlo = 4 if nw > 1 else 5
                        E.emit("scalar", lambda b, par=par, nw=nw, zlo=zlo:
                               b.activation(
                                   warm[:, 0:nw], ast[:, par, zlo:zlo + nw],
                                   AF.Copy, accum_out=ast[:, par, 0:1]),
                               inc="S")
                        s_exp = E.val("S")
                        E.emit("vector", lambda b, par=par: b.reciprocal(
                            ast[:, par, 1:2], ast[:, par, 0:1]),
                            wait=[("S", s_exp)], inc="V")
                        if "dump" in DBG and it[0] == 1 and bb == 0:
                            pass
                        pend[0] = (qt, hp, j, par, W, s_exp)
                flush()
                if "notail" in DBG:
                    continue

                def bmm2(b):
                    ins = None
                    for mw in range(2):
                        ins = b.matmul(ps_m[:, 512 * mw:512 * (mw + 1)],
                                       ones2[:, :],
                                       bias2[0:2, 512 * mw:512 * (mw + 1)],
                                       start=False, stop=True)
                    return ins
                E.emit("tensor", bmm2, inc="T")
                t_m = E.val("T")
                p = qt % 2
                row0 = tok0 + 128 * qt
                orow0 = row0 if out_row_fn is None else out_row_fn(bb, qt)
                w = [("V", last_xload[p])] if last_xload[p] else []
                E.emit("sync", lambda b, row0=row0, p=p: b.dma_start(
                    out=xres2[:, p, :], in_=xin_ap[row0:row0 + 128, :]),
                    wait=w, inc=xsem[p], amt=16)
                dv = E.val(xsem[p])
                w = [("T", t_m), (xsem[p], dv)]
                if last_store[p]:
                    w.append((osem[p], last_store[p]))
                E.emit("vector", lambda b, p=p: b.tensor_tensor(
                    out2[:, p, :], ps_m[:, :], xres2[:, p, :], ALU.add),
                    wait=w, inc="V")
                prev_merge_evac[0] = E.val("V")
                last_xload[p] = prev_merge_evac[0]
                E.emit("sync", lambda b, orow0=orow0, p=p: b.dma_start(
                    out=xout_ap[orow0:orow0 + 128, :], in_=out2[:, p, :]),
                    wait=[("V", prev_merge_evac[0])], inc=osem[p], amt=16)
                last_store[p] = E.val(osem[p])
    st.close()
    return [(osem[0], E.val(osem[0])), (osem[1], E.val(osem[1]))]



def emit_lstm_block(E, xin_ap, yin_ap, out_ap, w_aps, b_aps, ctx):
    nc = E.nc
    S2 = SEQ2
    st = contextlib.ExitStack()
    wsb = {}
    for g in ("f", "i", "c", "o"):
        wsb[g] = (st.enter_context(nc.sbuf_tensor(f"ls_w{g}", [128, 8, C], BF16)),
                  st.enter_context(nc.sbuf_tensor(f"ls_u{g}", [128, 8, C], BF16)))
    bcols = st.enter_context(nc.sbuf_tensor("ls_bc", [128, 24], F32))
    inccol = st.enter_context(nc.sbuf_tensor("ls_inc", [128, 8], F32))
    bo2 = st.enter_context(nc.sbuf_tensor("ls_bo2", [2, C], BF16))
    ones2 = st.enter_context(nc.sbuf_tensor("ls_on2", [2, 128], BF16))
    warm = st.enter_context(nc.sbuf_tensor("ls_warm", [128, 256], F32))
    c_fm = st.enter_context(nc.sbuf_tensor("ls_c", [128, 8, S2], F32))
    c_bf = st.enter_context(nc.sbuf_tensor("ls_cb", [128, 8, S2], BF16))
    f_sb = st.enter_context(nc.sbuf_tensor("ls_f", [128, 8, S2], F32))
    i_sb = st.enter_context(nc.sbuf_tensor("ls_i", [128, 8, S2], F32))
    g_sb = st.enter_context(nc.sbuf_tensor("ls_g", [128, 8, S2], F32))
    y2 = st.enter_context(nc.sbuf_tensor("ls_y", [128, 2, 8, S2], BF16))
    cfuo = st.enter_context(nc.sbuf_tensor("ls_cfuo", [128, 2, C], F32))
    xr2 = st.enter_context(nc.sbuf_tensor("ls_xr", [128, 2, C], F32))
    o2 = st.enter_context(nc.sbuf_tensor("ls_o2", [128, 2, C], F32))
    ps_g = st.enter_context(nc.psum_tensor("ls_psg", [128, 2, 512], F32))
    ps_h = st.enter_context(nc.psum_tensor("ls_psh", [128, 2, 512], F32))

    emit_act_warmup(E, [AF.Sigmoid, AF.Tanh], warm)
    for g in ("f", "i", "c", "o"):
        for k in range(2):
            E.emit("gpsimd", lambda b, g=g, k=k: b.dma_start(
                out=wsb[g][k][:, :, :], in_=w_aps[g][k]), inc=f"lw{g}{k}", amt=16)
    E.emit("sync", lambda b: b.dma_start(out=bcols[:, :], in_=b_aps["bcols"]),
           inc="Dc", amt=16)
    E.emit("sync", lambda b: b.dma_start(out=inccol[:, :], in_=b_aps["inc"]),
           inc="Dc", amt=16)
    E.emit("sync", lambda b: b.dma_start(out=ones2[:, :],
                                         in_=E.con["ones2"].ap()), inc="Dc", amt=16)
    with (nc.sbuf_tensor("ls_bo", [1, C], F32) as bo_sb,
          nc.sbuf_tensor("ls_hi", [1, C], BF16) as hi1,
          nc.sbuf_tensor("ls_lo", [1, C], BF16) as lo1):
        E.emit("sync", lambda b: b.dma_start(out=bo_sb[:, :], in_=b_aps["bo"]),
               inc="Dc", amt=16)
        dc = E.val("Dc")
        E.emit("vector", lambda b: b.memset(warm[:, :], 0.0),
               wait=[("Dc", dc), ("S", E.val("S"))])
        E.emit("vector", lambda b: b.tensor_copy(hi1[:, :], bo_sb[:, :]),
               inc="V")
        E.emit("vector", lambda b: b.tensor_tensor(
            lo1[:, :], bo_sb[:, :], hi1[:, :], ALU.subtract), inc="V")
        bv = E.val("V")
        E.emit("gpsimd", lambda b: b.dma_start(out=bo2[0:1, :], in_=hi1[:, :]),
               wait=[("V", bv)], inc="lsb2", amt=16)
        E.emit("gpsimd", lambda b: b.dma_start(out=bo2[1:2, :], in_=lo1[:, :]),
               inc="lsb2", amt=16)
        E.barrier([("lsb2", E.val("lsb2"))], engines=("vector", "scalar"))
    dc = E.val("Dc")
    wws = [(f"lw{g}{k}", E.val(f"lw{g}{k}")) for g in "fico" for k in range(2)]
    zz = warm
    for cc in range(8):
        E.emit("vector", lambda b, cc=cc: b.tensor_scalar(
            c_fm[:, cc, :], zz[:, :], inccol[:, cc:cc + 1], None, ALU.add))
    E.emit("vector", lambda b: b.tensor_copy(
        c_bf[:, :, :].rearrange("p a s -> p (a s)"),
        c_fm[:, :, :].rearrange("p a s -> p (a s)")), inc="V")
    E.barrier([("V", E.val("V")), ("Dc", dc), ("lsb2", E.val("lsb2"))] + wws)

    nstep = NSTEP
    ylast = [0, 0]
    for t in range(nstep):
        p = t % 2
        ysm = f"lsy{p}"
        w = [("T", ylast[p])] if ylast[p] else []
        E.emit("sync", lambda b, t=t, p=p: b.dma_start(
            out=y2[:, p, :, :], in_=yin_ap[:, :, S2 * t:S2 * (t + 1)]),
            wait=w, inc=ysm, amt=16)
        yv = E.val(ysm)
        gates = (("f", f_sb, AF.Sigmoid, 0), ("i", i_sb, AF.Sigmoid, 8),
                 ("c", g_sb, AF.Tanh, 16))
        bank_free = [0, 0]
        for (gn, dst, func, bcol) in gates:
            for mt in range(8):
                bk = mt % 2
                wl = [(ysm, yv)]
                if bank_free[bk]:
                    wl.append(("S", bank_free[bk]))

                def mm(b, gn=gn, mt=mt, bk=bk, p=p):
                    ins = None
                    for cc in range(8):
                        b.matmul(ps_g[:, bk, 0:S2],
                                 wsb[gn][0][:, cc, 128 * mt:128 * (mt + 1)],
                                 y2[:, p, cc, :], start=(cc == 0), stop=False)
                    for cc in range(8):
                        ins = b.matmul(
                            ps_g[:, bk, 0:S2],
                            wsb[gn][1][:, cc, 128 * mt:128 * (mt + 1)],
                            c_bf[:, cc, :], start=False, stop=(cc == 7))
                    return ins
                E.emit("tensor", mm, wait=wl, inc="T")
                tv = E.val("T")
                E.emit("scalar", lambda b, gn=gn, mt=mt, bk=bk, dst=dst,
                       func=func, bcol=bcol: b.activation(
                           dst[:, mt, :], ps_g[:, bk, 0:S2], func,
                           bias=bcols[:, bcol + mt:bcol + mt + 1]),
                       wait=[("T", tv)], inc="S")
                bank_free[bk] = E.val("S")
        ylast[p] = E.val("T")
        sg = E.val("S")
        fl = f_sb[:, :, :].rearrange("p a s -> p (a s)")
        il = i_sb[:, :, :].rearrange("p a s -> p (a s)")
        gl = g_sb[:, :, :].rearrange("p a s -> p (a s)")
        cl = c_fm[:, :, :].rearrange("p a s -> p (a s)")
        cbl = c_bf[:, :, :].rearrange("p a s -> p (a s)")
        E.emit("vector", lambda b, fl=fl, cl=cl: b.tensor_tensor(
            fl, fl, cl, ALU.mult), wait=[("S", sg)])
        E.emit("vector", lambda b, il=il, gl=gl: b.tensor_tensor(
            il, il, gl, ALU.mult))
        E.emit("vector", lambda b, cl=cl, fl=fl, il=il: b.tensor_tensor(
            cl, fl, il, ALU.add))
        E.emit("vector", lambda b, cbl=cbl, cl=cl: b.tensor_copy(cbl, cl),
               inc="V")
        cv = E.val("V")
        E.emit("tensor", lambda b: None, wait=[("V", cv)])
    if "noh" in DBG:
        st.close()
        return [("lsy0", E.val("lsy0")), ("lsy1", E.val("lsy1"))]
    cfin = E.val("V")
    for st2 in range(2):
        pst = ps_h if st2 == 0 else ps_g

        def mmc(b, st2=st2, pst=pst):
            ins = None
            for mw in range(2):
                for cc in range(8):
                    b.matmul(pst[:, mw, :],
                             c_bf[:, cc, 128 * st2:128 * (st2 + 1)],
                             wsb["o"][1][:, cc, 512 * mw:512 * (mw + 1)],
                             start=(cc == 0), stop=False)
                ins = b.matmul(pst[:, mw, :], ones2[:, :],
                               bo2[0:2, 512 * mw:512 * (mw + 1)],
                               start=False, stop=True)
            return ins
        E.emit("tensor", mmc, wait=[("V", cfin)], inc="T")
        tv = E.val("T")
        E.emit("vector", lambda b, st2=st2, pst=pst: b.tensor_copy(
            cfuo[:, st2, :], pst[:, :, :].rearrange("p a s -> p (a s)")),
            wait=[("T", tv)], inc="V")
    cfv = E.val("V")
    E.barrier([("V", cfv)], engines=("tensor",))
    if "nohh" in DBG:
        st.close()
        return [("lsy0", E.val("lsy0")), ("lsy1", E.val("lsy1"))]
    xlast = [0, 0]
    olast = [0, 0]
    hlast = [0, 0]
    for t in range(nstep):
        p = t % 2
        ysm = f"lsy{p}"
        w = [("T", ylast[p])] if ylast[p] else []
        E.emit("sync", lambda b, t=t, p=p: b.dma_start(
            out=y2[:, p, :, :], in_=yin_ap[:, :, S2 * t:S2 * (t + 1)]),
            wait=w, inc=ysm, amt=16)
        yv = E.val(ysm)
        for st2 in range(2):
            pp = st2
            row0 = S2 * t + 128 * st2
            sp = (t * 2 + st2) % 2
            wl = [(ysm, yv)]
            if hlast[pp]:
                wl.append(("S", hlast[pp]))

            def mmh(b, st2=st2, p=p, pp=pp):
                ins = None
                for mw in range(2):
                    for cc in range(8):
                        ins = b.matmul(
                            ps_h[:, pp, :] if mw == 0 else ps_h[:, pp, :],
                            y2[:, p, cc, 128 * st2:128 * (st2 + 1)],
                            wsb["o"][0][:, cc, 512 * mw:512 * (mw + 1)],
                            start=(cc == 0), stop=(cc == 7))
                return ins
            def mmh2(b, st2=st2, p=p, pp=pp):
                ins = None
                for cc in range(8):
                    b.matmul(ps_h[:, pp, :],
                             y2[:, p, cc, 128 * st2:128 * (st2 + 1)],
                             wsb["o"][0][:, cc, 0:512],
                             start=(cc == 0), stop=(cc == 7))
                for cc in range(8):
                    ins = b.matmul(ps_g[:, pp, :],
                                   y2[:, p, cc, 128 * st2:128 * (st2 + 1)],
                                   wsb["o"][0][:, cc, 512:1024],
                                   start=(cc == 0), stop=(cc == 7))
                return ins
            E.emit("tensor", mmh2, wait=wl, inc="T")
            tv = E.val("T")
            ylast[p] = tv
            E.emit("vector", lambda b, st2=st2, pp=pp: b.tensor_tensor(
                ps_h[:, pp, :], ps_h[:, pp, :], cfuo[:, st2, 0:512], ALU.add),
                wait=[("T", tv)])
            E.emit("vector", lambda b, st2=st2, pp=pp: b.tensor_tensor(
                ps_g[:, pp, :], ps_g[:, pp, :], cfuo[:, st2, 512:1024],
                ALU.add), inc="V")
            av = E.val("V")
            xsm, osm = f"lsx{sp}", f"lso{sp}"
            wx = [("V", xlast[sp])] if xlast[sp] else []
            E.emit("sync", lambda b, row0=row0, sp=sp: b.dma_start(
                out=xr2[:, sp, :], in_=xin_ap[row0:row0 + 128, :]),
                wait=wx, inc=xsm, amt=16)
            xv = E.val(xsm)
            E.emit("scalar", lambda b, sp=sp, pp=pp: b.activation(
                o2[:, sp, 0:512], ps_h[:, pp, :], AF.Tanh),
                wait=[("V", av)], inc="S")
            E.emit("scalar", lambda b, sp=sp, pp=pp: b.activation(
                o2[:, sp, 512:1024], ps_g[:, pp, :], AF.Tanh), inc="S")
            sv = E.val("S")
            hlast[pp] = sv
            wl = [("S", sv), (xsm, xv)]
            if olast[sp]:
                wl.append((osm, olast[sp]))
            E.emit("vector", lambda b, sp=sp: b.tensor_tensor(
                o2[:, sp, :], o2[:, sp, :], xr2[:, sp, :], ALU.add),
                wait=wl, inc="V")
            ov = E.val("V")
            xlast[sp] = ov
            E.emit("sync", lambda b, row0=row0, sp=sp: b.dma_start(
                out=out_ap[row0:row0 + 128, :], in_=o2[:, sp, :]),
                wait=[("V", ov)], inc=osm, amt=16)
            olast[sp] = E.val(osm)
    st.close()
    return [("lso0", E.val("lso0")), ("lso1", E.val("lso1"))]


def build_lstm_program():
    nc = bass.Bass("TRN2", target_bir_lowering=False)
    xin = nc.dram_tensor("xin", [NSTEP * SEQ2, C], F32, kind="ExternalInput").ap()
    yin = nc.dram_tensor("yin", [128, 8, NSTEP * SEQ2], BF16,
                         kind="ExternalInput").ap()
    w_aps = {}
    for g in ("f", "i", "c", "o"):
        w = nc.dram_tensor(f"w{g}", [128, 8, C], F32, kind="ExternalInput").ap()
        u = nc.dram_tensor(f"u{g}", [128, 8, C], F32, kind="ExternalInput").ap()
        w_aps[g] = (w, u)
    b_aps = {
        "bcols": nc.dram_tensor("bcols", [128, 24], F32, kind="ExternalInput").ap(),
        "bo": nc.dram_tensor("bo", [1, C], F32, kind="ExternalInput").ap(),
        "inc": nc.dram_tensor("incc", [128, 8], F32, kind="ExternalInput").ap(),
    }
    out = nc.dram_tensor("out", [NSTEP * SEQ2, C], F32, kind="ExternalOutput").ap()
    E = Emitter(nc)
    E.con = make_consts(nc)
    done = emit_lstm_block(E, xin, yin, out, w_aps, b_aps, {})
    E.emit("sync", lambda b: None, wait=done)
    E.finalize()
    return nc


def prep_lstm_weights(wf, uf, bf, wi, ui, bi, wc, uc, bc, wo, uo, bo, in_c):
    w_aps = {}
    for g, (w, u) in (("f", (wf, uf)), ("i", (wi, ui)), ("c", (wc, uc)),
                      ("o", (wo, uo))):
        w_aps[f"w{g}"] = _chunk_lhst(np.ascontiguousarray(np.asarray(w).T))
        w_aps[f"u{g}"] = _chunk_lhst(np.ascontiguousarray(np.asarray(u).T))
    bcols = np.concatenate([_col_t(np.asarray(bf)), _col_t(np.asarray(bi)),
                            _col_t(np.asarray(bc))], axis=1)
    bo_r = np.ascontiguousarray(np.asarray(bo).reshape(1, C))
    incc = _col_t(np.asarray(in_c))
    return w_aps, bcols, bo_r, incc



def _chunk_lhst(a, nch=8):
    n = a.shape[1]
    return np.ascontiguousarray(a.reshape(nch, 128, n).transpose(1, 0, 2))


def _col_t(v):
    k = v.shape[0] // 128
    return np.ascontiguousarray(v.reshape(k, 128).T)


def prep_attn0_weights(w_attn, b_attn, w_proj, b_proj):
    wqkv = _chunk_lhst(np.asarray(w_attn))
    bvec = _col_t(np.asarray(b_attn))
    wo = _chunk_lhst(np.asarray(w_proj))
    bo = np.ascontiguousarray(np.asarray(b_proj).reshape(1, C))
    return wqkv, bvec, wo, bo


def prep_attn1_weights(W_Q, b_Q, W_K, b_K, W_V, b_V, W_O, b_O):
    wq = _chunk_lhst(np.ascontiguousarray(
        np.asarray(W_Q).transpose(1, 0, 2).reshape(C, C)))
    wk = _chunk_lhst(np.ascontiguousarray(
        np.asarray(W_K).transpose(1, 0, 2).reshape(C, C)))
    wv = _chunk_lhst(np.ascontiguousarray(
        np.asarray(W_V).transpose(1, 0, 2).reshape(C, C)))
    wo = _chunk_lhst(np.ascontiguousarray(np.asarray(W_O).reshape(C, C)))
    bvec = np.concatenate([_col_t(np.asarray(b_Q).reshape(-1)),
                           _col_t(np.asarray(b_K).reshape(-1)),
                           _col_t(np.asarray(b_V).reshape(-1))], axis=1)
    bo = np.ascontiguousarray(np.asarray(b_O).reshape(1, C))
    return wq, wk, wv, wo, bvec, bo


def prep_ffn_weights(w_gate, w_up, w_down):
    wg = _chunk_lhst(np.ascontiguousarray(w_gate.T))
    wu = _chunk_lhst(np.ascontiguousarray(w_up.T))
    wd = _chunk_lhst(np.ascontiguousarray(w_down.T), 24)
    return wg, wu, wd



def build_ffn_program(in_ln=True, out_ln=False):
    nc = bass.Bass("TRN2", target_bir_lowering=False)
    xin = nc.dram_tensor("xin", [TPC, C], F32, kind="ExternalInput").ap()
    wg = nc.dram_tensor("wg", [128, 8, FH], I32, kind="ExternalInput").ap()
    wu = nc.dram_tensor("wu", [128, 8, FH], I32, kind="ExternalInput").ap()
    wd = nc.dram_tensor("wd", [128, 24, C], I32, kind="ExternalInput").ap()
    xout = nc.dram_tensor("xout", [TPC, C], F32, kind="ExternalOutput").ap()
    E = Emitter(nc)
    E.con = make_consts(nc)
    if in_ln:
        ydram = nc.dram_tensor("ffn_ydram", [128, 8, TPC], BF16, kind="Internal")
        ln_done = emit_ln_pass(
            E, [xin[128 * i:128 * (i + 1), :] for i in range(TPC // 128)],
            ydram.ap(), "ln0")
        E.barrier(ln_done)
    else:
        ydram = nc.dram_tensor("yin", [128, 8, TPC], BF16, kind="ExternalInput")
    ffn_done = emit_ffn_block(E, xin, ydram.ap(), xout, wg, wu, wd, {})
    if out_ln:
        yout = nc.dram_tensor("yout", [128, 8, TPC], BF16,
                              kind="ExternalOutput").ap()
        E.barrier(ffn_done)
        ln2 = emit_ln_pass(
            E, [xout[128 * i:128 * (i + 1), :] for i in range(TPC // 128)],
            yout, "ln2")
        E.emit("sync", lambda b: None, wait=ln2 + ffn_done)
    else:
        E.emit("sync", lambda b: None, wait=ffn_done)
    E.finalize()
    return nc


def build_attn_program(layer, out_ln=True, in_ln=True, dbg_nqt=16, dbg_nb=BPC):
    nc = bass.Bass("TRN2", target_bir_lowering=False)
    xin = nc.dram_tensor("xin", [TPC, C], F32, kind="ExternalInput").ap()
    if layer == 0:
        wqkv = nc.dram_tensor("wqkv", [128, 8, 3 * C], F32,
                              kind="ExternalInput").ap()
        wq_ap, wk_ap, wv_ap = (wqkv[:, :, 0:C], wqkv[:, :, C:2 * C],
                               wqkv[:, :, 2 * C:3 * C])
    else:
        wq_ap = nc.dram_tensor("wq", [128, 8, C], F32, kind="ExternalInput").ap()
        wk_ap = nc.dram_tensor("wk", [128, 8, C], F32, kind="ExternalInput").ap()
        wv_ap = nc.dram_tensor("wv", [128, 8, C], F32, kind="ExternalInput").ap()
    wo = nc.dram_tensor("wo", [128, 8, C], F32, kind="ExternalInput").ap()
    bvec = nc.dram_tensor("bvec", [128, 24], F32, kind="ExternalInput").ap()
    bo = nc.dram_tensor("bo", [1, C], F32, kind="ExternalInput").ap()
    xout = nc.dram_tensor("xout", [TPC, C], F32, kind="ExternalOutput").ap()
    yout = nc.dram_tensor("yout", [128, 8, TPC], BF16,
                          kind="ExternalOutput").ap()
    if in_ln:
        ydram = nc.dram_tensor("att_ydram", [128, 8, TPC], BF16, kind="Internal")
    else:
        ydram = nc.dram_tensor("yin", [128, 8, TPC], BF16, kind="ExternalInput")
    E = Emitter(nc)
    E.con = make_consts(nc)
    E.dbgP_ap = nc.dram_tensor("dbgP", [128, 128], BF16, kind="ExternalOutput").ap()
    E.dbgA_ap = nc.dram_tensor("dbgA", [128, 8], F32, kind="ExternalOutput").ap()
    if layer == 0:
        b48, selc = make_alibi_consts(nc)
        E.con["alibi_b48"] = b48
        E.con["alibi_sel"] = selc
    if in_ln:
        ln_done = emit_ln_pass(
            E, [xin[128 * i:128 * (i + 1), :] for i in range(TPC // 128)],
            ydram.ap(), "lni")
        E.barrier(ln_done)
    at_done = emit_attn_block(E, xin, ydram.ap(), xout, wq_ap, wk_ap, wv_ap,
                              wo, bvec, bo, layer, "att", dbg_nqt=dbg_nqt,
                              dbg_nb=dbg_nb)
    E.barrier(at_done)
    if out_ln:
        ln2 = emit_ln_pass(
            E, [xout[128 * i:128 * (i + 1), :] for i in range(TPC // 128)],
            yout, "lno")
        E.emit("sync", lambda b: None, wait=ln2 + at_done)
    else:
        E.emit("sync", lambda b: None, wait=at_done)
    E.finalize()
    return nc


def _core_inmaps(inputs):
    x = np.ascontiguousarray(np.asarray(inputs["L0_attn_x"], np.float32))
    wqkv, bvec0, wo0, bo0 = prep_attn0_weights(
        inputs["L0_attn_w_attn"], inputs["L0_attn_b_attn"],
        inputs["L0_attn_w_proj"], inputs["L0_attn_b_proj"])
    wg, wu, wd = prep_ffn_weights(
        np.asarray(inputs["L0_ffn_w_gate"], np.int32),
        np.asarray(inputs["L0_ffn_w_up"], np.int32),
        np.asarray(inputs["L0_ffn_w_down"], np.int32))
    wq1, wk1, wv1, wo1, bvec1, bo1 = prep_attn1_weights(
        inputs["L1_attn_W_Q"], inputs["L1_attn_b_Q"],
        inputs["L1_attn_W_K"], inputs["L1_attn_b_K"],
        inputs["L1_attn_W_V"], inputs["L1_attn_b_V"],
        inputs["L1_attn_W_O"], inputs["L1_attn_b_O"])
    w_aps, bcols, bo_r, incc = prep_lstm_weights(
        inputs["L1_ffn_wf"], inputs["L1_ffn_uf"], inputs["L1_ffn_bf"],
        inputs["L1_ffn_wi"], inputs["L1_ffn_ui"], inputs["L1_ffn_bi"],
        inputs["L1_ffn_wc"], inputs["L1_ffn_uc"], inputs["L1_ffn_bc"],
        inputs["L1_ffn_wo"], inputs["L1_ffn_uo"], inputs["L1_ffn_bo"],
        inputs["L1_ffn_in_c"])
    return dict(x=x, wqkv=wqkv, bvec0=bvec0, wo0=wo0, bo0=bo0,
                wg=wg, wu=wu, wd=wd, wq1=wq1, wk1=wk1, wv1=wv1, wo1=wo1,
                bvec1=bvec1, bo1=bo1, lstm_w=w_aps, bcols=bcols, bo_r=bo_r,
                incc=incc)


def build_fused_program():
    nc = bass.Bass("TRN2", target_bir_lowering=False)
    xin = nc.dram_tensor("xin", [TPC, C], F32, kind="ExternalInput").ap()
    wqkv = nc.dram_tensor("wqkv", [128, 8, 3 * C], F32, kind="ExternalInput").ap()
    wo0 = nc.dram_tensor("wo0", [128, 8, C], F32, kind="ExternalInput").ap()
    bvec0 = nc.dram_tensor("bvec0", [128, 24], F32, kind="ExternalInput").ap()
    bo0 = nc.dram_tensor("bo0", [1, C], F32, kind="ExternalInput").ap()
    wg = nc.dram_tensor("wg", [128, 8, FH], I32, kind="ExternalInput").ap()
    wu = nc.dram_tensor("wu", [128, 8, FH], I32, kind="ExternalInput").ap()
    wd = nc.dram_tensor("wd", [128, 24, C], I32, kind="ExternalInput").ap()
    wq1 = nc.dram_tensor("wq1", [128, 8, C], F32, kind="ExternalInput").ap()
    wk1 = nc.dram_tensor("wk1", [128, 8, C], F32, kind="ExternalInput").ap()
    wv1 = nc.dram_tensor("wv1", [128, 8, C], F32, kind="ExternalInput").ap()
    wo1 = nc.dram_tensor("wo1", [128, 8, C], F32, kind="ExternalInput").ap()
    bvec1 = nc.dram_tensor("bvec1", [128, 24], F32, kind="ExternalInput").ap()
    bo1 = nc.dram_tensor("bo1", [1, C], F32, kind="ExternalInput").ap()
    w_aps = {}
    for g in ("f", "i", "c", "o"):
        w = nc.dram_tensor(f"w{g}", [128, 8, C], F32, kind="ExternalInput").ap()
        u = nc.dram_tensor(f"u{g}", [128, 8, C], F32, kind="ExternalInput").ap()
        w_aps[g] = (w, u)
    b_aps = {
        "bcols": nc.dram_tensor("bcols", [128, 24], F32, kind="ExternalInput").ap(),
        "bo": nc.dram_tensor("bo", [1, C], F32, kind="ExternalInput").ap(),
        "inc": nc.dram_tensor("incc", [128, 8], F32, kind="ExternalInput").ap(),
    }
    out = nc.dram_tensor("out", [NSTEP * SEQ2, C], F32, kind="ExternalOutput").ap()
    x1 = nc.dram_tensor("x1s", [TPC, C], F32, kind="Internal").ap()
    x2 = nc.dram_tensor("x2s", [TPC, C], F32, kind="Internal").ap()
    x3 = nc.dram_tensor("x3s", [TPC, C], F32, kind="Internal")
    x4 = nc.dram_tensor("x4s", [TPC, C], F32, kind="Internal")
    ya = nc.dram_tensor("yas", [128, 8, TPC], BF16, kind="Internal").ap()
    yb = nc.dram_tensor("ybs", [128, 8, TPC], BF16, kind="Internal").ap()

    E = Emitter(nc)
    E.con = make_consts(nc)
    b48, selc = make_alibi_consts(nc)
    E.con["alibi_b48"] = b48
    E.con["alibi_sel"] = selc

    tiles = lambda ap: [ap[128 * i:128 * (i + 1), :] for i in range(TPC // 128)]
    d = emit_ln_pass(E, tiles(xin), ya, "ln0")
    E.barrier(d)
    d = emit_attn_block(E, xin, ya, x1, wqkv[:, :, 0:C], wqkv[:, :, C:2 * C],
                        wqkv[:, :, 2 * C:3 * C], wo0, bvec0, bo0, 0, "a0")
    E.barrier(d)
    d = emit_ln_pass(E, tiles(x1), yb, "ln1")
    E.barrier(d)
    d = emit_ffn_block(E, x1, yb, x2, wg, wu, wd, {})
    E.barrier(d)
    d = emit_ln_pass(E, tiles(x2), ya, "ln2")
    E.barrier(d)

    def a2a_row(bb, qt):
        t0 = 128 * qt
        return (t0 // SEQ2) * 2 * SEQ2 + bb * SEQ2 + (t0 % SEQ2)
    d = emit_attn_block(E, x2, ya, x3.ap(), wq1, wk1, wv1, wo1, bvec1, bo1,
                        1, "a1", out_row_fn=a2a_row)
    E.barrier(d)
    E.emit("gpsimd", lambda b: b.collective_compute(
        "AllToAll", ALU.bypass, replica_groups=[list(range(NCORES))],
        ins=[x3.ap().opt()], outs=[x4.ap().opt()]), inc="CC", amt=1)
    E.barrier([("CC", 1)])
    d = emit_ln_pass(E, tiles(x4.ap()), yb, "ln3")
    E.barrier(d)
    d = emit_lstm_block(E, x4.ap(), yb, out, w_aps, b_aps, {})
    E.emit("sync", lambda b: None, wait=d)
    E.finalize()
    return nc


def kernel_fused(**inputs):
    from concourse.aot_env import allow_ici_all_to_all
    pp = _core_inmaps(inputs)
    x = pp["x"]
    cores = list(range(NCORES))
    with allow_ici_all_to_all():
        nc = build_fused_program()
        im = []
        for r in cores:
            m = {"xin": np.ascontiguousarray(x[2 * r:2 * r + 2].reshape(TPC, C)),
                 "wqkv": pp["wqkv"], "wo0": pp["wo0"], "bvec0": pp["bvec0"],
                 "bo0": pp["bo0"], "wg": pp["wg"], "wu": pp["wu"], "wd": pp["wd"],
                 "wq1": pp["wq1"], "wk1": pp["wk1"], "wv1": pp["wv1"],
                 "wo1": pp["wo1"], "bvec1": pp["bvec1"], "bo1": pp["bo1"],
                 "bcols": pp["bcols"], "bo": pp["bo_r"], "incc": pp["incc"]}
            m.update(pp["lstm_w"])
            im.append(m)
        res = run_bass_kernel_spmd(nc, im, core_ids=cores).results
    out = np.empty((NSTEP, NCORES * SEQ2, C), np.float32)
    for r in cores:
        out[:, SEQ2 * r:SEQ2 * (r + 1), :] = res[r]["out"].reshape(
            NSTEP, SEQ2, C)
    return out


def kernel(**inputs):
    _ = inputs.get("L1_attn_pos_embed")
    return kernel_fused(**inputs)


def kernel_unfused(**inputs):
    pp = _core_inmaps(inputs)
    x = pp["x"]
    cores = list(range(NCORES))
    nc1 = build_attn_program(0, out_ln=True, in_ln=True)
    im1 = [{"xin": np.ascontiguousarray(x[2 * r:2 * r + 2].reshape(TPC, C)),
            "wqkv": pp["wqkv"], "wo": pp["wo0"], "bvec": pp["bvec0"],
            "bo": pp["bo0"]} for r in cores]
    r1 = run_bass_kernel_spmd(nc1, im1, core_ids=cores).results
    nc2 = build_ffn_program(in_ln=False, out_ln=True)
    im2 = [{"xin": r1[r]["xout"], "yin": r1[r]["yout"], "wg": pp["wg"],
            "wu": pp["wu"], "wd": pp["wd"]} for r in cores]
    r2 = run_bass_kernel_spmd(nc2, im2, core_ids=cores).results
    nc3 = build_attn_program(1, out_ln=True, in_ln=False)
    im3 = [{"xin": r2[r]["xout"], "yin": r2[r]["yout"], "wq": pp["wq1"],
            "wk": pp["wk1"], "wv": pp["wv1"], "wo": pp["wo1"],
            "bvec": pp["bvec1"], "bo": pp["bo1"]} for r in cores]
    r3 = run_bass_kernel_spmd(nc3, im3, core_ids=cores).results
    x3 = np.stack([r3[r]["xout"] for r in cores])
    y3 = np.stack([r3[r]["yout"] for r in cores])
    nc4 = build_lstm_program()
    im4 = []
    for r in cores:
        xs = np.empty((NSTEP * SEQ2, C), np.float32)
        ys = np.empty((128, 8, NSTEP * SEQ2), y3.dtype)
        for t in range(NSTEP):
            src_r = t // 2
            off = (t % 2) * TB + SEQ2 * r
            xs[SEQ2 * t:SEQ2 * (t + 1)] = x3[src_r][off:off + SEQ2]
            ys[:, :, SEQ2 * t:SEQ2 * (t + 1)] = y3[src_r][:, :, off:off + SEQ2]
        im4.append(dict(xin=xs, yin=np.ascontiguousarray(ys),
                        bcols=pp["bcols"], bo=pp["bo_r"], incc=pp["incc"],
                        **pp["lstm_w"]))
    r4 = run_bass_kernel_spmd(nc4, im4, core_ids=cores).results
    out = np.empty((NSTEP, NCORES * SEQ2, C), np.float32)
    for r in cores:
        out[:, SEQ2 * r:SEQ2 * (r + 1), :] = r4[r]["out"].reshape(
            NSTEP, SEQ2, C)
    return out


# version 08
# speedup vs baseline: 1.2219x; 1.2219x over previous
import contextlib
import numpy as np
import ml_dtypes
import concourse.bass as bass
import concourse.mybir as mybir
from concourse.bass_utils import run_bass_kernel_spmd

F32 = mybir.dt.float32
BF16 = mybir.dt.bfloat16
I32 = mybir.dt.int32
AF = mybir.ActivationFunctionType
ALU = mybir.AluOpType
AX = mybir.AxisListType
BF = ml_dtypes.bfloat16

NCORES = 8
C = 1024
NH = 16
DH = 64
FH = 3072
TB = 2048
BPC = 2
TPC = TB * BPC
SEQ2 = 256
NSTEP = 16
EPS = 1e-5
NEGBIG = -1e9
RCP127 = float(np.float32(1.0 / 127.0))

ENGS = ("sync", "gpsimd", "tensor", "vector", "scalar")
DBG = set()


class Emitter:

    def __init__(self, nc):
        self.nc = nc
        self.ops = {e: [] for e in ENGS}
        self.sems = {}
        self.cnt = {}
        self.producer = {}
        self.dma_sem = set()
        self.hw = {e: {} for e in ENGS}

    def sem(self, name):
        if name not in self.sems:
            self.sems[name] = self.nc.alloc_semaphore(name)
            self.cnt[name] = 0
        return self.sems[name]

    def emit(self, eng, fn, wait=(), inc=None, amt=1):
        for s, _v in wait:
            self.sem(s)
        if inc is not None:
            self.sem(inc)
            self.cnt[inc] += amt
            if amt >= 16:
                self.dma_sem.add(inc)
            p = self.producer.get(inc)
            if p is None:
                self.producer[inc] = eng
            elif p != eng:
                self.producer[inc] = "*"
        sems = self.sems
        hw = self.hw[eng]
        kept = []
        for s, v in wait:
            if v <= 0:
                continue
            if v <= hw.get(s, 0):
                continue
            if (s not in self.dma_sem and self.producer.get(s) == eng):
                continue
            hw[s] = v
            kept.append((s, v))

        def g(b, fn=fn, kept=tuple(kept), inc=inc, amt=amt):
            for s, v in kept:
                b.wait_ge(sems[s], v)
            ins = fn(b)
            if inc is not None:
                assert ins is not None, "inc requires an instruction"
                ins.then_inc(sems[inc], amt)
        self.ops[eng].append(g)

    def val(self, sem):
        self.sem(sem)
        return self.cnt[sem]

    def barrier(self, waits, engines=ENGS):
        for e in engines:
            self.emit(e, lambda b: None, wait=waits)

    def finalize(self):
        nc = self.nc
        with nc.Block() as block:
            @block.sync
            def _(b):
                for g in self.ops["sync"]:
                    g(b)

            @block.gpsimd
            def _(b):
                for g in self.ops["gpsimd"]:
                    g(b)

            @block.tensor
            def _(b):
                for g in self.ops["tensor"]:
                    g(b)

            @block.vector
            def _(b):
                for g in self.ops["vector"]:
                    g(b)

            @block.scalar
            def _(b):
                for g in self.ops["scalar"]:
                    g(b)



def make_consts(nc):
    con = {}
    con["ident"] = nc.inline_tensor(np.eye(128, dtype=BF), name="ident_c")
    con["diagmask"] = nc.inline_tensor(
        (np.triu(np.ones((128, 128), np.float32), 1) * NEGBIG), name="diagmask_c")
    con["ones2"] = nc.inline_tensor(np.ones((2, 128), dtype=BF), name="ones2_c")
    return con


def make_alibi_consts(nc):
    kt = np.arange(TB, dtype=np.float64) - (TB - 1)
    rows = np.zeros((48, TB), np.float64)
    for h in range(NH):
        slope = 2.0 ** (-0.5 * (h + 1))
        v = slope * kt
        b1 = v.astype(BF).astype(np.float64)
        b2 = (v - b1).astype(BF).astype(np.float64)
        b3 = (v - b1 - b2).astype(BF).astype(np.float64)
        rows[3 * h + 0] = b1
        rows[3 * h + 1] = b2
        rows[3 * h + 2] = b3
    sel = np.zeros((48, NH * 128), np.float32)
    for h in range(NH):
        sel[3 * h:3 * h + 3, 128 * h:128 * (h + 1)] = 1.0
    b48 = nc.inline_tensor(rows.astype(BF), name="alibi_b48_c")
    selc = nc.inline_tensor(sel.astype(BF), name="alibi_sel_c")
    return b48, selc



def emit_act_warmup(E, funcs, scratch):
    E.emit("vector", lambda b: b.memset(scratch[:, :], 1.0), inc="V")
    mw = [("V", E.val("V"))]
    for k in range(3):
        for f in funcs:
            E.emit("scalar", lambda b, f=f: b.activation(
                scratch[:, 0:256], scratch[:, 0:256], f), wait=mw, inc="S")
            mw = [("S", E.val("S"))]


def emit_ln_pass(E, src_aps, ydram_ap, tag, capture=None):
    nc = E.nc
    nt = len(src_aps)
    st = contextlib.ExitStack()
    xb_all = st.enter_context(nc.sbuf_tensor(f"{tag}_xb", [128, nt, C], BF16))
    stt = st.enter_context(nc.sbuf_tensor(f"{tag}_st", [128, nt, 8], F32))
    scr = st.enter_context(nc.sbuf_tensor(f"{tag}_scr", [128, C], BF16))
    scr2 = st.enter_context(nc.sbuf_tensor(f"{tag}_scr2", [128, C], BF16))
    ytile = st.enter_context(nc.sbuf_tensor(f"{tag}_yt", [128, 2, C], BF16))
    ystage = st.enter_context(nc.sbuf_tensor(f"{tag}_ys", [128, 2, C], BF16))
    warm = st.enter_context(nc.sbuf_tensor(f"{tag}_warm", [128, 256], F32))
    ident_sb = st.enter_context(nc.sbuf_tensor(f"{tag}_id", [128, 128], BF16))
    pst2 = st.enter_context(nc.psum_tensor(f"{tag}_psT", [128, 2, 1024], BF16))

    eps_col = st.enter_context(nc.sbuf_tensor(f"{tag}_eps", [128, 1], F32))
    E.emit("vector", lambda b: b.memset(eps_col[:, :], EPS))
    E.emit("sync", lambda b: b.dma_start(out=ident_sb[:, :],
                                         in_=E.con["ident"].ap()),
           inc="Dc", amt=16)
    idv = E.val("Dc")
    emit_act_warmup(E, [AF.Ln, AF.Square, AF.Copy], warm)
    nt4 = (nt + 3) // 4
    gsem = {}
    for g in range(4):
        lo, hi = g * nt4, min((g + 1) * nt4, nt)
        if lo >= hi:
            continue
        name = f"Gx{g}"
        for i in range(lo, hi):
            E.emit("gpsimd", lambda b, i=i: b.dma_start(
                out=xb_all[:, i, :], in_=src_aps[i]), inc=name, amt=16)
        gv = E.val(name)
        for i in range(lo, hi):
            E.emit("vector", lambda b, i=i: b.tensor_scalar(
                scr[:, :], xb_all[:, i, :], 1.0, 0.0, ALU.mult, ALU.add,
                accum_out=stt[:, i, 0:1]), wait=[(name, gv)], inc="V")
            E.emit("scalar", lambda b, i=i: b.activation(
                scr2[:, :], xb_all[:, i, :], AF.Square,
                accum_out=stt[:, i, 1:2]), wait=[(name, gv)], inc="S")
    v_stats = E.val("V")
    s_stats = E.val("S")
    spacer = lambda b: b.tensor_copy(scr[:, 0:512], scr2[:, 0:512])
    E.emit("vector", lambda b: b.tensor_scalar(
        stt[:, :, 2:3], stt[:, :, 0:1], 1.0 / C, None, ALU.mult),
        wait=[("V", v_stats), ("S", s_stats)])
    E.emit("vector", spacer)
    E.emit("vector", lambda b: b.tensor_tensor(
        stt[:, :, 3:4], stt[:, :, 2:3], stt[:, :, 2:3], ALU.mult))
    E.emit("vector", spacer)
    E.emit("vector", lambda b: b.scalar_tensor_tensor(
        stt[:, :, 4:5], stt[:, :, 1:2], 1.0 / C, stt[:, :, 3:4],
        ALU.mult, ALU.subtract))
    E.emit("vector", spacer)
    E.emit("vector", spacer, inc="V")
    v_var = E.val("V")
    E.emit("scalar", lambda b: b.activation(
        stt[:, :, 5:6], stt[:, :, 4:5], AF.Ln, bias=eps_col[:, 0:1]),
        wait=[("V", v_var)])
    emit_act_warmup(E, [AF.Exp], warm)
    E.emit("scalar", lambda b: b.activation(
        stt[:, :, 6:7], stt[:, :, 5:6], AF.Exp, scale=-0.5))
    E.emit("scalar", lambda b: b.activation(warm[:, 0:256], warm[:, 0:256],
                                            AF.Copy), inc="S")
    s_rstd = E.val("S")
    E.emit("vector", lambda b: b.scalar_tensor_tensor(
        stt[:, :, 7:8], stt[:, :, 2:3], -1.0, stt[:, :, 6:7],
        ALU.mult, ALU.mult), wait=[("S", s_rstd)], inc="V")
    E.emit("vector", spacer)
    E.emit("vector", spacer)
    v_nmr = E.val("V")
    if capture is not None:
        E.emit("vector", lambda b: b.tensor_copy(capture[:, 0:8 * nt],
                                                 stt[:, :, :]), inc="V")
    ev = [0, 0]
    yw = [0, 0]
    sv = [0, 0]
    for i in range(nt):
        p = i % 2
        w = [("V", v_nmr)]
        if yw[p]:
            w.append(("T", yw[p]))
        E.emit("vector", lambda b, i=i, p=p: b.tensor_scalar(
            ytile[:, p, :], xb_all[:, i, :], stt[:, i, 6:7], stt[:, i, 7:8],
            ALU.mult, ALU.add), wait=w, inc="V")
        v_y = E.val("V")
        w = [("V", v_y), ("Dc", idv)]
        if ev[p]:
            w.append(("V", ev[p]))

        def tr(b, p=p):
            ins = None
            for c in range(8):
                ins = b.transpose(pst2[:, p, 128 * c:128 * (c + 1)],
                                  ytile[:, p, 128 * c:128 * (c + 1)],
                                  ident_sb[:, :])
            return ins
        E.emit("tensor", tr, wait=w, inc="T")
        yw[p] = E.val("T")
        ysem = f"{tag}y{p}"
        w = [("T", yw[p])]
        if sv[p]:
            w.append((ysem, sv[p]))
        E.emit("vector", lambda b, p=p: b.tensor_copy(
            ystage[:, p, :], pst2[:, p, :]), wait=w, inc="V")
        ev[p] = E.val("V")
        E.emit("sync", lambda b, p=p, i=i: b.dma_start(
            out=ydram_ap[:, :, 128 * i:128 * (i + 1)],
            in_=ystage[:, p, :].rearrange("x (a c) -> x a c", a=8)),
            wait=[("V", ev[p])], inc=ysem, amt=16)
        sv[p] = E.val(ysem)
    st.close()
    return [(f"{tag}y0", E.val(f"{tag}y0")), (f"{tag}y1", E.val(f"{tag}y1"))]



def emit_ffn_block(E, xin_ap, yin_ap, xout_ap, wg_ap, wu_ap, wd_ap, ctx,
                   ntok=TPC):
    nc = E.nc
    st = contextlib.ExitStack()
    wg_sb = st.enter_context(nc.sbuf_tensor("ffn_wg", [128, 8, FH], BF16))
    wu_sb = st.enter_context(nc.sbuf_tensor("ffn_wu", [128, 8, FH], BF16))
    wd_sb = st.enter_context(nc.sbuf_tensor("ffn_wd", [128, 24, C], BF16))
    warm_sb = st.enter_context(nc.sbuf_tensor("ffn_warm", [128, 256], F32))
    emit_act_warmup(E, [AF.Tanh], warm_sb)

    with nc.sbuf_tensor("ffn_stage", [128, 2, FH], I32) as stage:
        last_dq = [0, 0]
        srcs = [(wg_ap, wg_sb, 8, FH, RCP127), (wu_ap, wu_sb, 8, FH, RCP127),
                (wd_ap, wd_sb, 24, C, float(np.float32(0.5) * np.float32(RCP127)))]
        k = 0
        for (src, dst, nch, wid, dqs) in srcs:
            for cc in range(nch):
                p = k % 2
                w = [("V", last_dq[p])] if last_dq[p] else []
                gsem = f"Gs{p}"
                E.emit("gpsimd", lambda b, src=src, cc=cc, p=p, wid=wid:
                       b.dma_start(out=stage[:, p, 0:wid], in_=src[:, cc, :]),
                       wait=w, inc=gsem, amt=16)
                gv = E.val(gsem)
                E.emit("vector", lambda b, dst=dst, cc=cc, p=p, wid=wid, dqs=dqs:
                       b.tensor_scalar(dst[:, cc, :], stage[:, p, 0:wid],
                                       dqs, None, ALU.mult),
                       wait=[(gsem, gv)], inc="V")
                last_dq[p] = E.val("V")
                k += 1
    dq_done = E.val("V")
    E.barrier([("V", dq_done)])

    y_blk = st.enter_context(nc.sbuf_tensor("ffn_y", [128, 2, 8, 512], BF16))
    hid = st.enter_context(nc.sbuf_tensor("ffn_hid", [128, 24, 512], BF16))
    t_sb = st.enter_context(nc.sbuf_tensor("ffn_t", [128, 2, 512], BF16))
    g_sb = st.enter_context(nc.sbuf_tensor("ffn_g", [128, 2, 512], BF16))
    xres2 = st.enter_context(nc.sbuf_tensor("ffn_xr", [128, 2, C], F32))
    out2 = st.enter_context(nc.sbuf_tensor("ffn_out", [128, 2, C], F32))
    ps_s = st.enter_context(nc.psum_tensor("ffn_psA", [128, 2048], F32))
    ps_m = st.enter_context(nc.psum_tensor("ffn_psM", [128, 1024], F32))

    last_hmul = [0, 0]
    last_gu = 0
    prev_res = 0
    last_store = [0, 0]
    last_xload = [0, 0]
    last_yload = [0, 0]

    nblk = ntok // 512
    for blk in range(nblk):
        bp = blk % 2
        w = [("T", last_yload[bp])] if last_yload[bp] else []
        ybsem = f"Dyb{bp}"
        E.emit("sync", lambda b, blk=blk, bp=bp: b.dma_start(
            out=y_blk[:, bp, :, :],
            in_=yin_ap[:, :, 512 * blk:512 * (blk + 1)]),
            wait=w, inc=ybsem, amt=16)
        yv = (ybsem, E.val(ybsem))
        for ht in range(24):
            pp = ht % 2
            gslc = (1024 * pp, 1024 * pp + 512)
            uslc = (1024 * pp + 512, 1024 * pp + 1024)
            w = [yv]
            if last_hmul[pp]:
                w.append(("V", last_hmul[pp]))

            def gu(b, ht=ht, bp=bp, gslc=gslc, uslc=uslc):
                ins = None
                for cc in range(8):
                    b.matmul(ps_s[:, gslc[0]:gslc[1]],
                             wg_sb[:, cc, 128 * ht:128 * (ht + 1)],
                             y_blk[:, bp, cc, :], start=(cc == 0),
                             stop=(cc == 7))
                for cc in range(8):
                    ins = b.matmul(ps_s[:, uslc[0]:uslc[1]],
                                   wu_sb[:, cc, 128 * ht:128 * (ht + 1)],
                                   y_blk[:, bp, cc, :], start=(cc == 0),
                                   stop=(cc == 7))
                return ins
            E.emit("tensor", gu, wait=w, inc="T")
            last_gu = E.val("T")
            E.emit("scalar", lambda b, pp=pp, gslc=gslc: b.activation(
                t_sb[:, pp, :], ps_s[:, gslc[0]:gslc[1]], AF.Tanh, scale=0.5),
                wait=[("T", last_gu)], inc="S")
            sv = E.val("S")
            E.emit("vector", lambda b, pp=pp, gslc=gslc:
                   b.scalar_tensor_tensor(g_sb[:, pp, :], t_sb[:, pp, :], 1.0,
                                          ps_s[:, gslc[0]:gslc[1]],
                                          ALU.add, ALU.mult),
                   wait=[("S", sv)])
            E.emit("vector", lambda b, ht=ht, pp=pp, uslc=uslc:
                   b.tensor_tensor(hid[:, ht, :], ps_s[:, uslc[0]:uslc[1]],
                                   g_sb[:, pp, :], ALU.mult), inc="V")
            last_hmul[pp] = E.val("V")
        last_yload[bp] = last_gu
        hv = E.val("V")
        for tt in range(4):
            p = tt % 2
            w = [("V", hv)]
            if prev_res:
                w.append(("V", prev_res))

            def down(b, tt=tt):
                ins = None
                for mc in range(2):
                    for hc in range(24):
                        ins = b.matmul(ps_m[:, 512 * mc:512 * (mc + 1)],
                                       hid[:, hc, 128 * tt:128 * (tt + 1)],
                                       wd_sb[:, hc, 512 * mc:512 * (mc + 1)],
                                       start=(hc == 0), stop=(hc == 23))
                return ins
            E.emit("tensor", down, wait=w, inc="T")
            tv = E.val("T")
            row0 = 512 * blk + 128 * tt
            xsem, osem = f"Dx{p}", f"Do{p}"
            wl = [("V", last_xload[p])] if last_xload[p] else []
            E.emit("sync", lambda b, row0=row0, p=p: b.dma_start(
                out=xres2[:, p, :], in_=xin_ap[row0:row0 + 128, :]),
                wait=wl, inc=xsem, amt=16)
            dv = E.val(xsem)
            w = [("T", tv), (xsem, dv)]
            if last_store[p]:
                w.append((osem, last_store[p]))
            E.emit("vector", lambda b, p=p: b.tensor_tensor(
                out2[:, p, :], ps_m[:, :], xres2[:, p, :], ALU.add),
                wait=w, inc="V")
            prev_res = E.val("V")
            last_xload[p] = prev_res
            E.emit("sync", lambda b, row0=row0, p=p: b.dma_start(
                out=xout_ap[row0:row0 + 128, :], in_=out2[:, p, :]),
                wait=[("V", prev_res)], inc=osem, amt=16)
            last_store[p] = E.val(osem)
    st.close()
    return [("Do0", E.val("Do0")), ("Do1", E.val("Do1"))]



def emit_attn_block(E, xin_ap, yin_ap, xout_ap, wq_ap, wk_ap, wv_ap, wo_ap,
                    bvec_ap, bo_ap, layer, tag, dbg_nqt=16, dbg_nb=BPC,
                    out_row_fn=None):
    nc = E.nc
    st = contextlib.ExitStack()
    warm = st.enter_context(nc.sbuf_tensor(f"{tag}_warm", [128, 256], F32))
    Q2 = st.enter_context(nc.sbuf_tensor(f"{tag}_q2", [128, 8, TB], BF16))
    K2 = st.enter_context(nc.sbuf_tensor(f"{tag}_k2", [128, 8, TB], BF16))
    Vt = st.enter_context(nc.sbuf_tensor(f"{tag}_vt", [128, 16, C], BF16))
    bvec = st.enter_context(nc.sbuf_tensor(f"{tag}_bv", [128, 24], F32))
    bvbf = st.enter_context(nc.sbuf_tensor(f"{tag}_bvb", [128, 8], BF16))
    bo_sb = st.enter_context(nc.sbuf_tensor(f"{tag}_bo", [1, C], F32))
    bias2 = st.enter_context(nc.sbuf_tensor(f"{tag}_b2", [2, C], BF16))
    ones2 = st.enter_context(nc.sbuf_tensor(f"{tag}_on2", [2, 128], BF16))
    dmask = st.enter_context(nc.sbuf_tensor(f"{tag}_dm", [128, 128], F32))
    ident_sb = st.enter_context(nc.sbuf_tensor(f"{tag}_id", [128, 128], BF16))
    if layer == 0:
        sel_sb = st.enter_context(nc.sbuf_tensor(f"{tag}_sel", [48, NH * 128], BF16))
        b48_sb = st.enter_context(nc.sbuf_tensor(f"{tag}_b48", [48, TB], BF16))
    ast = st.enter_context(nc.sbuf_tensor(f"{tag}_ast", [128, 2, 8], F32))
    E.dbgP = st.enter_context(nc.sbuf_tensor(f"{tag}_dbgp", [128, 128], BF16))
    E.dbgA = st.enter_context(nc.sbuf_tensor(f"{tag}_dbga", [128, 8], F32))
    P_sb = st.enter_context(nc.sbuf_tensor(f"{tag}_p", [128, 2, TB], BF16))
    PT_sb = st.enter_context(nc.sbuf_tensor(f"{tag}_pt", [128, TB], BF16))
    zn2 = st.enter_context(nc.sbuf_tensor(f"{tag}_zn", [128, 128], BF16))
    qst = st.enter_context(nc.sbuf_tensor(f"{tag}_qst", [128, 2, 128], BF16))
    zT2 = st.enter_context(nc.sbuf_tensor(f"{tag}_zt", [128, 128], BF16))
    ps_s = st.enter_context(nc.psum_tensor(f"{tag}_pss", [128, 2048], F32))
    ps_t = st.enter_context(nc.psum_tensor(f"{tag}_pst", [128, 1024], BF16))
    ps_z = st.enter_context(nc.psum_tensor(f"{tag}_psz", [128, 128], F32))
    ps_m = st.enter_context(nc.psum_tensor(f"{tag}_psm", [128, 1024], F32))

    emit_act_warmup(E, [AF.Exp], warm)
    for dst, srcap in ((bvec, bvec_ap), (bo_sb, bo_ap),
                       (ones2, E.con["ones2"].ap()),
                       (dmask, E.con["diagmask"].ap()),
                       (ident_sb, E.con["ident"].ap())):
        E.emit("sync", lambda b, dst=dst, srcap=srcap: b.dma_start(
            out=dst[:, :], in_=srcap), inc="Dc", amt=16)
    if layer == 0:
        E.emit("sync", lambda b: b.dma_start(out=sel_sb[:, :],
                                             in_=E.con["alibi_sel"].ap()),
               inc="Dc", amt=16)
        E.emit("sync", lambda b: b.dma_start(out=b48_sb[:, :],
                                             in_=E.con["alibi_b48"].ap()),
               inc="Dc", amt=16)
    dc_v = E.val("Dc")
    E.emit("vector", lambda b: b.tensor_copy(bvbf[:, :], bvec[:, 16:24]),
           wait=[("Dc", dc_v)], inc="V")
    E.emit("vector", lambda b: b.memset(qst[:, :, :], 0.0), inc="V")
    E.barrier([("Dc", dc_v), ("V", E.val("V"))])

    osem = [f"{tag}o0", f"{tag}o1"]
    xsem = [f"{tag}x0", f"{tag}x1"]
    last_store = [0, 0]
    last_xload = [0, 0]
    bias_ready = False
    prev_merge_evac = [0]

    for bb in range(dbg_nb):
        tok0 = bb * TB
        E.barrier([("T", E.val("T"))], engines=("vector", "sync", "gpsimd"))
        with contextlib.ExitStack() as stA:
            wq_sb = stA.enter_context(nc.sbuf_tensor(f"{tag}_wq{bb}", [128, 8, C], BF16))
            wk_sb = stA.enter_context(nc.sbuf_tensor(f"{tag}_wk{bb}", [128, 8, C], BF16))
            wv_sb = stA.enter_context(nc.sbuf_tensor(f"{tag}_wv{bb}", [128, 8, C], BF16))
            y_win = stA.enter_context(nc.sbuf_tensor(f"{tag}_yw{bb}", [128, 2, 8, 512], BF16))
            wsems = {}
            for nm, dst, srcap in (("q", wq_sb, wq_ap), ("k", wk_sb, wk_ap),
                                   ("v", wv_sb, wv_ap)):
                sname = f"{tag}w{nm}{bb}"
                E.emit("gpsimd", lambda b, dst=dst, srcap=srcap: b.dma_start(
                    out=dst[:, :, :], in_=srcap), inc=sname, amt=16)
                wsems[nm] = (sname, E.val(sname))
            bank_evac = [0, 0, 0, 0]
            ywin_T = [0, 0]
            kbank = [0]
            for tw in range(4):
                p = tw % 2
                w = [("T", ywin_T[p])] if ywin_T[p] else []
                ysm = f"{tag}ya{p}"
                E.emit("sync", lambda b, tw=tw, p=p, tok0=tok0: b.dma_start(
                    out=y_win[:, p, :, :],
                    in_=yin_ap[:, :, tok0 + 512 * tw: tok0 + 512 * (tw + 1)]),
                    wait=w, inc=ysm, amt=16)
                yv = E.val(ysm)

                def proj(dst, w_sb, wwait, bcol, tw=tw, p=p, yv=yv, ysm=ysm):
                    for hp in range(8):
                        bk = kbank[0] % 4
                        kbank[0] += 1
                        w = [(ysm, yv), wwait]
                        if bank_evac[bk]:
                            w.append(("V", bank_evac[bk]))

                        def mm(b, hp=hp, bk=bk, w_sb=w_sb, p=p):
                            ins = None
                            for cc in range(8):
                                ins = b.matmul(
                                    ps_s[:, 512 * bk:512 * (bk + 1)],
                                    w_sb[:, cc, 128 * hp:128 * (hp + 1)],
                                    y_win[:, p, cc, :],
                                    start=(cc == 0), stop=(cc == 7))
                            return ins
                        E.emit("tensor", mm, wait=w, inc="T")
                        tv = E.val("T")
                        E.emit("vector", lambda b, hp=hp, bk=bk, dst=dst,
                               bcol=bcol, tw=tw: b.tensor_scalar(
                                   dst[:, hp, 512 * tw:512 * (tw + 1)],
                                   ps_s[:, 512 * bk:512 * (bk + 1)],
                                   bvec[:, bcol + hp:bcol + hp + 1], None,
                                   ALU.add), wait=[("T", tv)], inc="V")
                        bank_evac[bk] = E.val("V")
                proj(Q2, wq_sb, wsems["q"], 0)
                proj(K2, wk_sb, wsems["k"], 8)
                for tt in range(4):
                    for vc in range(2):
                        bk = kbank[0] % 4
                        kbank[0] += 1
                        w = [(ysm, yv), wsems["v"]]
                        if bank_evac[bk]:
                            w.append(("V", bank_evac[bk]))

                        def mmv(b, tt=tt, vc=vc, bk=bk, p=p):
                            ins = None
                            for cc in range(8):
                                ins = b.matmul(
                                    ps_s[:, 512 * bk:512 * (bk + 1)],
                                    y_win[:, p, cc, 128 * tt:128 * (tt + 1)],
                                    wv_sb[:, cc, 512 * vc:512 * (vc + 1)],
                                    start=(cc == 0), stop=(cc == 7))
                            return ins
                        E.emit("tensor", mmv, wait=w, inc="T")
                        tv = E.val("T")
                        E.emit("vector", lambda b, tt=tt, vc=vc, bk=bk, tw=tw:
                               b.tensor_copy(
                                   Vt[:, 4 * tw + tt, 512 * vc:512 * (vc + 1)],
                                   ps_s[:, 512 * bk:512 * (bk + 1)]),
                               wait=[("T", tv)], inc="V")
                        bank_evac[bk] = E.val("V")
                ywin_T[p] = E.val("T")
        E.barrier([("V", E.val("V"))])
        with contextlib.ExitStack() as stB:
            wo_sb = stB.enter_context(nc.sbuf_tensor(f"{tag}_wo{bb}", [128, 8, C], BF16))
            comb = stB.enter_context(nc.sbuf_tensor(f"{tag}_cmb{bb}", [1, C], F32))
            hi1 = stB.enter_context(nc.sbuf_tensor(f"{tag}_hi{bb}", [1, C], BF16))
            lo1 = stB.enter_context(nc.sbuf_tensor(f"{tag}_lo{bb}", [1, C], BF16))
            out2 = stB.enter_context(nc.sbuf_tensor(f"{tag}_o2{bb}", [128, 2, C], F32))
            xres2 = stB.enter_context(nc.sbuf_tensor(f"{tag}_xr{bb}", [128, 2, C], F32))
            wosem = f"{tag}wo{bb}"
            E.emit("gpsimd", lambda b: b.dma_start(out=wo_sb[:, :, :],
                                                   in_=wo_ap), inc=wosem, amt=16)
            wo_v = E.val(wosem)
            if not bias_ready:
                bias_ready = True

                def bmm(b):
                    ins = None
                    for j in range(8):
                        for mw in range(2):
                            ins = b.matmul(ps_m[0:1, 512 * mw:512 * (mw + 1)],
                                           bvbf[:, j:j + 1],
                                           wo_sb[:, j, 512 * mw:512 * (mw + 1)],
                                           start=(j == 0), stop=(j == 7))
                    return ins
                E.emit("tensor", bmm, wait=[(wosem, wo_v)], inc="T")
                tv = E.val("T")
                E.emit("vector", lambda b: b.tensor_tensor(
                    comb[:, :], ps_m[0:1, :], bo_sb[:, :], ALU.add),
                    wait=[("T", tv)], inc="V")
                E.emit("vector", lambda b: b.tensor_copy(hi1[:, :],
                                                         comb[:, :]))
                E.emit("vector", lambda b: b.tensor_tensor(
                    lo1[:, :], comb[:, :], hi1[:, :], ALU.subtract),
                    inc="V")
                prev_merge_evac[0] = E.val("V")
                E.emit("gpsimd", lambda b: b.dma_start(
                    out=bias2[0:1, :], in_=hi1[:, :]),
                    wait=[("V", E.val("V"))], inc=f"{tag}b2", amt=16)
                E.emit("gpsimd", lambda b: b.dma_start(
                    out=bias2[1:2, :], in_=lo1[:, :]),
                    inc=f"{tag}b2", amt=16)
                E.barrier([("V", E.val("V")), (f"{tag}b2", E.val(f"{tag}b2"))],
                          engines=("tensor",))

            pend = [None]
            qst_war = [0, 0]
            reg_free = {0: (0, 0), 1: (0, 0)}
            psz_free = [0]
            pt_war = [0]
            psb_war = [0, 0]
            it = [0]

            def flush():
                if pend[0] is None or "noflush" in DBG:
                    pend[0] = None
                    return
                (qt, hp, j, par, W, s_exp) = pend[0]
                pend[0] = None
                h = 2 * hp + j
                nkt = W // 128
                for rnd in range((nkt + 7) // 8):
                    k0, k1 = 8 * rnd, min(8 * rnd + 8, nkt)

                    def tr(b, k0=k0, k1=k1, par=par):
                        ins = None
                        for k in range(k0, k1):
                            ins = b.transpose(
                                ps_t[:, 128 * (k - k0):128 * (k - k0 + 1)],
                                P_sb[:, par, 128 * k:128 * (k + 1)],
                                ident_sb[:, :])
                        return ins
                    w = [("S", s_exp)]
                    if pt_war[0]:
                        w.append(pt_war[0])
                    E.emit("tensor", tr, wait=w, inc="T")
                    tv = E.val("T")
                    E.emit("scalar", lambda b, k0=k0, k1=k1: b.activation(
                        PT_sb[:, 128 * k0:128 * k1],
                        ps_t[:, 0:128 * (k1 - k0)], AF.Copy),
                        wait=[("T", tv)], inc="Sp")
                    pt_war[0] = ("Sp", E.val("Sp"))
                psb_war[par] = E.val("T")
                w = [pt_war[0]]
                if psz_free[0]:
                    w.append(("V", psz_free[0]))

                def pv(b, nkt=nkt, h=h):
                    ins = None
                    for k in range(nkt):
                        ins = b.matmul(ps_z[:, 0:DH],
                                       PT_sb[:, 128 * k:128 * (k + 1)],
                                       Vt[:, k, DH * h:DH * h + DH],
                                       start=(k == 0), stop=(k == nkt - 1))
                    return ins
                E.emit("tensor", pv, wait=w, inc="T")
                tv = E.val("T")
                E.emit("vector", lambda b, j=j, par=par: b.tensor_scalar(
                    zn2[:, DH * j:DH * (j + 1)], ps_z[:, 0:DH],
                    ast[:, par, 1:2], None, ALU.mult),
                    wait=[("T", tv)], inc="V")
                psz_free[0] = E.val("V")
                if j == 1:
                    w = [("V", psz_free[0])]
                    if pt_war[0]:
                        w.append(pt_war[0])
                    E.emit("tensor", lambda b: b.transpose(
                        ps_t[:, 0:128], zn2[:, :], ident_sb[:, :]),
                        wait=w, inc="T")
                    tv = E.val("T")
                    E.emit("vector", lambda b: b.tensor_copy(
                        zT2[:, :], ps_t[:, 0:128]), wait=[("T", tv)], inc="V")
                    pt_war[0] = ("V", E.val("V"))
                    zv = E.val("V")
                    w = [("V", zv), (wosem, wo_v)]
                    if hp == 0 and prev_merge_evac[0]:
                        w.append(("V", prev_merge_evac[0]))

                    def mrg(b, hp=hp):
                        ins = None
                        for mw in range(2):
                            ins = b.matmul(ps_m[:, 512 * mw:512 * (mw + 1)],
                                           zT2[:, :],
                                           wo_sb[:, hp, 512 * mw:512 * (mw + 1)],
                                           start=(hp == 0), stop=False)
                        return ins
                    E.emit("tensor", mrg, wait=w, inc="T")

            for qt in range(dbg_nqt):
                W = 128 * (qt + 1)
                two_chunk = W > 1024
                for hp in range(8):
                    for j in range(2):
                        h = 2 * hp + j
                        it[0] += 1
                        par = it[0] % 2
                        if two_chunk:
                            regs = [0, 1]
                            roff = 0
                        else:
                            regs = [it[0] % 2]
                            roff = 1024 * (it[0] % 2)
                        w = []
                        for r in regs:
                            if reg_free[r][0]:
                                w.append(("S", reg_free[r][0]))
                            if reg_free[r][1]:
                                w.append(("V", reg_free[r][1]))

                        alibi = (layer == 0) and ("noalibi" not in DBG)
                        wq = []
                        if qst_war[j]:
                            wq.append(("T", qst_war[j]))
                        E.emit("scalar", lambda b, qt=qt, hp=hp, j=j:
                               b.activation(
                                   qst[64 * j:64 * (j + 1), j, :],
                                   Q2[64 * j:64 * (j + 1), hp,
                                      128 * qt:128 * (qt + 1)],
                                   AF.Copy, scale=0.125),
                               wait=wq, inc="S")
                        w.append(("S", E.val("S")))

                        def smm(b, hp=hp, j=j, W=W, roff=roff, h=h,
                                alibi=alibi):
                            ins = None
                            for wnd in range(0, W, 512):
                                wid = min(512, W - wnd)
                                ins = b.matmul(
                                    ps_s[:, roff + wnd:roff + wnd + wid],
                                    qst[:, j, :],
                                    K2[:, hp, wnd:wnd + wid],
                                    start=True, stop=not alibi)
                                if alibi:
                                    ins = b.matmul(
                                        ps_s[:, roff + wnd:roff + wnd + wid],
                                        sel_sb[:, 128 * h:128 * (h + 1)],
                                        b48_sb[:, wnd:wnd + wid],
                                        start=False, stop=True)
                            return ins
                        if "nosmm" in DBG:
                            E.emit("tensor", lambda b, roff=roff, W=W: b.matmul(
                                ps_s[:, roff:roff + W], ident_sb[:, 0:128],
                                K2[:, 0, 0:W], start=True, stop=True),
                                wait=w, inc="T")
                        else:
                            E.emit("tensor", smm, wait=w, inc="T")
                        t_s = E.val("T")
                        qst_war[j] = t_s
                        E.emit("vector", lambda b, roff=roff, W=W:
                               b.tensor_tensor(
                                   ps_s[:, roff + W - 128:roff + W],
                                   ps_s[:, roff + W - 128:roff + W],
                                   dmask[:, :], ALU.add),
                               wait=[("T", t_s)], inc="V")
                        nw = (W + 511) // 512
                        for wi in range(nw):
                            wnd = 512 * wi
                            wid = min(512, W - wnd)
                            E.emit("vector", lambda b, roff=roff, wnd=wnd,
                                   wid=wid, par=par, wi=wi: b.tensor_reduce(
                                       ast[:, par, 4 + wi:5 + wi],
                                       ps_s[:, roff + wnd:roff + wnd + wid],
                                       AX.X, ALU.max, negate=True),
                                   inc=("V" if wi == nw - 1 else None))
                        had_pend = pend[0] is not None
                        flush()
                        if nw > 1:
                            if not had_pend:
                                E.emit("vector", lambda b: b.tensor_copy(
                                    warm[:, 0:256], warm[:, 0:256]))
                            E.emit("vector", lambda b, par=par, nw=nw:
                                   b.tensor_reduce(
                                       ast[:, par, 3:4],
                                       ast[:, par, 4:4 + nw],
                                       AX.X, ALU.min), inc="V")
                            bcol = 3
                        else:
                            bcol = 4
                        v_bias = E.val("V")
                        w = [("V", v_bias)]
                        if psb_war[par]:
                            w.append(("T", psb_war[par]))
                        if "noexp" in DBG:
                            E.emit("scalar", lambda b, par=par, roff=roff, W=W:
                                   b.activation(
                                       P_sb[:, par, 0:W], ps_s[:, roff:roff + W],
                                       AF.Copy), wait=w, inc="S")
                        else:
                            E.emit("scalar", lambda b, par=par, roff=roff, W=W,
                                   bcol=bcol: b.activation(
                                       P_sb[:, par, 0:W], ps_s[:, roff:roff + W],
                                       AF.Exp, bias=ast[:, par, bcol:bcol + 1],
                                       scale=1.0,
                                       accum_out=ast[:, par, 0:1]),
                                   wait=w, inc="S")
                        s_exp = E.val("S")
                        reg_done_v = E.val("V")
                        for r in regs:
                            reg_free[r] = (s_exp, reg_done_v)
                        E.emit("vector", lambda b, par=par: b.reciprocal(
                            ast[:, par, 1:2], ast[:, par, 0:1]),
                            wait=[("S", s_exp)], inc="V")
                        if "dump" in DBG and it[0] == 1 and bb == 0:
                            E.emit("vector", lambda b, par=par: b.tensor_copy(
                                E.dbgP[:, :], P_sb[:, par, 0:128]),
                                wait=[("S", s_exp)], inc="V")
                            E.emit("vector", lambda b, par=par: b.tensor_copy(
                                E.dbgA[:, :], ast[:, par, :]), inc="V")
                            dv = E.val("V")
                            E.emit("sync", lambda b: b.dma_start(
                                out=E.dbgP_ap, in_=E.dbgP[:, :]),
                                wait=[("V", dv)], inc="Dc", amt=16)
                            E.emit("sync", lambda b: b.dma_start(
                                out=E.dbgA_ap, in_=E.dbgA[:, :]),
                                inc="Dc", amt=16)
                        pend[0] = (qt, hp, j, par, W, s_exp)
                flush()
                if "notail" in DBG:
                    continue

                def bmm2(b):
                    ins = None
                    for mw in range(2):
                        ins = b.matmul(ps_m[:, 512 * mw:512 * (mw + 1)],
                                       ones2[:, :],
                                       bias2[0:2, 512 * mw:512 * (mw + 1)],
                                       start=False, stop=True)
                    return ins
                E.emit("tensor", bmm2, inc="T")
                t_m = E.val("T")
                p = qt % 2
                row0 = tok0 + 128 * qt
                orow0 = row0 if out_row_fn is None else out_row_fn(bb, qt)
                w = [("V", last_xload[p])] if last_xload[p] else []
                E.emit("sync", lambda b, row0=row0, p=p: b.dma_start(
                    out=xres2[:, p, :], in_=xin_ap[row0:row0 + 128, :]),
                    wait=w, inc=xsem[p], amt=16)
                dv = E.val(xsem[p])
                w = [("T", t_m), (xsem[p], dv)]
                if last_store[p]:
                    w.append((osem[p], last_store[p]))
                E.emit("vector", lambda b, p=p: b.tensor_tensor(
                    out2[:, p, :], ps_m[:, :], xres2[:, p, :], ALU.add),
                    wait=w, inc="V")
                prev_merge_evac[0] = E.val("V")
                last_xload[p] = prev_merge_evac[0]
                E.emit("sync", lambda b, orow0=orow0, p=p: b.dma_start(
                    out=xout_ap[orow0:orow0 + 128, :], in_=out2[:, p, :]),
                    wait=[("V", prev_merge_evac[0])], inc=osem[p], amt=16)
                last_store[p] = E.val(osem[p])
    st.close()
    return [(osem[0], E.val(osem[0])), (osem[1], E.val(osem[1]))]



def emit_lstm_block(E, xin_ap, yin_ap, out_ap, w_aps, b_aps, ctx):
    nc = E.nc
    S2 = SEQ2
    st = contextlib.ExitStack()
    wsb = {}
    for g in ("f", "i", "c", "o"):
        wsb[g] = (st.enter_context(nc.sbuf_tensor(f"ls_w{g}", [128, 8, C], BF16)),
                  st.enter_context(nc.sbuf_tensor(f"ls_u{g}", [128, 8, C], BF16)))
    bcols = st.enter_context(nc.sbuf_tensor("ls_bc", [128, 24], F32))
    inccol = st.enter_context(nc.sbuf_tensor("ls_inc", [128, 8], F32))
    bo2 = st.enter_context(nc.sbuf_tensor("ls_bo2", [2, C], BF16))
    ones2 = st.enter_context(nc.sbuf_tensor("ls_on2", [2, 128], BF16))
    warm = st.enter_context(nc.sbuf_tensor("ls_warm", [128, 256], F32))
    c_fm = st.enter_context(nc.sbuf_tensor("ls_c", [128, 8, S2], F32))
    c_bf = st.enter_context(nc.sbuf_tensor("ls_cb", [128, 8, S2], BF16))
    f_sb = st.enter_context(nc.sbuf_tensor("ls_f", [128, 8, S2], F32))
    i_sb = st.enter_context(nc.sbuf_tensor("ls_i", [128, 8, S2], F32))
    g_sb = st.enter_context(nc.sbuf_tensor("ls_g", [128, 8, S2], F32))
    y2 = st.enter_context(nc.sbuf_tensor("ls_y", [128, 2, 8, S2], BF16))
    cfuo = st.enter_context(nc.sbuf_tensor("ls_cfuo", [128, 2, C], F32))
    xr2 = st.enter_context(nc.sbuf_tensor("ls_xr", [128, 2, C], F32))
    o2 = st.enter_context(nc.sbuf_tensor("ls_o2", [128, 2, C], F32))
    ps_g = st.enter_context(nc.psum_tensor("ls_psg", [128, 2, 512], F32))
    ps_h = st.enter_context(nc.psum_tensor("ls_psh", [128, 2, 512], F32))

    emit_act_warmup(E, [AF.Sigmoid, AF.Tanh], warm)
    for g in ("f", "i", "c", "o"):
        for k in range(2):
            E.emit("gpsimd", lambda b, g=g, k=k: b.dma_start(
                out=wsb[g][k][:, :, :], in_=w_aps[g][k]), inc=f"lw{g}{k}", amt=16)
    E.emit("sync", lambda b: b.dma_start(out=bcols[:, :], in_=b_aps["bcols"]),
           inc="Dc", amt=16)
    E.emit("sync", lambda b: b.dma_start(out=inccol[:, :], in_=b_aps["inc"]),
           inc="Dc", amt=16)
    E.emit("sync", lambda b: b.dma_start(out=ones2[:, :],
                                         in_=E.con["ones2"].ap()), inc="Dc", amt=16)
    with (nc.sbuf_tensor("ls_bo", [1, C], F32) as bo_sb,
          nc.sbuf_tensor("ls_hi", [1, C], BF16) as hi1,
          nc.sbuf_tensor("ls_lo", [1, C], BF16) as lo1):
        E.emit("sync", lambda b: b.dma_start(out=bo_sb[:, :], in_=b_aps["bo"]),
               inc="Dc", amt=16)
        dc = E.val("Dc")
        E.emit("vector", lambda b: b.memset(warm[:, :], 0.0),
               wait=[("Dc", dc), ("S", E.val("S"))])
        E.emit("vector", lambda b: b.tensor_copy(hi1[:, :], bo_sb[:, :]),
               inc="V")
        E.emit("vector", lambda b: b.tensor_tensor(
            lo1[:, :], bo_sb[:, :], hi1[:, :], ALU.subtract), inc="V")
        bv = E.val("V")
        E.emit("gpsimd", lambda b: b.dma_start(out=bo2[0:1, :], in_=hi1[:, :]),
               wait=[("V", bv)], inc="lsb2", amt=16)
        E.emit("gpsimd", lambda b: b.dma_start(out=bo2[1:2, :], in_=lo1[:, :]),
               inc="lsb2", amt=16)
        E.barrier([("lsb2", E.val("lsb2"))], engines=("vector", "scalar"))
    dc = E.val("Dc")
    wws = [(f"lw{g}{k}", E.val(f"lw{g}{k}")) for g in "fico" for k in range(2)]
    zz = warm
    for cc in range(8):
        E.emit("vector", lambda b, cc=cc: b.tensor_scalar(
            c_fm[:, cc, :], zz[:, :], inccol[:, cc:cc + 1], None, ALU.add))
    E.emit("vector", lambda b: b.tensor_copy(
        c_bf[:, :, :].rearrange("p a s -> p (a s)"),
        c_fm[:, :, :].rearrange("p a s -> p (a s)")), inc="V")
    E.barrier([("V", E.val("V")), ("Dc", dc), ("lsb2", E.val("lsb2"))] + wws)

    nstep = NSTEP
    ylast = [0, 0]
    for t in range(nstep):
        p = t % 2
        ysm = f"lsy{p}"
        w = [("T", ylast[p])] if ylast[p] else []
        E.emit("sync", lambda b, t=t, p=p: b.dma_start(
            out=y2[:, p, :, :], in_=yin_ap[:, :, S2 * t:S2 * (t + 1)]),
            wait=w, inc=ysm, amt=16)
        yv = E.val(ysm)
        gates = (("f", f_sb, AF.Sigmoid, 0), ("i", i_sb, AF.Sigmoid, 8),
                 ("c", g_sb, AF.Tanh, 16))
        bank_free = [0, 0]
        for (gn, dst, func, bcol) in gates:
            for mt in range(8):
                bk = mt % 2
                wl = [(ysm, yv)]
                if bank_free[bk]:
                    wl.append(("S", bank_free[bk]))

                def mm(b, gn=gn, mt=mt, bk=bk, p=p):
                    ins = None
                    for cc in range(8):
                        b.matmul(ps_g[:, bk, 0:S2],
                                 wsb[gn][0][:, cc, 128 * mt:128 * (mt + 1)],
                                 y2[:, p, cc, :], start=(cc == 0), stop=False)
                    for cc in range(8):
                        ins = b.matmul(
                            ps_g[:, bk, 0:S2],
                            wsb[gn][1][:, cc, 128 * mt:128 * (mt + 1)],
                            c_bf[:, cc, :], start=False, stop=(cc == 7))
                    return ins
                E.emit("tensor", mm, wait=wl, inc="T")
                tv = E.val("T")
                E.emit("scalar", lambda b, gn=gn, mt=mt, bk=bk, dst=dst,
                       func=func, bcol=bcol: b.activation(
                           dst[:, mt, :], ps_g[:, bk, 0:S2], func,
                           bias=bcols[:, bcol + mt:bcol + mt + 1]),
                       wait=[("T", tv)], inc="S")
                bank_free[bk] = E.val("S")
        ylast[p] = E.val("T")
        sg = E.val("S")
        fl = f_sb[:, :, :].rearrange("p a s -> p (a s)")
        il = i_sb[:, :, :].rearrange("p a s -> p (a s)")
        gl = g_sb[:, :, :].rearrange("p a s -> p (a s)")
        cl = c_fm[:, :, :].rearrange("p a s -> p (a s)")
        cbl = c_bf[:, :, :].rearrange("p a s -> p (a s)")
        E.emit("vector", lambda b, fl=fl, cl=cl: b.tensor_tensor(
            fl, fl, cl, ALU.mult), wait=[("S", sg)])
        E.emit("vector", lambda b, il=il, gl=gl: b.tensor_tensor(
            il, il, gl, ALU.mult))
        E.emit("vector", lambda b, cl=cl, fl=fl, il=il: b.tensor_tensor(
            cl, fl, il, ALU.add))
        E.emit("vector", lambda b, cbl=cbl, cl=cl: b.tensor_copy(cbl, cl),
               inc="V")
        cv = E.val("V")
        E.emit("tensor", lambda b: None, wait=[("V", cv)])
    if "noh" in DBG:
        st.close()
        return [("lsy0", E.val("lsy0")), ("lsy1", E.val("lsy1"))]
    cfin = E.val("V")
    for st2 in range(2):
        pst = ps_h if st2 == 0 else ps_g

        def mmc(b, st2=st2, pst=pst):
            ins = None
            for mw in range(2):
                for cc in range(8):
                    b.matmul(pst[:, mw, :],
                             c_bf[:, cc, 128 * st2:128 * (st2 + 1)],
                             wsb["o"][1][:, cc, 512 * mw:512 * (mw + 1)],
                             start=(cc == 0), stop=False)
                ins = b.matmul(pst[:, mw, :], ones2[:, :],
                               bo2[0:2, 512 * mw:512 * (mw + 1)],
                               start=False, stop=True)
            return ins
        E.emit("tensor", mmc, wait=[("V", cfin)], inc="T")
        tv = E.val("T")
        E.emit("vector", lambda b, st2=st2, pst=pst: b.tensor_copy(
            cfuo[:, st2, :], pst[:, :, :].rearrange("p a s -> p (a s)")),
            wait=[("T", tv)], inc="V")
    cfv = E.val("V")
    E.barrier([("V", cfv)], engines=("tensor",))
    if "nohh" in DBG:
        st.close()
        return [("lsy0", E.val("lsy0")), ("lsy1", E.val("lsy1"))]
    xlast = [0, 0]
    olast = [0, 0]
    hlast = [0, 0]
    for t in range(nstep):
        p = t % 2
        ysm = f"lsy{p}"
        w = [("T", ylast[p])] if ylast[p] else []
        E.emit("sync", lambda b, t=t, p=p: b.dma_start(
            out=y2[:, p, :, :], in_=yin_ap[:, :, S2 * t:S2 * (t + 1)]),
            wait=w, inc=ysm, amt=16)
        yv = E.val(ysm)
        for st2 in range(2):
            pp = st2
            row0 = S2 * t + 128 * st2
            sp = (t * 2 + st2) % 2
            wl = [(ysm, yv)]
            if hlast[pp]:
                wl.append(("S", hlast[pp]))

            def mmh(b, st2=st2, p=p, pp=pp):
                ins = None
                for mw in range(2):
                    for cc in range(8):
                        ins = b.matmul(
                            ps_h[:, pp, :] if mw == 0 else ps_h[:, pp, :],
                            y2[:, p, cc, 128 * st2:128 * (st2 + 1)],
                            wsb["o"][0][:, cc, 512 * mw:512 * (mw + 1)],
                            start=(cc == 0), stop=(cc == 7))
                return ins
            def mmh2(b, st2=st2, p=p, pp=pp):
                ins = None
                for cc in range(8):
                    b.matmul(ps_h[:, pp, :],
                             y2[:, p, cc, 128 * st2:128 * (st2 + 1)],
                             wsb["o"][0][:, cc, 0:512],
                             start=(cc == 0), stop=(cc == 7))
                for cc in range(8):
                    ins = b.matmul(ps_g[:, pp, :],
                                   y2[:, p, cc, 128 * st2:128 * (st2 + 1)],
                                   wsb["o"][0][:, cc, 512:1024],
                                   start=(cc == 0), stop=(cc == 7))
                return ins
            E.emit("tensor", mmh2, wait=wl, inc="T")
            tv = E.val("T")
            ylast[p] = tv
            E.emit("vector", lambda b, st2=st2, pp=pp: b.tensor_tensor(
                ps_h[:, pp, :], ps_h[:, pp, :], cfuo[:, st2, 0:512], ALU.add),
                wait=[("T", tv)])
            E.emit("vector", lambda b, st2=st2, pp=pp: b.tensor_tensor(
                ps_g[:, pp, :], ps_g[:, pp, :], cfuo[:, st2, 512:1024],
                ALU.add), inc="V")
            av = E.val("V")
            xsm, osm = f"lsx{sp}", f"lso{sp}"
            wx = [("V", xlast[sp])] if xlast[sp] else []
            E.emit("sync", lambda b, row0=row0, sp=sp: b.dma_start(
                out=xr2[:, sp, :], in_=xin_ap[row0:row0 + 128, :]),
                wait=wx, inc=xsm, amt=16)
            xv = E.val(xsm)
            E.emit("scalar", lambda b, sp=sp, pp=pp: b.activation(
                o2[:, sp, 0:512], ps_h[:, pp, :], AF.Tanh),
                wait=[("V", av)], inc="S")
            E.emit("scalar", lambda b, sp=sp, pp=pp: b.activation(
                o2[:, sp, 512:1024], ps_g[:, pp, :], AF.Tanh), inc="S")
            sv = E.val("S")
            hlast[pp] = sv
            wl = [("S", sv), (xsm, xv)]
            if olast[sp]:
                wl.append((osm, olast[sp]))
            E.emit("vector", lambda b, sp=sp: b.tensor_tensor(
                o2[:, sp, :], o2[:, sp, :], xr2[:, sp, :], ALU.add),
                wait=wl, inc="V")
            ov = E.val("V")
            xlast[sp] = ov
            E.emit("sync", lambda b, row0=row0, sp=sp: b.dma_start(
                out=out_ap[row0:row0 + 128, :], in_=o2[:, sp, :]),
                wait=[("V", ov)], inc=osm, amt=16)
            olast[sp] = E.val(osm)
    st.close()
    return [("lso0", E.val("lso0")), ("lso1", E.val("lso1"))]


def build_lstm_program():
    nc = bass.Bass("TRN2", target_bir_lowering=False)
    xin = nc.dram_tensor("xin", [NSTEP * SEQ2, C], F32, kind="ExternalInput").ap()
    yin = nc.dram_tensor("yin", [128, 8, NSTEP * SEQ2], BF16,
                         kind="ExternalInput").ap()
    w_aps = {}
    for g in ("f", "i", "c", "o"):
        w = nc.dram_tensor(f"w{g}", [128, 8, C], F32, kind="ExternalInput").ap()
        u = nc.dram_tensor(f"u{g}", [128, 8, C], F32, kind="ExternalInput").ap()
        w_aps[g] = (w, u)
    b_aps = {
        "bcols": nc.dram_tensor("bcols", [128, 24], F32, kind="ExternalInput").ap(),
        "bo": nc.dram_tensor("bo", [1, C], F32, kind="ExternalInput").ap(),
        "inc": nc.dram_tensor("incc", [128, 8], F32, kind="ExternalInput").ap(),
    }
    out = nc.dram_tensor("out", [NSTEP * SEQ2, C], F32, kind="ExternalOutput").ap()
    E = Emitter(nc)
    E.con = make_consts(nc)
    done = emit_lstm_block(E, xin, yin, out, w_aps, b_aps, {})
    E.emit("sync", lambda b: None, wait=done)
    E.finalize()
    return nc


def prep_lstm_weights(wf, uf, bf, wi, ui, bi, wc, uc, bc, wo, uo, bo, in_c):
    w_aps = {}
    for g, (w, u) in (("f", (wf, uf)), ("i", (wi, ui)), ("c", (wc, uc)),
                      ("o", (wo, uo))):
        w_aps[f"w{g}"] = _chunk_lhst(np.ascontiguousarray(np.asarray(w).T))
        w_aps[f"u{g}"] = _chunk_lhst(np.ascontiguousarray(np.asarray(u).T))
    bcols = np.concatenate([_col_t(np.asarray(bf)), _col_t(np.asarray(bi)),
                            _col_t(np.asarray(bc))], axis=1)
    bo_r = np.ascontiguousarray(np.asarray(bo).reshape(1, C))
    incc = _col_t(np.asarray(in_c))
    return w_aps, bcols, bo_r, incc



def _chunk_lhst(a, nch=8):
    n = a.shape[1]
    return np.ascontiguousarray(a.reshape(nch, 128, n).transpose(1, 0, 2))


def _col_t(v):
    k = v.shape[0] // 128
    return np.ascontiguousarray(v.reshape(k, 128).T)


def prep_attn0_weights(w_attn, b_attn, w_proj, b_proj):
    wqkv = _chunk_lhst(np.asarray(w_attn))
    bvec = _col_t(np.asarray(b_attn))
    wo = _chunk_lhst(np.asarray(w_proj))
    bo = np.ascontiguousarray(np.asarray(b_proj).reshape(1, C))
    return wqkv, bvec, wo, bo


def prep_attn1_weights(W_Q, b_Q, W_K, b_K, W_V, b_V, W_O, b_O):
    wq = _chunk_lhst(np.ascontiguousarray(
        np.asarray(W_Q).transpose(1, 0, 2).reshape(C, C)))
    wk = _chunk_lhst(np.ascontiguousarray(
        np.asarray(W_K).transpose(1, 0, 2).reshape(C, C)))
    wv = _chunk_lhst(np.ascontiguousarray(
        np.asarray(W_V).transpose(1, 0, 2).reshape(C, C)))
    wo = _chunk_lhst(np.ascontiguousarray(np.asarray(W_O).reshape(C, C)))
    bvec = np.concatenate([_col_t(np.asarray(b_Q).reshape(-1)),
                           _col_t(np.asarray(b_K).reshape(-1)),
                           _col_t(np.asarray(b_V).reshape(-1))], axis=1)
    bo = np.ascontiguousarray(np.asarray(b_O).reshape(1, C))
    return wq, wk, wv, wo, bvec, bo


def prep_ffn_weights(w_gate, w_up, w_down):
    wg = _chunk_lhst(np.ascontiguousarray(w_gate.T))
    wu = _chunk_lhst(np.ascontiguousarray(w_up.T))
    wd = _chunk_lhst(np.ascontiguousarray(w_down.T), 24)
    return wg, wu, wd



def build_ffn_program(in_ln=True, out_ln=False):
    nc = bass.Bass("TRN2", target_bir_lowering=False)
    xin = nc.dram_tensor("xin", [TPC, C], F32, kind="ExternalInput").ap()
    wg = nc.dram_tensor("wg", [128, 8, FH], I32, kind="ExternalInput").ap()
    wu = nc.dram_tensor("wu", [128, 8, FH], I32, kind="ExternalInput").ap()
    wd = nc.dram_tensor("wd", [128, 24, C], I32, kind="ExternalInput").ap()
    xout = nc.dram_tensor("xout", [TPC, C], F32, kind="ExternalOutput").ap()
    E = Emitter(nc)
    E.con = make_consts(nc)
    if in_ln:
        ydram = nc.dram_tensor("ffn_ydram", [128, 8, TPC], BF16, kind="Internal")
        ln_done = emit_ln_pass(
            E, [xin[128 * i:128 * (i + 1), :] for i in range(TPC // 128)],
            ydram.ap(), "ln0")
        E.barrier(ln_done)
    else:
        ydram = nc.dram_tensor("yin", [128, 8, TPC], BF16, kind="ExternalInput")
    ffn_done = emit_ffn_block(E, xin, ydram.ap(), xout, wg, wu, wd, {})
    if out_ln:
        yout = nc.dram_tensor("yout", [128, 8, TPC], BF16,
                              kind="ExternalOutput").ap()
        E.barrier(ffn_done)
        ln2 = emit_ln_pass(
            E, [xout[128 * i:128 * (i + 1), :] for i in range(TPC // 128)],
            yout, "ln2")
        E.emit("sync", lambda b: None, wait=ln2 + ffn_done)
    else:
        E.emit("sync", lambda b: None, wait=ffn_done)
    E.finalize()
    return nc


def build_attn_program(layer, out_ln=True, in_ln=True, dbg_nqt=16, dbg_nb=BPC):
    nc = bass.Bass("TRN2", target_bir_lowering=False)
    xin = nc.dram_tensor("xin", [TPC, C], F32, kind="ExternalInput").ap()
    if layer == 0:
        wqkv = nc.dram_tensor("wqkv", [128, 8, 3 * C], F32,
                              kind="ExternalInput").ap()
        wq_ap, wk_ap, wv_ap = (wqkv[:, :, 0:C], wqkv[:, :, C:2 * C],
                               wqkv[:, :, 2 * C:3 * C])
    else:
        wq_ap = nc.dram_tensor("wq", [128, 8, C], F32, kind="ExternalInput").ap()
        wk_ap = nc.dram_tensor("wk", [128, 8, C], F32, kind="ExternalInput").ap()
        wv_ap = nc.dram_tensor("wv", [128, 8, C], F32, kind="ExternalInput").ap()
    wo = nc.dram_tensor("wo", [128, 8, C], F32, kind="ExternalInput").ap()
    bvec = nc.dram_tensor("bvec", [128, 24], F32, kind="ExternalInput").ap()
    bo = nc.dram_tensor("bo", [1, C], F32, kind="ExternalInput").ap()
    xout = nc.dram_tensor("xout", [TPC, C], F32, kind="ExternalOutput").ap()
    yout = nc.dram_tensor("yout", [128, 8, TPC], BF16,
                          kind="ExternalOutput").ap()
    if in_ln:
        ydram = nc.dram_tensor("att_ydram", [128, 8, TPC], BF16, kind="Internal")
    else:
        ydram = nc.dram_tensor("yin", [128, 8, TPC], BF16, kind="ExternalInput")
    E = Emitter(nc)
    E.con = make_consts(nc)
    E.dbgP_ap = nc.dram_tensor("dbgP", [128, 128], BF16, kind="ExternalOutput").ap()
    E.dbgA_ap = nc.dram_tensor("dbgA", [128, 8], F32, kind="ExternalOutput").ap()
    if layer == 0:
        b48, selc = make_alibi_consts(nc)
        E.con["alibi_b48"] = b48
        E.con["alibi_sel"] = selc
    if in_ln:
        ln_done = emit_ln_pass(
            E, [xin[128 * i:128 * (i + 1), :] for i in range(TPC // 128)],
            ydram.ap(), "lni")
        E.barrier(ln_done)
    at_done = emit_attn_block(E, xin, ydram.ap(), xout, wq_ap, wk_ap, wv_ap,
                              wo, bvec, bo, layer, "att", dbg_nqt=dbg_nqt,
                              dbg_nb=dbg_nb)
    E.barrier(at_done)
    if out_ln:
        ln2 = emit_ln_pass(
            E, [xout[128 * i:128 * (i + 1), :] for i in range(TPC // 128)],
            yout, "lno")
        E.emit("sync", lambda b: None, wait=ln2 + at_done)
    else:
        E.emit("sync", lambda b: None, wait=at_done)
    E.finalize()
    return nc


def _core_inmaps(inputs):
    x = np.ascontiguousarray(np.asarray(inputs["L0_attn_x"], np.float32))
    wqkv, bvec0, wo0, bo0 = prep_attn0_weights(
        inputs["L0_attn_w_attn"], inputs["L0_attn_b_attn"],
        inputs["L0_attn_w_proj"], inputs["L0_attn_b_proj"])
    wg, wu, wd = prep_ffn_weights(
        np.asarray(inputs["L0_ffn_w_gate"], np.int32),
        np.asarray(inputs["L0_ffn_w_up"], np.int32),
        np.asarray(inputs["L0_ffn_w_down"], np.int32))
    wq1, wk1, wv1, wo1, bvec1, bo1 = prep_attn1_weights(
        inputs["L1_attn_W_Q"], inputs["L1_attn_b_Q"],
        inputs["L1_attn_W_K"], inputs["L1_attn_b_K"],
        inputs["L1_attn_W_V"], inputs["L1_attn_b_V"],
        inputs["L1_attn_W_O"], inputs["L1_attn_b_O"])
    w_aps, bcols, bo_r, incc = prep_lstm_weights(
        inputs["L1_ffn_wf"], inputs["L1_ffn_uf"], inputs["L1_ffn_bf"],
        inputs["L1_ffn_wi"], inputs["L1_ffn_ui"], inputs["L1_ffn_bi"],
        inputs["L1_ffn_wc"], inputs["L1_ffn_uc"], inputs["L1_ffn_bc"],
        inputs["L1_ffn_wo"], inputs["L1_ffn_uo"], inputs["L1_ffn_bo"],
        inputs["L1_ffn_in_c"])
    return dict(x=x, wqkv=wqkv, bvec0=bvec0, wo0=wo0, bo0=bo0,
                wg=wg, wu=wu, wd=wd, wq1=wq1, wk1=wk1, wv1=wv1, wo1=wo1,
                bvec1=bvec1, bo1=bo1, lstm_w=w_aps, bcols=bcols, bo_r=bo_r,
                incc=incc)


def build_fused_program():
    nc = bass.Bass("TRN2", target_bir_lowering=False)
    xin = nc.dram_tensor("xin", [TPC, C], F32, kind="ExternalInput").ap()
    wqkv = nc.dram_tensor("wqkv", [128, 8, 3 * C], F32, kind="ExternalInput").ap()
    wo0 = nc.dram_tensor("wo0", [128, 8, C], F32, kind="ExternalInput").ap()
    bvec0 = nc.dram_tensor("bvec0", [128, 24], F32, kind="ExternalInput").ap()
    bo0 = nc.dram_tensor("bo0", [1, C], F32, kind="ExternalInput").ap()
    wg = nc.dram_tensor("wg", [128, 8, FH], I32, kind="ExternalInput").ap()
    wu = nc.dram_tensor("wu", [128, 8, FH], I32, kind="ExternalInput").ap()
    wd = nc.dram_tensor("wd", [128, 24, C], I32, kind="ExternalInput").ap()
    wq1 = nc.dram_tensor("wq1", [128, 8, C], F32, kind="ExternalInput").ap()
    wk1 = nc.dram_tensor("wk1", [128, 8, C], F32, kind="ExternalInput").ap()
    wv1 = nc.dram_tensor("wv1", [128, 8, C], F32, kind="ExternalInput").ap()
    wo1 = nc.dram_tensor("wo1", [128, 8, C], F32, kind="ExternalInput").ap()
    bvec1 = nc.dram_tensor("bvec1", [128, 24], F32, kind="ExternalInput").ap()
    bo1 = nc.dram_tensor("bo1", [1, C], F32, kind="ExternalInput").ap()
    w_aps = {}
    for g in ("f", "i", "c", "o"):
        w = nc.dram_tensor(f"w{g}", [128, 8, C], F32, kind="ExternalInput").ap()
        u = nc.dram_tensor(f"u{g}", [128, 8, C], F32, kind="ExternalInput").ap()
        w_aps[g] = (w, u)
    b_aps = {
        "bcols": nc.dram_tensor("bcols", [128, 24], F32, kind="ExternalInput").ap(),
        "bo": nc.dram_tensor("bo", [1, C], F32, kind="ExternalInput").ap(),
        "inc": nc.dram_tensor("incc", [128, 8], F32, kind="ExternalInput").ap(),
    }
    out = nc.dram_tensor("out", [NSTEP * SEQ2, C], F32, kind="ExternalOutput").ap()
    x1 = nc.dram_tensor("x1s", [TPC, C], F32, kind="Internal").ap()
    x2 = nc.dram_tensor("x2s", [TPC, C], F32, kind="Internal").ap()
    x3 = nc.dram_tensor("x3s", [TPC, C], F32, kind="Internal")
    x4 = nc.dram_tensor("x4s", [TPC, C], F32, kind="Internal")
    ya = nc.dram_tensor("yas", [128, 8, TPC], BF16, kind="Internal").ap()
    yb = nc.dram_tensor("ybs", [128, 8, TPC], BF16, kind="Internal").ap()

    E = Emitter(nc)
    E.con = make_consts(nc)
    b48, selc = make_alibi_consts(nc)
    E.con["alibi_b48"] = b48
    E.con["alibi_sel"] = selc

    tiles = lambda ap: [ap[128 * i:128 * (i + 1), :] for i in range(TPC // 128)]
    d = emit_ln_pass(E, tiles(xin), ya, "ln0")
    E.barrier(d)
    d = emit_attn_block(E, xin, ya, x1, wqkv[:, :, 0:C], wqkv[:, :, C:2 * C],
                        wqkv[:, :, 2 * C:3 * C], wo0, bvec0, bo0, 0, "a0")
    E.barrier(d)
    d = emit_ln_pass(E, tiles(x1), yb, "ln1")
    E.barrier(d)
    d = emit_ffn_block(E, x1, yb, x2, wg, wu, wd, {})
    E.barrier(d)
    d = emit_ln_pass(E, tiles(x2), ya, "ln2")
    E.barrier(d)

    def a2a_row(bb, qt):
        t0 = 128 * qt
        return (t0 // SEQ2) * 2 * SEQ2 + bb * SEQ2 + (t0 % SEQ2)
    d = emit_attn_block(E, x2, ya, x3.ap(), wq1, wk1, wv1, wo1, bvec1, bo1,
                        1, "a1", out_row_fn=a2a_row)
    E.barrier(d)
    E.emit("gpsimd", lambda b: b.collective_compute(
        "AllToAll", ALU.bypass, replica_groups=[list(range(NCORES))],
        ins=[x3.ap().opt()], outs=[x4.ap().opt()]), inc="CC", amt=1)
    E.barrier([("CC", 1)])
    d = emit_ln_pass(E, tiles(x4.ap()), yb, "ln3")
    E.barrier(d)
    d = emit_lstm_block(E, x4.ap(), yb, out, w_aps, b_aps, {})
    E.emit("sync", lambda b: None, wait=d)
    E.finalize()
    return nc


def kernel_fused(**inputs):
    from concourse.aot_env import allow_ici_all_to_all
    pp = _core_inmaps(inputs)
    x = pp["x"]
    cores = list(range(NCORES))
    with allow_ici_all_to_all():
        nc = build_fused_program()
        im = []
        for r in cores:
            m = {"xin": np.ascontiguousarray(x[2 * r:2 * r + 2].reshape(TPC, C)),
                 "wqkv": pp["wqkv"], "wo0": pp["wo0"], "bvec0": pp["bvec0"],
                 "bo0": pp["bo0"], "wg": pp["wg"], "wu": pp["wu"], "wd": pp["wd"],
                 "wq1": pp["wq1"], "wk1": pp["wk1"], "wv1": pp["wv1"],
                 "wo1": pp["wo1"], "bvec1": pp["bvec1"], "bo1": pp["bo1"],
                 "bcols": pp["bcols"], "bo": pp["bo_r"], "incc": pp["incc"]}
            m.update(pp["lstm_w"])
            im.append(m)
        res = run_bass_kernel_spmd(nc, im, core_ids=cores).results
    out = np.empty((NSTEP, NCORES * SEQ2, C), np.float32)
    for r in cores:
        out[:, SEQ2 * r:SEQ2 * (r + 1), :] = res[r]["out"].reshape(
            NSTEP, SEQ2, C)
    return out


def kernel(**inputs):
    _ = inputs.get("L1_attn_pos_embed")
    return kernel_fused(**inputs)


def kernel_unfused(**inputs):
    pp = _core_inmaps(inputs)
    x = pp["x"]
    cores = list(range(NCORES))
    nc1 = build_attn_program(0, out_ln=True, in_ln=True)
    im1 = [{"xin": np.ascontiguousarray(x[2 * r:2 * r + 2].reshape(TPC, C)),
            "wqkv": pp["wqkv"], "wo": pp["wo0"], "bvec": pp["bvec0"],
            "bo": pp["bo0"]} for r in cores]
    r1 = run_bass_kernel_spmd(nc1, im1, core_ids=cores).results
    nc2 = build_ffn_program(in_ln=False, out_ln=True)
    im2 = [{"xin": r1[r]["xout"], "yin": r1[r]["yout"], "wg": pp["wg"],
            "wu": pp["wu"], "wd": pp["wd"]} for r in cores]
    r2 = run_bass_kernel_spmd(nc2, im2, core_ids=cores).results
    nc3 = build_attn_program(1, out_ln=True, in_ln=False)
    im3 = [{"xin": r2[r]["xout"], "yin": r2[r]["yout"], "wq": pp["wq1"],
            "wk": pp["wk1"], "wv": pp["wv1"], "wo": pp["wo1"],
            "bvec": pp["bvec1"], "bo": pp["bo1"]} for r in cores]
    r3 = run_bass_kernel_spmd(nc3, im3, core_ids=cores).results
    x3 = np.stack([r3[r]["xout"] for r in cores])
    y3 = np.stack([r3[r]["yout"] for r in cores])
    nc4 = build_lstm_program()
    im4 = []
    for r in cores:
        xs = np.empty((NSTEP * SEQ2, C), np.float32)
        ys = np.empty((128, 8, NSTEP * SEQ2), y3.dtype)
        for t in range(NSTEP):
            src_r = t // 2
            off = (t % 2) * TB + SEQ2 * r
            xs[SEQ2 * t:SEQ2 * (t + 1)] = x3[src_r][off:off + SEQ2]
            ys[:, :, SEQ2 * t:SEQ2 * (t + 1)] = y3[src_r][:, :, off:off + SEQ2]
        im4.append(dict(xin=xs, yin=np.ascontiguousarray(ys),
                        bcols=pp["bcols"], bo=pp["bo_r"], incc=pp["incc"],
                        **pp["lstm_w"]))
    r4 = run_bass_kernel_spmd(nc4, im4, core_ids=cores).results
    out = np.empty((NSTEP, NCORES * SEQ2, C), np.float32)
    for r in cores:
        out[:, SEQ2 * r:SEQ2 * (r + 1), :] = r4[r]["out"].reshape(
            NSTEP, SEQ2, C)
    return out


# version 10
# speedup vs baseline: 1.2350x; 1.0107x over previous
import contextlib
import numpy as np
import ml_dtypes
import concourse.bass as bass
import concourse.mybir as mybir
from concourse.bass_utils import run_bass_kernel_spmd

F32 = mybir.dt.float32
BF16 = mybir.dt.bfloat16
I32 = mybir.dt.int32
AF = mybir.ActivationFunctionType
ALU = mybir.AluOpType
AX = mybir.AxisListType
BF = ml_dtypes.bfloat16

NCORES = 8
C = 1024
NH = 16
DH = 64
FH = 3072
TB = 2048
BPC = 2
TPC = TB * BPC
SEQ2 = 256
NSTEP = 16
EPS = 1e-5
NEGBIG = -1e9
RCP127 = float(np.float32(1.0 / 127.0))

ENGS = ("sync", "gpsimd", "tensor", "vector", "scalar")
DBG = set()


class Emitter:

    SNAP = {"V": "vector", "S": "scalar", "T": "tensor"}

    def __init__(self, nc):
        self.nc = nc
        self.ops = {e: [] for e in ENGS}
        self.sems = {}
        self.cnt = {}
        self.producer = {}
        self.dma_sem = set()
        self.hw = {e: {} for e in ENGS}
        self.know = {e: {} for e in ENGS}
        self.ck = {s: {} for s in self.SNAP}

    def sem(self, name):
        if name not in self.sems:
            self.sems[name] = self.nc.alloc_semaphore(name)
            self.cnt[name] = 0
        return self.sems[name]

    def emit(self, eng, fn, wait=(), inc=None, amt=1):
        for s, _v in wait:
            self.sem(s)
        if inc is not None:
            self.sem(inc)
            self.cnt[inc] += amt
            if amt >= 16:
                self.dma_sem.add(inc)
            p = self.producer.get(inc)
            if p is None:
                self.producer[inc] = eng
            elif p != eng:
                self.producer[inc] = "*"
        if inc is not None and inc in self.SNAP:
            assert eng == self.SNAP[inc], (inc, eng)
        sems = self.sems
        hw = self.hw[eng]
        know = self.know[eng]
        kept = []
        for s, v in wait:
            if v <= 0:
                continue
            if v <= hw.get(s, 0):
                continue
            if (s not in self.dma_sem and self.producer.get(s) == eng):
                continue
            if know.get(s, 0) >= v:
                continue
            hw[s] = v
            kept.append((s, v))
        for s, v in wait:
            if v <= 0:
                continue
            if know.get(s, 0) < v:
                know[s] = v
            snap = self.ck.get(s)
            if snap is not None:
                ex = snap.get(v)
                if ex:
                    for s2, v2 in ex.items():
                        if know.get(s2, 0) < v2:
                            know[s2] = v2
        if inc is not None and inc in self.SNAP:
            newv = self.cnt[inc]
            kk = dict(know)
            kk[inc] = newv
            self.ck[inc][newv] = kk
            know[inc] = newv

        def g(b, fn=fn, kept=tuple(kept), inc=inc, amt=amt):
            for s, v in kept:
                b.wait_ge(sems[s], v)
            ins = fn(b)
            if inc is not None:
                assert ins is not None, "inc requires an instruction"
                ins.then_inc(sems[inc], amt)
        self.ops[eng].append(g)

    def val(self, sem):
        self.sem(sem)
        return self.cnt[sem]

    def barrier(self, waits, engines=ENGS):
        for e in engines:
            self.emit(e, lambda b: None, wait=waits)

    def finalize(self):
        nc = self.nc
        with nc.Block() as block:
            @block.sync
            def _(b):
                for g in self.ops["sync"]:
                    g(b)

            @block.gpsimd
            def _(b):
                for g in self.ops["gpsimd"]:
                    g(b)

            @block.tensor
            def _(b):
                for g in self.ops["tensor"]:
                    g(b)

            @block.vector
            def _(b):
                for g in self.ops["vector"]:
                    g(b)

            @block.scalar
            def _(b):
                for g in self.ops["scalar"]:
                    g(b)



def make_consts(nc):
    con = {}
    con["ident"] = nc.inline_tensor(np.eye(128, dtype=BF), name="ident_c")
    con["diagmask"] = nc.inline_tensor(
        (np.triu(np.ones((128, 128), np.float32), 1) * NEGBIG), name="diagmask_c")
    con["ones2"] = nc.inline_tensor(np.ones((2, 128), dtype=BF), name="ones2_c")
    return con


def make_alibi_consts(nc):
    kt = np.arange(TB, dtype=np.float64) - (TB - 1)
    rows = np.zeros((48, TB), np.float64)
    for h in range(NH):
        slope = 2.0 ** (-0.5 * (h + 1))
        v = slope * kt
        b1 = v.astype(BF).astype(np.float64)
        b2 = (v - b1).astype(BF).astype(np.float64)
        b3 = (v - b1 - b2).astype(BF).astype(np.float64)
        rows[3 * h + 0] = b1
        rows[3 * h + 1] = b2
        rows[3 * h + 2] = b3
    sel = np.zeros((48, NH * 128), np.float32)
    for h in range(NH):
        sel[3 * h:3 * h + 3, 128 * h:128 * (h + 1)] = 1.0
    b48 = nc.inline_tensor(rows.astype(BF), name="alibi_b48_c")
    selc = nc.inline_tensor(sel.astype(BF), name="alibi_sel_c")
    return b48, selc



def emit_act_warmup(E, funcs, scratch):
    E.emit("vector", lambda b: b.memset(scratch[:, :], 1.0), inc="V")
    mw = [("V", E.val("V"))]
    for k in range(3):
        for f in funcs:
            E.emit("scalar", lambda b, f=f: b.activation(
                scratch[:, 0:256], scratch[:, 0:256], f), wait=mw, inc="S")
            mw = [("S", E.val("S"))]


def emit_ln_pass(E, src_aps, ydram_ap, tag, capture=None):
    nc = E.nc
    nt = len(src_aps)
    st = contextlib.ExitStack()
    xb_all = st.enter_context(nc.sbuf_tensor(f"{tag}_xb", [128, nt, C], BF16))
    stt = st.enter_context(nc.sbuf_tensor(f"{tag}_st", [128, nt, 8], F32))
    scr = st.enter_context(nc.sbuf_tensor(f"{tag}_scr", [128, C], BF16))
    scr2 = st.enter_context(nc.sbuf_tensor(f"{tag}_scr2", [128, C], BF16))
    ytile = st.enter_context(nc.sbuf_tensor(f"{tag}_yt", [128, 2, C], BF16))
    ystage = st.enter_context(nc.sbuf_tensor(f"{tag}_ys", [128, 2, C], BF16))
    warm = st.enter_context(nc.sbuf_tensor(f"{tag}_warm", [128, 256], F32))
    ident_sb = st.enter_context(nc.sbuf_tensor(f"{tag}_id", [128, 128], BF16))
    pst2 = st.enter_context(nc.psum_tensor(f"{tag}_psT", [128, 2, 1024], BF16))

    eps_col = st.enter_context(nc.sbuf_tensor(f"{tag}_eps", [128, 1], F32))
    E.emit("vector", lambda b: b.memset(eps_col[:, :], EPS))
    E.emit("sync", lambda b: b.dma_start(out=ident_sb[:, :],
                                         in_=E.con["ident"].ap()),
           inc="Dc", amt=16)
    idv = E.val("Dc")
    emit_act_warmup(E, [AF.Ln, AF.Square, AF.Copy], warm)
    nt4 = (nt + 3) // 4
    gsem = {}
    for g in range(4):
        lo, hi = g * nt4, min((g + 1) * nt4, nt)
        if lo >= hi:
            continue
        name = f"Gx{g}"
        for i in range(lo, hi):
            E.emit("gpsimd", lambda b, i=i: b.dma_start(
                out=xb_all[:, i, :], in_=src_aps[i]), inc=name, amt=16)
        gv = E.val(name)
        for i in range(lo, hi):
            E.emit("vector", lambda b, i=i: b.tensor_scalar(
                scr[:, :], xb_all[:, i, :], 1.0, 0.0, ALU.mult, ALU.add,
                accum_out=stt[:, i, 0:1]), wait=[(name, gv)], inc="V")
            E.emit("scalar", lambda b, i=i: b.activation(
                scr2[:, :], xb_all[:, i, :], AF.Square,
                accum_out=stt[:, i, 1:2]), wait=[(name, gv)], inc="S")
    v_stats = E.val("V")
    s_stats = E.val("S")
    spacer = lambda b: b.tensor_copy(scr[:, 0:512], scr2[:, 0:512])
    E.emit("vector", lambda b: b.tensor_scalar(
        stt[:, :, 2:3], stt[:, :, 0:1], 1.0 / C, None, ALU.mult),
        wait=[("V", v_stats), ("S", s_stats)])
    E.emit("vector", spacer)
    E.emit("vector", lambda b: b.tensor_tensor(
        stt[:, :, 3:4], stt[:, :, 2:3], stt[:, :, 2:3], ALU.mult))
    E.emit("vector", spacer)
    E.emit("vector", lambda b: b.scalar_tensor_tensor(
        stt[:, :, 4:5], stt[:, :, 1:2], 1.0 / C, stt[:, :, 3:4],
        ALU.mult, ALU.subtract))
    E.emit("vector", spacer)
    E.emit("vector", spacer, inc="V")
    v_var = E.val("V")
    E.emit("scalar", lambda b: b.activation(
        stt[:, :, 5:6], stt[:, :, 4:5], AF.Ln, bias=eps_col[:, 0:1]),
        wait=[("V", v_var)])
    emit_act_warmup(E, [AF.Exp], warm)
    E.emit("scalar", lambda b: b.activation(
        stt[:, :, 6:7], stt[:, :, 5:6], AF.Exp, scale=-0.5))
    E.emit("scalar", lambda b: b.activation(warm[:, 0:256], warm[:, 0:256],
                                            AF.Copy), inc="S")
    s_rstd = E.val("S")
    E.emit("vector", lambda b: b.scalar_tensor_tensor(
        stt[:, :, 7:8], stt[:, :, 2:3], -1.0, stt[:, :, 6:7],
        ALU.mult, ALU.mult), wait=[("S", s_rstd)], inc="V")
    E.emit("vector", spacer)
    E.emit("vector", spacer)
    v_nmr = E.val("V")
    if capture is not None:
        E.emit("vector", lambda b: b.tensor_copy(capture[:, 0:8 * nt],
                                                 stt[:, :, :]), inc="V")
    ev = [0, 0]
    yw = [0, 0]
    sv = [0, 0]
    for i in range(nt):
        p = i % 2
        w = [("V", v_nmr)]
        if yw[p]:
            w.append(("T", yw[p]))
        E.emit("vector", lambda b, i=i, p=p: b.tensor_scalar(
            ytile[:, p, :], xb_all[:, i, :], stt[:, i, 6:7], stt[:, i, 7:8],
            ALU.mult, ALU.add), wait=w, inc="V")
        v_y = E.val("V")
        w = [("V", v_y), ("Dc", idv)]
        if ev[p]:
            w.append(("V", ev[p]))

        def tr(b, p=p):
            ins = None
            for c in range(8):
                ins = b.transpose(pst2[:, p, 128 * c:128 * (c + 1)],
                                  ytile[:, p, 128 * c:128 * (c + 1)],
                                  ident_sb[:, :])
            return ins
        E.emit("tensor", tr, wait=w, inc="T")
        yw[p] = E.val("T")
        ysem = f"{tag}y{p}"
        w = [("T", yw[p])]
        if sv[p]:
            w.append((ysem, sv[p]))
        E.emit("vector", lambda b, p=p: b.tensor_copy(
            ystage[:, p, :], pst2[:, p, :]), wait=w, inc="V")
        ev[p] = E.val("V")
        E.emit("sync", lambda b, p=p, i=i: b.dma_start(
            out=ydram_ap[:, :, 128 * i:128 * (i + 1)],
            in_=ystage[:, p, :].rearrange("x (a c) -> x a c", a=8)),
            wait=[("V", ev[p])], inc=ysem, amt=16)
        sv[p] = E.val(ysem)
    st.close()
    return [(f"{tag}y0", E.val(f"{tag}y0")), (f"{tag}y1", E.val(f"{tag}y1"))]



def emit_ffn_block(E, xin_ap, yin_ap, xout_ap, wg_ap, wu_ap, wd_ap, ctx,
                   ntok=TPC):
    nc = E.nc
    st = contextlib.ExitStack()
    wg_sb = st.enter_context(nc.sbuf_tensor("ffn_wg", [128, 8, FH], BF16))
    wu_sb = st.enter_context(nc.sbuf_tensor("ffn_wu", [128, 8, FH], BF16))
    wd_sb = st.enter_context(nc.sbuf_tensor("ffn_wd", [128, 24, C], BF16))
    warm_sb = st.enter_context(nc.sbuf_tensor("ffn_warm", [128, 256], F32))
    emit_act_warmup(E, [AF.Tanh], warm_sb)

    with nc.sbuf_tensor("ffn_stage", [128, 2, FH], I32) as stage:
        last_dq = [0, 0]
        srcs = [(wg_ap, wg_sb, 8, FH, RCP127), (wu_ap, wu_sb, 8, FH, RCP127),
                (wd_ap, wd_sb, 24, C, float(np.float32(0.5) * np.float32(RCP127)))]
        k = 0
        for (src, dst, nch, wid, dqs) in srcs:
            for cc in range(nch):
                p = k % 2
                w = [("V", last_dq[p])] if last_dq[p] else []
                gsem = f"Gs{p}"
                E.emit("gpsimd", lambda b, src=src, cc=cc, p=p, wid=wid:
                       b.dma_start(out=stage[:, p, 0:wid], in_=src[:, cc, :]),
                       wait=w, inc=gsem, amt=16)
                gv = E.val(gsem)
                E.emit("vector", lambda b, dst=dst, cc=cc, p=p, wid=wid, dqs=dqs:
                       b.tensor_scalar(dst[:, cc, :], stage[:, p, 0:wid],
                                       dqs, None, ALU.mult),
                       wait=[(gsem, gv)], inc="V")
                last_dq[p] = E.val("V")
                k += 1
    dq_done = E.val("V")
    E.barrier([("V", dq_done)])

    y_blk = st.enter_context(nc.sbuf_tensor("ffn_y", [128, 2, 8, 512], BF16))
    hid = st.enter_context(nc.sbuf_tensor("ffn_hid", [128, 24, 512], BF16))
    t_sb = st.enter_context(nc.sbuf_tensor("ffn_t", [128, 2, 512], BF16))
    g_sb = st.enter_context(nc.sbuf_tensor("ffn_g", [128, 2, 512], BF16))
    xres2 = st.enter_context(nc.sbuf_tensor("ffn_xr", [128, 2, C], F32))
    out2 = st.enter_context(nc.sbuf_tensor("ffn_out", [128, 2, C], F32))
    ps_s = st.enter_context(nc.psum_tensor("ffn_psA", [128, 2048], F32))
    ps_m = st.enter_context(nc.psum_tensor("ffn_psM", [128, 1024], F32))

    last_hmul = [0, 0]
    last_gu = 0
    prev_res = 0
    last_store = [0, 0]
    last_xload = [0, 0]
    last_yload = [0, 0]

    nblk = ntok // 512
    for blk in range(nblk):
        bp = blk % 2
        w = [("T", last_yload[bp])] if last_yload[bp] else []
        ybsem = f"Dyb{bp}"
        E.emit("sync", lambda b, blk=blk, bp=bp: b.dma_start(
            out=y_blk[:, bp, :, :],
            in_=yin_ap[:, :, 512 * blk:512 * (blk + 1)]),
            wait=w, inc=ybsem, amt=16)
        yv = (ybsem, E.val(ybsem))
        for ht in range(24):
            pp = ht % 2
            gslc = (1024 * pp, 1024 * pp + 512)
            uslc = (1024 * pp + 512, 1024 * pp + 1024)
            w = [yv]
            if last_hmul[pp]:
                w.append(("V", last_hmul[pp]))

            def gu(b, ht=ht, bp=bp, gslc=gslc, uslc=uslc):
                ins = None
                for cc in range(8):
                    b.matmul(ps_s[:, gslc[0]:gslc[1]],
                             wg_sb[:, cc, 128 * ht:128 * (ht + 1)],
                             y_blk[:, bp, cc, :], start=(cc == 0),
                             stop=(cc == 7))
                for cc in range(8):
                    ins = b.matmul(ps_s[:, uslc[0]:uslc[1]],
                                   wu_sb[:, cc, 128 * ht:128 * (ht + 1)],
                                   y_blk[:, bp, cc, :], start=(cc == 0),
                                   stop=(cc == 7))
                return ins
            E.emit("tensor", gu, wait=w, inc="T")
            last_gu = E.val("T")
            E.emit("scalar", lambda b, pp=pp, gslc=gslc: b.activation(
                t_sb[:, pp, :], ps_s[:, gslc[0]:gslc[1]], AF.Tanh, scale=0.5),
                wait=[("T", last_gu)], inc="S")
            sv = E.val("S")
            E.emit("vector", lambda b, pp=pp, gslc=gslc:
                   b.scalar_tensor_tensor(g_sb[:, pp, :], t_sb[:, pp, :], 1.0,
                                          ps_s[:, gslc[0]:gslc[1]],
                                          ALU.add, ALU.mult),
                   wait=[("S", sv)])
            E.emit("vector", lambda b, ht=ht, pp=pp, uslc=uslc:
                   b.tensor_tensor(hid[:, ht, :], ps_s[:, uslc[0]:uslc[1]],
                                   g_sb[:, pp, :], ALU.mult), inc="V")
            last_hmul[pp] = E.val("V")
        last_yload[bp] = last_gu
        hv = E.val("V")
        for tt in range(4):
            p = tt % 2
            w = [("V", hv)]
            if prev_res:
                w.append(("V", prev_res))

            def down(b, tt=tt):
                ins = None
                for mc in range(2):
                    for hc in range(24):
                        ins = b.matmul(ps_m[:, 512 * mc:512 * (mc + 1)],
                                       hid[:, hc, 128 * tt:128 * (tt + 1)],
                                       wd_sb[:, hc, 512 * mc:512 * (mc + 1)],
                                       start=(hc == 0), stop=(hc == 23))
                return ins
            E.emit("tensor", down, wait=w, inc="T")
            tv = E.val("T")
            row0 = 512 * blk + 128 * tt
            xsem, osem = f"Dx{p}", f"Do{p}"
            wl = [("V", last_xload[p])] if last_xload[p] else []
            E.emit("sync", lambda b, row0=row0, p=p: b.dma_start(
                out=xres2[:, p, :], in_=xin_ap[row0:row0 + 128, :]),
                wait=wl, inc=xsem, amt=16)
            dv = E.val(xsem)
            w = [("T", tv), (xsem, dv)]
            if last_store[p]:
                w.append((osem, last_store[p]))
            E.emit("vector", lambda b, p=p: b.tensor_tensor(
                out2[:, p, :], ps_m[:, :], xres2[:, p, :], ALU.add),
                wait=w, inc="V")
            prev_res = E.val("V")
            last_xload[p] = prev_res
            E.emit("sync", lambda b, row0=row0, p=p: b.dma_start(
                out=xout_ap[row0:row0 + 128, :], in_=out2[:, p, :]),
                wait=[("V", prev_res)], inc=osem, amt=16)
            last_store[p] = E.val(osem)
    st.close()
    return [("Do0", E.val("Do0")), ("Do1", E.val("Do1"))]



def emit_attn_block(E, xin_ap, yin_ap, xout_ap, wq_ap, wk_ap, wv_ap, wo_ap,
                    bvec_ap, bo_ap, layer, tag, dbg_nqt=16, dbg_nb=BPC,
                    out_row_fn=None):
    nc = E.nc
    st = contextlib.ExitStack()
    warm = st.enter_context(nc.sbuf_tensor(f"{tag}_warm", [128, 256], F32))
    Q2 = st.enter_context(nc.sbuf_tensor(f"{tag}_q2", [128, 8, TB], BF16))
    K2 = st.enter_context(nc.sbuf_tensor(f"{tag}_k2", [128, 8, TB], BF16))
    Vt = st.enter_context(nc.sbuf_tensor(f"{tag}_vt", [128, 16, C], BF16))
    bvec = st.enter_context(nc.sbuf_tensor(f"{tag}_bv", [128, 24], F32))
    bvbf = st.enter_context(nc.sbuf_tensor(f"{tag}_bvb", [128, 8], BF16))
    bo_sb = st.enter_context(nc.sbuf_tensor(f"{tag}_bo", [1, C], F32))
    bias2 = st.enter_context(nc.sbuf_tensor(f"{tag}_b2", [2, C], BF16))
    ones2 = st.enter_context(nc.sbuf_tensor(f"{tag}_on2", [2, 128], BF16))
    dmask = st.enter_context(nc.sbuf_tensor(f"{tag}_dm", [128, 128], F32))
    ident_sb = st.enter_context(nc.sbuf_tensor(f"{tag}_id", [128, 128], BF16))
    if layer == 0:
        sel_sb = st.enter_context(nc.sbuf_tensor(f"{tag}_sel", [48, NH * 128], BF16))
        b48_sb = st.enter_context(nc.sbuf_tensor(f"{tag}_b48", [48, TB], BF16))
    ast = st.enter_context(nc.sbuf_tensor(f"{tag}_ast", [128, 2, 8], F32))
    E.dbgP = st.enter_context(nc.sbuf_tensor(f"{tag}_dbgp", [128, 128], BF16))
    E.dbgA = st.enter_context(nc.sbuf_tensor(f"{tag}_dbga", [128, 8], F32))
    P_sb = st.enter_context(nc.sbuf_tensor(f"{tag}_p", [128, 2, TB], BF16))
    PT_sb = st.enter_context(nc.sbuf_tensor(f"{tag}_pt", [128, TB], BF16))
    zn2 = st.enter_context(nc.sbuf_tensor(f"{tag}_zn", [128, 128], BF16))
    qst = st.enter_context(nc.sbuf_tensor(f"{tag}_qst", [128, 2, 128], BF16))
    zT2 = st.enter_context(nc.sbuf_tensor(f"{tag}_zt", [128, 128], BF16))
    ps_s = st.enter_context(nc.psum_tensor(f"{tag}_pss", [128, 2048], F32))
    ps_t = st.enter_context(nc.psum_tensor(f"{tag}_pst", [128, 1024], BF16))
    ps_z = st.enter_context(nc.psum_tensor(f"{tag}_psz", [128, 128], F32))
    ps_m = st.enter_context(nc.psum_tensor(f"{tag}_psm", [128, 1024], F32))

    emit_act_warmup(E, [AF.Exp], warm)
    for dst, srcap in ((bvec, bvec_ap), (bo_sb, bo_ap),
                       (ones2, E.con["ones2"].ap()),
                       (dmask, E.con["diagmask"].ap()),
                       (ident_sb, E.con["ident"].ap())):
        E.emit("sync", lambda b, dst=dst, srcap=srcap: b.dma_start(
            out=dst[:, :], in_=srcap), inc="Dc", amt=16)
    if layer == 0:
        E.emit("sync", lambda b: b.dma_start(out=sel_sb[:, :],
                                             in_=E.con["alibi_sel"].ap()),
               inc="Dc", amt=16)
        E.emit("sync", lambda b: b.dma_start(out=b48_sb[:, :],
                                             in_=E.con["alibi_b48"].ap()),
               inc="Dc", amt=16)
    dc_v = E.val("Dc")
    E.emit("vector", lambda b: b.tensor_copy(bvbf[:, :], bvec[:, 16:24]),
           wait=[("Dc", dc_v)], inc="V")
    E.emit("vector", lambda b: b.memset(qst[:, :, :], 0.0), inc="V")
    E.barrier([("Dc", dc_v), ("V", E.val("V"))])

    osem = [f"{tag}o0", f"{tag}o1"]
    xsem = [f"{tag}x0", f"{tag}x1"]
    last_store = [0, 0]
    last_xload = [0, 0]
    bias_ready = False
    prev_merge_evac = [0]

    for bb in range(dbg_nb):
        tok0 = bb * TB
        E.barrier([("T", E.val("T"))], engines=("vector", "sync", "gpsimd"))
        with contextlib.ExitStack() as stA:
            wq_sb = stA.enter_context(nc.sbuf_tensor(f"{tag}_wq{bb}", [128, 8, C], BF16))
            wk_sb = stA.enter_context(nc.sbuf_tensor(f"{tag}_wk{bb}", [128, 8, C], BF16))
            wv_sb = stA.enter_context(nc.sbuf_tensor(f"{tag}_wv{bb}", [128, 8, C], BF16))
            y_win = stA.enter_context(nc.sbuf_tensor(f"{tag}_yw{bb}", [128, 2, 8, 512], BF16))
            wsems = {}
            for nm, dst, srcap in (("q", wq_sb, wq_ap), ("k", wk_sb, wk_ap),
                                   ("v", wv_sb, wv_ap)):
                sname = f"{tag}w{nm}{bb}"
                E.emit("gpsimd", lambda b, dst=dst, srcap=srcap: b.dma_start(
                    out=dst[:, :, :], in_=srcap), inc=sname, amt=16)
                wsems[nm] = (sname, E.val(sname))
            bank_evac = [0, 0, 0, 0]
            ywin_T = [0, 0]
            kbank = [0]
            for tw in range(4):
                p = tw % 2
                w = [("T", ywin_T[p])] if ywin_T[p] else []
                ysm = f"{tag}ya{p}"
                E.emit("sync", lambda b, tw=tw, p=p, tok0=tok0: b.dma_start(
                    out=y_win[:, p, :, :],
                    in_=yin_ap[:, :, tok0 + 512 * tw: tok0 + 512 * (tw + 1)]),
                    wait=w, inc=ysm, amt=16)
                yv = E.val(ysm)

                def proj(dst, w_sb, wwait, bcol, tw=tw, p=p, yv=yv, ysm=ysm):
                    for hp in range(8):
                        bk = kbank[0] % 4
                        kbank[0] += 1
                        w = [(ysm, yv), wwait]
                        if bank_evac[bk]:
                            w.append(("V", bank_evac[bk]))

                        def mm(b, hp=hp, bk=bk, w_sb=w_sb, p=p):
                            ins = None
                            for cc in range(8):
                                ins = b.matmul(
                                    ps_s[:, 512 * bk:512 * (bk + 1)],
                                    w_sb[:, cc, 128 * hp:128 * (hp + 1)],
                                    y_win[:, p, cc, :],
                                    start=(cc == 0), stop=(cc == 7))
                            return ins
                        E.emit("tensor", mm, wait=w, inc="T")
                        tv = E.val("T")
                        E.emit("vector", lambda b, hp=hp, bk=bk, dst=dst,
                               bcol=bcol, tw=tw: b.tensor_scalar(
                                   dst[:, hp, 512 * tw:512 * (tw + 1)],
                                   ps_s[:, 512 * bk:512 * (bk + 1)],
                                   bvec[:, bcol + hp:bcol + hp + 1], None,
                                   ALU.add), wait=[("T", tv)], inc="V")
                        bank_evac[bk] = E.val("V")
                proj(Q2, wq_sb, wsems["q"], 0)
                proj(K2, wk_sb, wsems["k"], 8)
                for tt in range(4):
                    for vc in range(2):
                        bk = kbank[0] % 4
                        kbank[0] += 1
                        w = [(ysm, yv), wsems["v"]]
                        if bank_evac[bk]:
                            w.append(("V", bank_evac[bk]))

                        def mmv(b, tt=tt, vc=vc, bk=bk, p=p):
                            ins = None
                            for cc in range(8):
                                ins = b.matmul(
                                    ps_s[:, 512 * bk:512 * (bk + 1)],
                                    y_win[:, p, cc, 128 * tt:128 * (tt + 1)],
                                    wv_sb[:, cc, 512 * vc:512 * (vc + 1)],
                                    start=(cc == 0), stop=(cc == 7))
                            return ins
                        E.emit("tensor", mmv, wait=w, inc="T")
                        tv = E.val("T")
                        E.emit("vector", lambda b, tt=tt, vc=vc, bk=bk, tw=tw:
                               b.tensor_copy(
                                   Vt[:, 4 * tw + tt, 512 * vc:512 * (vc + 1)],
                                   ps_s[:, 512 * bk:512 * (bk + 1)]),
                               wait=[("T", tv)], inc="V")
                        bank_evac[bk] = E.val("V")
                ywin_T[p] = E.val("T")
        E.barrier([("V", E.val("V"))])
        with contextlib.ExitStack() as stB:
            wo_sb = stB.enter_context(nc.sbuf_tensor(f"{tag}_wo{bb}", [128, 8, C], BF16))
            comb = stB.enter_context(nc.sbuf_tensor(f"{tag}_cmb{bb}", [1, C], F32))
            hi1 = stB.enter_context(nc.sbuf_tensor(f"{tag}_hi{bb}", [1, C], BF16))
            lo1 = stB.enter_context(nc.sbuf_tensor(f"{tag}_lo{bb}", [1, C], BF16))
            out2 = stB.enter_context(nc.sbuf_tensor(f"{tag}_o2{bb}", [128, 2, C], F32))
            xres2 = stB.enter_context(nc.sbuf_tensor(f"{tag}_xr{bb}", [128, 2, C], F32))
            wosem = f"{tag}wo{bb}"
            E.emit("gpsimd", lambda b: b.dma_start(out=wo_sb[:, :, :],
                                                   in_=wo_ap), inc=wosem, amt=16)
            wo_v = E.val(wosem)
            if not bias_ready:
                bias_ready = True

                def bmm(b):
                    ins = None
                    for j in range(8):
                        for mw in range(2):
                            ins = b.matmul(ps_m[0:1, 512 * mw:512 * (mw + 1)],
                                           bvbf[:, j:j + 1],
                                           wo_sb[:, j, 512 * mw:512 * (mw + 1)],
                                           start=(j == 0), stop=(j == 7))
                    return ins
                E.emit("tensor", bmm, wait=[(wosem, wo_v)], inc="T")
                tv = E.val("T")
                E.emit("vector", lambda b: b.tensor_tensor(
                    comb[:, :], ps_m[0:1, :], bo_sb[:, :], ALU.add),
                    wait=[("T", tv)], inc="V")
                E.emit("vector", lambda b: b.tensor_copy(hi1[:, :],
                                                         comb[:, :]))
                E.emit("vector", lambda b: b.tensor_tensor(
                    lo1[:, :], comb[:, :], hi1[:, :], ALU.subtract),
                    inc="V")
                prev_merge_evac[0] = E.val("V")
                E.emit("gpsimd", lambda b: b.dma_start(
                    out=bias2[0:1, :], in_=hi1[:, :]),
                    wait=[("V", E.val("V"))], inc=f"{tag}b2", amt=16)
                E.emit("gpsimd", lambda b: b.dma_start(
                    out=bias2[1:2, :], in_=lo1[:, :]),
                    inc=f"{tag}b2", amt=16)
                E.barrier([("V", E.val("V")), (f"{tag}b2", E.val(f"{tag}b2"))],
                          engines=("tensor",))

            pend = [None]
            qst_war = [0, 0]
            reg_free = {0: (0, 0), 1: (0, 0)}
            psz_free = [0]
            pt_war = [0]
            psb_war = [0, 0]
            it = [0]

            def flush():
                if pend[0] is None or "noflush" in DBG:
                    pend[0] = None
                    return
                (qt, hp, j, par, W, s_exp) = pend[0]
                pend[0] = None
                h = 2 * hp + j
                nkt = W // 128
                for rnd in range((nkt + 7) // 8):
                    k0, k1 = 8 * rnd, min(8 * rnd + 8, nkt)

                    def tr(b, k0=k0, k1=k1, par=par):
                        ins = None
                        for k in range(k0, k1):
                            ins = b.transpose(
                                ps_t[:, 128 * (k - k0):128 * (k - k0 + 1)],
                                P_sb[:, par, 128 * k:128 * (k + 1)],
                                ident_sb[:, :])
                        return ins
                    w = [("S", s_exp)]
                    if pt_war[0]:
                        w.append(pt_war[0])
                    E.emit("tensor", tr, wait=w, inc="T")
                    tv = E.val("T")
                    E.emit("scalar", lambda b, k0=k0, k1=k1: b.activation(
                        PT_sb[:, 128 * k0:128 * k1],
                        ps_t[:, 0:128 * (k1 - k0)], AF.Copy),
                        wait=[("T", tv)], inc="Sp")
                    pt_war[0] = ("Sp", E.val("Sp"))
                psb_war[par] = E.val("T")
                w = [pt_war[0]]
                if psz_free[0]:
                    w.append(("V", psz_free[0]))

                def pv(b, nkt=nkt, h=h):
                    ins = None
                    for k in range(nkt):
                        ins = b.matmul(ps_z[:, 0:DH],
                                       PT_sb[:, 128 * k:128 * (k + 1)],
                                       Vt[:, k, DH * h:DH * h + DH],
                                       start=(k == 0), stop=(k == nkt - 1))
                    return ins
                E.emit("tensor", pv, wait=w, inc="T")
                tv = E.val("T")
                E.emit("vector", lambda b, j=j, par=par: b.tensor_scalar(
                    zn2[:, DH * j:DH * (j + 1)], ps_z[:, 0:DH],
                    ast[:, par, 1:2], None, ALU.mult),
                    wait=[("T", tv)], inc="V")
                psz_free[0] = E.val("V")
                if j == 1:
                    w = [("V", psz_free[0])]
                    if pt_war[0]:
                        w.append(pt_war[0])
                    E.emit("tensor", lambda b: b.transpose(
                        ps_t[:, 0:128], zn2[:, :], ident_sb[:, :]),
                        wait=w, inc="T")
                    tv = E.val("T")
                    E.emit("vector", lambda b: b.tensor_copy(
                        zT2[:, :], ps_t[:, 0:128]), wait=[("T", tv)], inc="V")
                    pt_war[0] = ("V", E.val("V"))
                    zv = E.val("V")
                    w = [("V", zv), (wosem, wo_v)]
                    if hp == 0 and prev_merge_evac[0]:
                        w.append(("V", prev_merge_evac[0]))

                    def mrg(b, hp=hp):
                        ins = None
                        for mw in range(2):
                            ins = b.matmul(ps_m[:, 512 * mw:512 * (mw + 1)],
                                           zT2[:, :],
                                           wo_sb[:, hp, 512 * mw:512 * (mw + 1)],
                                           start=(hp == 0), stop=False)
                        return ins
                    E.emit("tensor", mrg, wait=w, inc="T")

            for qt in range(dbg_nqt):
                W = 128 * (qt + 1)
                two_chunk = W > 1024
                for hp in range(8):
                    for j in range(2):
                        h = 2 * hp + j
                        it[0] += 1
                        par = it[0] % 2
                        if two_chunk:
                            regs = [0, 1]
                            roff = 0
                        else:
                            regs = [it[0] % 2]
                            roff = 1024 * (it[0] % 2)
                        w = []
                        for r in regs:
                            if reg_free[r][0]:
                                w.append(("S", reg_free[r][0]))
                            if reg_free[r][1]:
                                w.append(("V", reg_free[r][1]))

                        alibi = (layer == 0) and ("noalibi" not in DBG)
                        wq = []
                        if qst_war[j]:
                            wq.append(("T", qst_war[j]))
                        E.emit("scalar", lambda b, qt=qt, hp=hp, j=j:
                               b.activation(
                                   qst[64 * j:64 * (j + 1), j, :],
                                   Q2[64 * j:64 * (j + 1), hp,
                                      128 * qt:128 * (qt + 1)],
                                   AF.Copy, scale=0.125),
                               wait=wq, inc="S")
                        w.append(("S", E.val("S")))

                        def smm(b, hp=hp, j=j, W=W, roff=roff, h=h,
                                alibi=alibi):
                            ins = None
                            for wnd in range(0, W, 512):
                                wid = min(512, W - wnd)
                                ins = b.matmul(
                                    ps_s[:, roff + wnd:roff + wnd + wid],
                                    qst[:, j, :],
                                    K2[:, hp, wnd:wnd + wid],
                                    start=True, stop=not alibi)
                                if alibi:
                                    ins = b.matmul(
                                        ps_s[:, roff + wnd:roff + wnd + wid],
                                        sel_sb[:, 128 * h:128 * (h + 1)],
                                        b48_sb[:, wnd:wnd + wid],
                                        start=False, stop=True)
                            return ins
                        if "nosmm" in DBG:
                            E.emit("tensor", lambda b, roff=roff, W=W: b.matmul(
                                ps_s[:, roff:roff + W], ident_sb[:, 0:128],
                                K2[:, 0, 0:W], start=True, stop=True),
                                wait=w, inc="T")
                        else:
                            E.emit("tensor", smm, wait=w, inc="T")
                        t_s = E.val("T")
                        qst_war[j] = t_s
                        E.emit("vector", lambda b, roff=roff, W=W:
                               b.tensor_tensor(
                                   ps_s[:, roff + W - 128:roff + W],
                                   ps_s[:, roff + W - 128:roff + W],
                                   dmask[:, :], ALU.add),
                               wait=[("T", t_s)], inc="V")
                        E.emit("vector", lambda b, roff=roff, W=W, par=par:
                               b.tensor_reduce(
                                   ast[:, par, 3:4],
                                   ps_s[:, roff:roff + W],
                                   AX.X, ALU.max, negate=True), inc="V")
                        bcol = 3
                        flush()
                        v_bias = E.val("V")
                        w = [("V", v_bias)]
                        if psb_war[par]:
                            w.append(("T", psb_war[par]))
                        if "noexp" in DBG:
                            E.emit("scalar", lambda b, par=par, roff=roff, W=W:
                                   b.activation(
                                       P_sb[:, par, 0:W], ps_s[:, roff:roff + W],
                                       AF.Copy), wait=w, inc="S")
                        else:
                            E.emit("scalar", lambda b, par=par, roff=roff, W=W,
                                   bcol=bcol: b.activation(
                                       P_sb[:, par, 0:W], ps_s[:, roff:roff + W],
                                       AF.Exp, bias=ast[:, par, bcol:bcol + 1],
                                       scale=1.0,
                                       accum_out=ast[:, par, 0:1]),
                                   wait=w, inc="S")
                        s_exp = E.val("S")
                        reg_done_v = E.val("V")
                        for r in regs:
                            reg_free[r] = (s_exp, reg_done_v)
                        E.emit("vector", lambda b, par=par: b.reciprocal(
                            ast[:, par, 1:2], ast[:, par, 0:1]),
                            wait=[("S", s_exp)], inc="V")
                        if "dump" in DBG and it[0] == 1 and bb == 0:
                            E.emit("vector", lambda b, par=par: b.tensor_copy(
                                E.dbgP[:, :], P_sb[:, par, 0:128]),
                                wait=[("S", s_exp)], inc="V")
                            E.emit("vector", lambda b, par=par: b.tensor_copy(
                                E.dbgA[:, :], ast[:, par, :]), inc="V")
                            dv = E.val("V")
                            E.emit("sync", lambda b: b.dma_start(
                                out=E.dbgP_ap, in_=E.dbgP[:, :]),
                                wait=[("V", dv)], inc="Dc", amt=16)
                            E.emit("sync", lambda b: b.dma_start(
                                out=E.dbgA_ap, in_=E.dbgA[:, :]),
                                inc="Dc", amt=16)
                        pend[0] = (qt, hp, j, par, W, s_exp)
                flush()
                if "notail" in DBG:
                    continue

                def bmm2(b):
                    ins = None
                    for mw in range(2):
                        ins = b.matmul(ps_m[:, 512 * mw:512 * (mw + 1)],
                                       ones2[:, :],
                                       bias2[0:2, 512 * mw:512 * (mw + 1)],
                                       start=False, stop=True)
                    return ins
                E.emit("tensor", bmm2, inc="T")
                t_m = E.val("T")
                p = qt % 2
                row0 = tok0 + 128 * qt
                orow0 = row0 if out_row_fn is None else out_row_fn(bb, qt)
                w = [("V", last_xload[p])] if last_xload[p] else []
                E.emit("sync", lambda b, row0=row0, p=p: b.dma_start(
                    out=xres2[:, p, :], in_=xin_ap[row0:row0 + 128, :]),
                    wait=w, inc=xsem[p], amt=16)
                dv = E.val(xsem[p])
                w = [("T", t_m), (xsem[p], dv)]
                if last_store[p]:
                    w.append((osem[p], last_store[p]))
                E.emit("vector", lambda b, p=p: b.tensor_tensor(
                    out2[:, p, :], ps_m[:, :], xres2[:, p, :], ALU.add),
                    wait=w, inc="V")
                prev_merge_evac[0] = E.val("V")
                last_xload[p] = prev_merge_evac[0]
                E.emit("sync", lambda b, orow0=orow0, p=p: b.dma_start(
                    out=xout_ap[orow0:orow0 + 128, :], in_=out2[:, p, :]),
                    wait=[("V", prev_merge_evac[0])], inc=osem[p], amt=16)
                last_store[p] = E.val(osem[p])
    st.close()
    return [(osem[0], E.val(osem[0])), (osem[1], E.val(osem[1]))]



def emit_lstm_block(E, xin_ap, yin_ap, out_ap, w_aps, b_aps, ctx):
    nc = E.nc
    S2 = SEQ2
    st = contextlib.ExitStack()
    wsb = {}
    for g in ("f", "i", "c", "o"):
        wsb[g] = (st.enter_context(nc.sbuf_tensor(f"ls_w{g}", [128, 8, C], BF16)),
                  st.enter_context(nc.sbuf_tensor(f"ls_u{g}", [128, 8, C], BF16)))
    bcols = st.enter_context(nc.sbuf_tensor("ls_bc", [128, 24], F32))
    inccol = st.enter_context(nc.sbuf_tensor("ls_inc", [128, 8], F32))
    bo2 = st.enter_context(nc.sbuf_tensor("ls_bo2", [2, C], BF16))
    ones2 = st.enter_context(nc.sbuf_tensor("ls_on2", [2, 128], BF16))
    warm = st.enter_context(nc.sbuf_tensor("ls_warm", [128, 256], F32))
    c_fm = st.enter_context(nc.sbuf_tensor("ls_c", [128, 8, S2], F32))
    c_bf = st.enter_context(nc.sbuf_tensor("ls_cb", [128, 8, S2], BF16))
    f_sb = st.enter_context(nc.sbuf_tensor("ls_f", [128, 8, S2], F32))
    i_sb = st.enter_context(nc.sbuf_tensor("ls_i", [128, 8, S2], F32))
    g_sb = st.enter_context(nc.sbuf_tensor("ls_g", [128, 8, S2], F32))
    y2 = st.enter_context(nc.sbuf_tensor("ls_y", [128, 2, 8, S2], BF16))
    cfuo = st.enter_context(nc.sbuf_tensor("ls_cfuo", [128, 2, C], F32))
    xr2 = st.enter_context(nc.sbuf_tensor("ls_xr", [128, 2, C], F32))
    o2 = st.enter_context(nc.sbuf_tensor("ls_o2", [128, 2, C], F32))
    ps_g = st.enter_context(nc.psum_tensor("ls_psg", [128, 2, 512], F32))
    ps_h = st.enter_context(nc.psum_tensor("ls_psh", [128, 2, 512], F32))

    emit_act_warmup(E, [AF.Sigmoid, AF.Tanh], warm)
    for g in ("f", "i", "c", "o"):
        for k in range(2):
            E.emit("gpsimd", lambda b, g=g, k=k: b.dma_start(
                out=wsb[g][k][:, :, :], in_=w_aps[g][k]), inc=f"lw{g}{k}", amt=16)
    E.emit("sync", lambda b: b.dma_start(out=bcols[:, :], in_=b_aps["bcols"]),
           inc="Dc", amt=16)
    E.emit("sync", lambda b: b.dma_start(out=inccol[:, :], in_=b_aps["inc"]),
           inc="Dc", amt=16)
    E.emit("sync", lambda b: b.dma_start(out=ones2[:, :],
                                         in_=E.con["ones2"].ap()), inc="Dc", amt=16)
    with (nc.sbuf_tensor("ls_bo", [1, C], F32) as bo_sb,
          nc.sbuf_tensor("ls_hi", [1, C], BF16) as hi1,
          nc.sbuf_tensor("ls_lo", [1, C], BF16) as lo1):
        E.emit("sync", lambda b: b.dma_start(out=bo_sb[:, :], in_=b_aps["bo"]),
               inc="Dc", amt=16)
        dc = E.val("Dc")
        E.emit("vector", lambda b: b.memset(warm[:, :], 0.0),
               wait=[("Dc", dc), ("S", E.val("S"))])
        E.emit("vector", lambda b: b.tensor_copy(hi1[:, :], bo_sb[:, :]),
               inc="V")
        E.emit("vector", lambda b: b.tensor_tensor(
            lo1[:, :], bo_sb[:, :], hi1[:, :], ALU.subtract), inc="V")
        bv = E.val("V")
        E.emit("gpsimd", lambda b: b.dma_start(out=bo2[0:1, :], in_=hi1[:, :]),
               wait=[("V", bv)], inc="lsb2", amt=16)
        E.emit("gpsimd", lambda b: b.dma_start(out=bo2[1:2, :], in_=lo1[:, :]),
               inc="lsb2", amt=16)
        E.barrier([("lsb2", E.val("lsb2"))], engines=("vector", "scalar"))
    dc = E.val("Dc")
    wws = [(f"lw{g}{k}", E.val(f"lw{g}{k}")) for g in "fico" for k in range(2)]
    zz = warm
    for cc in range(8):
        E.emit("vector", lambda b, cc=cc: b.tensor_scalar(
            c_fm[:, cc, :], zz[:, :], inccol[:, cc:cc + 1], None, ALU.add))
    E.emit("vector", lambda b: b.tensor_copy(
        c_bf[:, :, :].rearrange("p a s -> p (a s)"),
        c_fm[:, :, :].rearrange("p a s -> p (a s)")), inc="V")
    E.barrier([("V", E.val("V")), ("Dc", dc), ("lsb2", E.val("lsb2"))] + wws)

    nstep = NSTEP
    ylast = [0, 0]
    for t in range(nstep):
        p = t % 2
        ysm = f"lsy{p}"
        w = [("T", ylast[p])] if ylast[p] else []
        E.emit("sync", lambda b, t=t, p=p: b.dma_start(
            out=y2[:, p, :, :], in_=yin_ap[:, :, S2 * t:S2 * (t + 1)]),
            wait=w, inc=ysm, amt=16)
        yv = E.val(ysm)
        gates = (("f", f_sb, AF.Sigmoid, 0), ("i", i_sb, AF.Sigmoid, 8),
                 ("c", g_sb, AF.Tanh, 16))
        bank_free = [0, 0]
        for (gn, dst, func, bcol) in gates:
            for mt in range(8):
                bk = mt % 2
                wl = [(ysm, yv)]
                if bank_free[bk]:
                    wl.append(("S", bank_free[bk]))

                def mm(b, gn=gn, mt=mt, bk=bk, p=p):
                    ins = None
                    for cc in range(8):
                        b.matmul(ps_g[:, bk, 0:S2],
                                 wsb[gn][0][:, cc, 128 * mt:128 * (mt + 1)],
                                 y2[:, p, cc, :], start=(cc == 0), stop=False)
                    for cc in range(8):
                        ins = b.matmul(
                            ps_g[:, bk, 0:S2],
                            wsb[gn][1][:, cc, 128 * mt:128 * (mt + 1)],
                            c_bf[:, cc, :], start=False, stop=(cc == 7))
                    return ins
                E.emit("tensor", mm, wait=wl, inc="T")
                tv = E.val("T")
                E.emit("scalar", lambda b, gn=gn, mt=mt, bk=bk, dst=dst,
                       func=func, bcol=bcol: b.activation(
                           dst[:, mt, :], ps_g[:, bk, 0:S2], func,
                           bias=bcols[:, bcol + mt:bcol + mt + 1]),
                       wait=[("T", tv)], inc="S")
                bank_free[bk] = E.val("S")
        ylast[p] = E.val("T")
        sg = E.val("S")
        fl = f_sb[:, :, :].rearrange("p a s -> p (a s)")
        il = i_sb[:, :, :].rearrange("p a s -> p (a s)")
        gl = g_sb[:, :, :].rearrange("p a s -> p (a s)")
        cl = c_fm[:, :, :].rearrange("p a s -> p (a s)")
        cbl = c_bf[:, :, :].rearrange("p a s -> p (a s)")
        E.emit("vector", lambda b, fl=fl, cl=cl: b.tensor_tensor(
            fl, fl, cl, ALU.mult), wait=[("S", sg)])
        E.emit("vector", lambda b, il=il, gl=gl: b.tensor_tensor(
            il, il, gl, ALU.mult))
        E.emit("vector", lambda b, cl=cl, fl=fl, il=il: b.tensor_tensor(
            cl, fl, il, ALU.add))
        E.emit("vector", lambda b, cbl=cbl, cl=cl: b.tensor_copy(cbl, cl),
               inc="V")
        cv = E.val("V")
        E.emit("tensor", lambda b: None, wait=[("V", cv)])
    if "noh" in DBG:
        st.close()
        return [("lsy0", E.val("lsy0")), ("lsy1", E.val("lsy1"))]
    cfin = E.val("V")
    for st2 in range(2):
        pst = ps_h if st2 == 0 else ps_g

        def mmc(b, st2=st2, pst=pst):
            ins = None
            for mw in range(2):
                for cc in range(8):
                    b.matmul(pst[:, mw, :],
                             c_bf[:, cc, 128 * st2:128 * (st2 + 1)],
                             wsb["o"][1][:, cc, 512 * mw:512 * (mw + 1)],
                             start=(cc == 0), stop=False)
                ins = b.matmul(pst[:, mw, :], ones2[:, :],
                               bo2[0:2, 512 * mw:512 * (mw + 1)],
                               start=False, stop=True)
            return ins
        E.emit("tensor", mmc, wait=[("V", cfin)], inc="T")
        tv = E.val("T")
        E.emit("vector", lambda b, st2=st2, pst=pst: b.tensor_copy(
            cfuo[:, st2, :], pst[:, :, :].rearrange("p a s -> p (a s)")),
            wait=[("T", tv)], inc="V")
    cfv = E.val("V")
    E.barrier([("V", cfv)], engines=("tensor",))
    if "nohh" in DBG:
        st.close()
        return [("lsy0", E.val("lsy0")), ("lsy1", E.val("lsy1"))]
    xlast = [0, 0]
    olast = [0, 0]
    hlast = [0, 0]
    for t in range(nstep):
        p = t % 2
        ysm = f"lsy{p}"
        w = [("T", ylast[p])] if ylast[p] else []
        E.emit("sync", lambda b, t=t, p=p: b.dma_start(
            out=y2[:, p, :, :], in_=yin_ap[:, :, S2 * t:S2 * (t + 1)]),
            wait=w, inc=ysm, amt=16)
        yv = E.val(ysm)
        for st2 in range(2):
            pp = st2
            row0 = S2 * t + 128 * st2
            sp = (t * 2 + st2) % 2
            wl = [(ysm, yv)]
            if hlast[pp]:
                wl.append(("S", hlast[pp]))

            def mmh(b, st2=st2, p=p, pp=pp):
                ins = None
                for mw in range(2):
                    for cc in range(8):
                        ins = b.matmul(
                            ps_h[:, pp, :] if mw == 0 else ps_h[:, pp, :],
                            y2[:, p, cc, 128 * st2:128 * (st2 + 1)],
                            wsb["o"][0][:, cc, 512 * mw:512 * (mw + 1)],
                            start=(cc == 0), stop=(cc == 7))
                return ins
            def mmh2(b, st2=st2, p=p, pp=pp):
                ins = None
                for cc in range(8):
                    b.matmul(ps_h[:, pp, :],
                             y2[:, p, cc, 128 * st2:128 * (st2 + 1)],
                             wsb["o"][0][:, cc, 0:512],
                             start=(cc == 0), stop=(cc == 7))
                for cc in range(8):
                    ins = b.matmul(ps_g[:, pp, :],
                                   y2[:, p, cc, 128 * st2:128 * (st2 + 1)],
                                   wsb["o"][0][:, cc, 512:1024],
                                   start=(cc == 0), stop=(cc == 7))
                return ins
            E.emit("tensor", mmh2, wait=wl, inc="T")
            tv = E.val("T")
            ylast[p] = tv
            E.emit("vector", lambda b, st2=st2, pp=pp: b.tensor_tensor(
                ps_h[:, pp, :], ps_h[:, pp, :], cfuo[:, st2, 0:512], ALU.add),
                wait=[("T", tv)])
            E.emit("vector", lambda b, st2=st2, pp=pp: b.tensor_tensor(
                ps_g[:, pp, :], ps_g[:, pp, :], cfuo[:, st2, 512:1024],
                ALU.add), inc="V")
            av = E.val("V")
            xsm, osm = f"lsx{sp}", f"lso{sp}"
            wx = [("V", xlast[sp])] if xlast[sp] else []
            E.emit("sync", lambda b, row0=row0, sp=sp: b.dma_start(
                out=xr2[:, sp, :], in_=xin_ap[row0:row0 + 128, :]),
                wait=wx, inc=xsm, amt=16)
            xv = E.val(xsm)
            E.emit("scalar", lambda b, sp=sp, pp=pp: b.activation(
                o2[:, sp, 0:512], ps_h[:, pp, :], AF.Tanh),
                wait=[("V", av)], inc="S")
            E.emit("scalar", lambda b, sp=sp, pp=pp: b.activation(
                o2[:, sp, 512:1024], ps_g[:, pp, :], AF.Tanh), inc="S")
            sv = E.val("S")
            hlast[pp] = sv
            wl = [("S", sv), (xsm, xv)]
            if olast[sp]:
                wl.append((osm, olast[sp]))
            E.emit("vector", lambda b, sp=sp: b.tensor_tensor(
                o2[:, sp, :], o2[:, sp, :], xr2[:, sp, :], ALU.add),
                wait=wl, inc="V")
            ov = E.val("V")
            xlast[sp] = ov
            E.emit("sync", lambda b, row0=row0, sp=sp: b.dma_start(
                out=out_ap[row0:row0 + 128, :], in_=o2[:, sp, :]),
                wait=[("V", ov)], inc=osm, amt=16)
            olast[sp] = E.val(osm)
    st.close()
    return [("lso0", E.val("lso0")), ("lso1", E.val("lso1"))]


def build_lstm_program():
    nc = bass.Bass("TRN2", target_bir_lowering=False)
    xin = nc.dram_tensor("xin", [NSTEP * SEQ2, C], F32, kind="ExternalInput").ap()
    yin = nc.dram_tensor("yin", [128, 8, NSTEP * SEQ2], BF16,
                         kind="ExternalInput").ap()
    w_aps = {}
    for g in ("f", "i", "c", "o"):
        w = nc.dram_tensor(f"w{g}", [128, 8, C], F32, kind="ExternalInput").ap()
        u = nc.dram_tensor(f"u{g}", [128, 8, C], F32, kind="ExternalInput").ap()
        w_aps[g] = (w, u)
    b_aps = {
        "bcols": nc.dram_tensor("bcols", [128, 24], F32, kind="ExternalInput").ap(),
        "bo": nc.dram_tensor("bo", [1, C], F32, kind="ExternalInput").ap(),
        "inc": nc.dram_tensor("incc", [128, 8], F32, kind="ExternalInput").ap(),
    }
    out = nc.dram_tensor("out", [NSTEP * SEQ2, C], F32, kind="ExternalOutput").ap()
    E = Emitter(nc)
    E.con = make_consts(nc)
    done = emit_lstm_block(E, xin, yin, out, w_aps, b_aps, {})
    E.emit("sync", lambda b: None, wait=done)
    E.finalize()
    return nc


def prep_lstm_weights(wf, uf, bf, wi, ui, bi, wc, uc, bc, wo, uo, bo, in_c):
    w_aps = {}
    for g, (w, u) in (("f", (wf, uf)), ("i", (wi, ui)), ("c", (wc, uc)),
                      ("o", (wo, uo))):
        w_aps[f"w{g}"] = _chunk_lhst(np.ascontiguousarray(np.asarray(w).T))
        w_aps[f"u{g}"] = _chunk_lhst(np.ascontiguousarray(np.asarray(u).T))
    bcols = np.concatenate([_col_t(np.asarray(bf)), _col_t(np.asarray(bi)),
                            _col_t(np.asarray(bc))], axis=1)
    bo_r = np.ascontiguousarray(np.asarray(bo).reshape(1, C))
    incc = _col_t(np.asarray(in_c))
    return w_aps, bcols, bo_r, incc



def _chunk_lhst(a, nch=8):
    n = a.shape[1]
    return np.ascontiguousarray(a.reshape(nch, 128, n).transpose(1, 0, 2))


def _col_t(v):
    k = v.shape[0] // 128
    return np.ascontiguousarray(v.reshape(k, 128).T)


def prep_attn0_weights(w_attn, b_attn, w_proj, b_proj):
    wqkv = _chunk_lhst(np.asarray(w_attn))
    bvec = _col_t(np.asarray(b_attn))
    wo = _chunk_lhst(np.asarray(w_proj))
    bo = np.ascontiguousarray(np.asarray(b_proj).reshape(1, C))
    return wqkv, bvec, wo, bo


def prep_attn1_weights(W_Q, b_Q, W_K, b_K, W_V, b_V, W_O, b_O):
    wq = _chunk_lhst(np.ascontiguousarray(
        np.asarray(W_Q).transpose(1, 0, 2).reshape(C, C)))
    wk = _chunk_lhst(np.ascontiguousarray(
        np.asarray(W_K).transpose(1, 0, 2).reshape(C, C)))
    wv = _chunk_lhst(np.ascontiguousarray(
        np.asarray(W_V).transpose(1, 0, 2).reshape(C, C)))
    wo = _chunk_lhst(np.ascontiguousarray(np.asarray(W_O).reshape(C, C)))
    bvec = np.concatenate([_col_t(np.asarray(b_Q).reshape(-1)),
                           _col_t(np.asarray(b_K).reshape(-1)),
                           _col_t(np.asarray(b_V).reshape(-1))], axis=1)
    bo = np.ascontiguousarray(np.asarray(b_O).reshape(1, C))
    return wq, wk, wv, wo, bvec, bo


def prep_ffn_weights(w_gate, w_up, w_down):
    wg = _chunk_lhst(np.ascontiguousarray(w_gate.T))
    wu = _chunk_lhst(np.ascontiguousarray(w_up.T))
    wd = _chunk_lhst(np.ascontiguousarray(w_down.T), 24)
    return wg, wu, wd



def build_ffn_program(in_ln=True, out_ln=False):
    nc = bass.Bass("TRN2", target_bir_lowering=False)
    xin = nc.dram_tensor("xin", [TPC, C], F32, kind="ExternalInput").ap()
    wg = nc.dram_tensor("wg", [128, 8, FH], I32, kind="ExternalInput").ap()
    wu = nc.dram_tensor("wu", [128, 8, FH], I32, kind="ExternalInput").ap()
    wd = nc.dram_tensor("wd", [128, 24, C], I32, kind="ExternalInput").ap()
    xout = nc.dram_tensor("xout", [TPC, C], F32, kind="ExternalOutput").ap()
    E = Emitter(nc)
    E.con = make_consts(nc)
    if in_ln:
        ydram = nc.dram_tensor("ffn_ydram", [128, 8, TPC], BF16, kind="Internal")
        ln_done = emit_ln_pass(
            E, [xin[128 * i:128 * (i + 1), :] for i in range(TPC // 128)],
            ydram.ap(), "ln0")
        E.barrier(ln_done)
    else:
        ydram = nc.dram_tensor("yin", [128, 8, TPC], BF16, kind="ExternalInput")
    ffn_done = emit_ffn_block(E, xin, ydram.ap(), xout, wg, wu, wd, {})
    if out_ln:
        yout = nc.dram_tensor("yout", [128, 8, TPC], BF16,
                              kind="ExternalOutput").ap()
        E.barrier(ffn_done)
        ln2 = emit_ln_pass(
            E, [xout[128 * i:128 * (i + 1), :] for i in range(TPC // 128)],
            yout, "ln2")
        E.emit("sync", lambda b: None, wait=ln2 + ffn_done)
    else:
        E.emit("sync", lambda b: None, wait=ffn_done)
    E.finalize()
    return nc


def build_attn_program(layer, out_ln=True, in_ln=True, dbg_nqt=16, dbg_nb=BPC):
    nc = bass.Bass("TRN2", target_bir_lowering=False)
    xin = nc.dram_tensor("xin", [TPC, C], F32, kind="ExternalInput").ap()
    if layer == 0:
        wqkv = nc.dram_tensor("wqkv", [128, 8, 3 * C], F32,
                              kind="ExternalInput").ap()
        wq_ap, wk_ap, wv_ap = (wqkv[:, :, 0:C], wqkv[:, :, C:2 * C],
                               wqkv[:, :, 2 * C:3 * C])
    else:
        wq_ap = nc.dram_tensor("wq", [128, 8, C], F32, kind="ExternalInput").ap()
        wk_ap = nc.dram_tensor("wk", [128, 8, C], F32, kind="ExternalInput").ap()
        wv_ap = nc.dram_tensor("wv", [128, 8, C], F32, kind="ExternalInput").ap()
    wo = nc.dram_tensor("wo", [128, 8, C], F32, kind="ExternalInput").ap()
    bvec = nc.dram_tensor("bvec", [128, 24], F32, kind="ExternalInput").ap()
    bo = nc.dram_tensor("bo", [1, C], F32, kind="ExternalInput").ap()
    xout = nc.dram_tensor("xout", [TPC, C], F32, kind="ExternalOutput").ap()
    yout = nc.dram_tensor("yout", [128, 8, TPC], BF16,
                          kind="ExternalOutput").ap()
    if in_ln:
        ydram = nc.dram_tensor("att_ydram", [128, 8, TPC], BF16, kind="Internal")
    else:
        ydram = nc.dram_tensor("yin", [128, 8, TPC], BF16, kind="ExternalInput")
    E = Emitter(nc)
    E.con = make_consts(nc)
    E.dbgP_ap = nc.dram_tensor("dbgP", [128, 128], BF16, kind="ExternalOutput").ap()
    E.dbgA_ap = nc.dram_tensor("dbgA", [128, 8], F32, kind="ExternalOutput").ap()
    if layer == 0:
        b48, selc = make_alibi_consts(nc)
        E.con["alibi_b48"] = b48
        E.con["alibi_sel"] = selc
    if in_ln:
        ln_done = emit_ln_pass(
            E, [xin[128 * i:128 * (i + 1), :] for i in range(TPC // 128)],
            ydram.ap(), "lni")
        E.barrier(ln_done)
    at_done = emit_attn_block(E, xin, ydram.ap(), xout, wq_ap, wk_ap, wv_ap,
                              wo, bvec, bo, layer, "att", dbg_nqt=dbg_nqt,
                              dbg_nb=dbg_nb)
    E.barrier(at_done)
    if out_ln:
        ln2 = emit_ln_pass(
            E, [xout[128 * i:128 * (i + 1), :] for i in range(TPC // 128)],
            yout, "lno")
        E.emit("sync", lambda b: None, wait=ln2 + at_done)
    else:
        E.emit("sync", lambda b: None, wait=at_done)
    E.finalize()
    return nc


def _core_inmaps(inputs):
    x = np.ascontiguousarray(np.asarray(inputs["L0_attn_x"], np.float32))
    wqkv, bvec0, wo0, bo0 = prep_attn0_weights(
        inputs["L0_attn_w_attn"], inputs["L0_attn_b_attn"],
        inputs["L0_attn_w_proj"], inputs["L0_attn_b_proj"])
    wg, wu, wd = prep_ffn_weights(
        np.asarray(inputs["L0_ffn_w_gate"], np.int32),
        np.asarray(inputs["L0_ffn_w_up"], np.int32),
        np.asarray(inputs["L0_ffn_w_down"], np.int32))
    wq1, wk1, wv1, wo1, bvec1, bo1 = prep_attn1_weights(
        inputs["L1_attn_W_Q"], inputs["L1_attn_b_Q"],
        inputs["L1_attn_W_K"], inputs["L1_attn_b_K"],
        inputs["L1_attn_W_V"], inputs["L1_attn_b_V"],
        inputs["L1_attn_W_O"], inputs["L1_attn_b_O"])
    w_aps, bcols, bo_r, incc = prep_lstm_weights(
        inputs["L1_ffn_wf"], inputs["L1_ffn_uf"], inputs["L1_ffn_bf"],
        inputs["L1_ffn_wi"], inputs["L1_ffn_ui"], inputs["L1_ffn_bi"],
        inputs["L1_ffn_wc"], inputs["L1_ffn_uc"], inputs["L1_ffn_bc"],
        inputs["L1_ffn_wo"], inputs["L1_ffn_uo"], inputs["L1_ffn_bo"],
        inputs["L1_ffn_in_c"])
    return dict(x=x, wqkv=wqkv, bvec0=bvec0, wo0=wo0, bo0=bo0,
                wg=wg, wu=wu, wd=wd, wq1=wq1, wk1=wk1, wv1=wv1, wo1=wo1,
                bvec1=bvec1, bo1=bo1, lstm_w=w_aps, bcols=bcols, bo_r=bo_r,
                incc=incc)


def build_fused_program():
    nc = bass.Bass("TRN2", target_bir_lowering=False)
    xin = nc.dram_tensor("xin", [TPC, C], F32, kind="ExternalInput").ap()
    wqkv = nc.dram_tensor("wqkv", [128, 8, 3 * C], F32, kind="ExternalInput").ap()
    wo0 = nc.dram_tensor("wo0", [128, 8, C], F32, kind="ExternalInput").ap()
    bvec0 = nc.dram_tensor("bvec0", [128, 24], F32, kind="ExternalInput").ap()
    bo0 = nc.dram_tensor("bo0", [1, C], F32, kind="ExternalInput").ap()
    wg = nc.dram_tensor("wg", [128, 8, FH], I32, kind="ExternalInput").ap()
    wu = nc.dram_tensor("wu", [128, 8, FH], I32, kind="ExternalInput").ap()
    wd = nc.dram_tensor("wd", [128, 24, C], I32, kind="ExternalInput").ap()
    wq1 = nc.dram_tensor("wq1", [128, 8, C], F32, kind="ExternalInput").ap()
    wk1 = nc.dram_tensor("wk1", [128, 8, C], F32, kind="ExternalInput").ap()
    wv1 = nc.dram_tensor("wv1", [128, 8, C], F32, kind="ExternalInput").ap()
    wo1 = nc.dram_tensor("wo1", [128, 8, C], F32, kind="ExternalInput").ap()
    bvec1 = nc.dram_tensor("bvec1", [128, 24], F32, kind="ExternalInput").ap()
    bo1 = nc.dram_tensor("bo1", [1, C], F32, kind="ExternalInput").ap()
    w_aps = {}
    for g in ("f", "i", "c", "o"):
        w = nc.dram_tensor(f"w{g}", [128, 8, C], F32, kind="ExternalInput").ap()
        u = nc.dram_tensor(f"u{g}", [128, 8, C], F32, kind="ExternalInput").ap()
        w_aps[g] = (w, u)
    b_aps = {
        "bcols": nc.dram_tensor("bcols", [128, 24], F32, kind="ExternalInput").ap(),
        "bo": nc.dram_tensor("bo", [1, C], F32, kind="ExternalInput").ap(),
        "inc": nc.dram_tensor("incc", [128, 8], F32, kind="ExternalInput").ap(),
    }
    out = nc.dram_tensor("out", [NSTEP * SEQ2, C], F32, kind="ExternalOutput").ap()
    x1 = nc.dram_tensor("x1s", [TPC, C], F32, kind="Internal").ap()
    x2 = nc.dram_tensor("x2s", [TPC, C], F32, kind="Internal").ap()
    x3 = nc.dram_tensor("x3s", [TPC, C], F32, kind="Internal")
    x4 = nc.dram_tensor("x4s", [TPC, C], F32, kind="Internal")
    ya = nc.dram_tensor("yas", [128, 8, TPC], BF16, kind="Internal").ap()
    yb = nc.dram_tensor("ybs", [128, 8, TPC], BF16, kind="Internal").ap()

    E = Emitter(nc)
    E.con = make_consts(nc)
    b48, selc = make_alibi_consts(nc)
    E.con["alibi_b48"] = b48
    E.con["alibi_sel"] = selc

    tiles = lambda ap: [ap[128 * i:128 * (i + 1), :] for i in range(TPC // 128)]
    d = emit_ln_pass(E, tiles(xin), ya, "ln0")
    E.barrier(d)
    d = emit_attn_block(E, xin, ya, x1, wqkv[:, :, 0:C], wqkv[:, :, C:2 * C],
                        wqkv[:, :, 2 * C:3 * C], wo0, bvec0, bo0, 0, "a0")
    E.barrier(d)
    d = emit_ln_pass(E, tiles(x1), yb, "ln1")
    E.barrier(d)
    d = emit_ffn_block(E, x1, yb, x2, wg, wu, wd, {})
    E.barrier(d)
    d = emit_ln_pass(E, tiles(x2), ya, "ln2")
    E.barrier(d)

    def a2a_row(bb, qt):
        t0 = 128 * qt
        return (t0 // SEQ2) * 2 * SEQ2 + bb * SEQ2 + (t0 % SEQ2)
    d = emit_attn_block(E, x2, ya, x3.ap(), wq1, wk1, wv1, wo1, bvec1, bo1,
                        1, "a1", out_row_fn=a2a_row)
    E.barrier(d)
    E.emit("gpsimd", lambda b: b.collective_compute(
        "AllToAll", ALU.bypass, replica_groups=[list(range(NCORES))],
        ins=[x3.ap().opt()], outs=[x4.ap().opt()]), inc="CC", amt=1)
    E.barrier([("CC", 1)])
    d = emit_ln_pass(E, tiles(x4.ap()), yb, "ln3")
    E.barrier(d)
    d = emit_lstm_block(E, x4.ap(), yb, out, w_aps, b_aps, {})
    E.emit("sync", lambda b: None, wait=d)
    E.finalize()
    return nc


def kernel_fused(**inputs):
    from concourse.aot_env import allow_ici_all_to_all
    pp = _core_inmaps(inputs)
    x = pp["x"]
    cores = list(range(NCORES))
    with allow_ici_all_to_all():
        nc = build_fused_program()
        im = []
        for r in cores:
            m = {"xin": np.ascontiguousarray(x[2 * r:2 * r + 2].reshape(TPC, C)),
                 "wqkv": pp["wqkv"], "wo0": pp["wo0"], "bvec0": pp["bvec0"],
                 "bo0": pp["bo0"], "wg": pp["wg"], "wu": pp["wu"], "wd": pp["wd"],
                 "wq1": pp["wq1"], "wk1": pp["wk1"], "wv1": pp["wv1"],
                 "wo1": pp["wo1"], "bvec1": pp["bvec1"], "bo1": pp["bo1"],
                 "bcols": pp["bcols"], "bo": pp["bo_r"], "incc": pp["incc"]}
            m.update(pp["lstm_w"])
            im.append(m)
        res = run_bass_kernel_spmd(nc, im, core_ids=cores).results
    out = np.empty((NSTEP, NCORES * SEQ2, C), np.float32)
    for r in cores:
        out[:, SEQ2 * r:SEQ2 * (r + 1), :] = res[r]["out"].reshape(
            NSTEP, SEQ2, C)
    return out


def kernel(**inputs):
    _ = inputs.get("L1_attn_pos_embed")
    return kernel_fused(**inputs)


def kernel_unfused(**inputs):
    pp = _core_inmaps(inputs)
    x = pp["x"]
    cores = list(range(NCORES))
    nc1 = build_attn_program(0, out_ln=True, in_ln=True)
    im1 = [{"xin": np.ascontiguousarray(x[2 * r:2 * r + 2].reshape(TPC, C)),
            "wqkv": pp["wqkv"], "wo": pp["wo0"], "bvec": pp["bvec0"],
            "bo": pp["bo0"]} for r in cores]
    r1 = run_bass_kernel_spmd(nc1, im1, core_ids=cores).results
    nc2 = build_ffn_program(in_ln=False, out_ln=True)
    im2 = [{"xin": r1[r]["xout"], "yin": r1[r]["yout"], "wg": pp["wg"],
            "wu": pp["wu"], "wd": pp["wd"]} for r in cores]
    r2 = run_bass_kernel_spmd(nc2, im2, core_ids=cores).results
    nc3 = build_attn_program(1, out_ln=True, in_ln=False)
    im3 = [{"xin": r2[r]["xout"], "yin": r2[r]["yout"], "wq": pp["wq1"],
            "wk": pp["wk1"], "wv": pp["wv1"], "wo": pp["wo1"],
            "bvec": pp["bvec1"], "bo": pp["bo1"]} for r in cores]
    r3 = run_bass_kernel_spmd(nc3, im3, core_ids=cores).results
    x3 = np.stack([r3[r]["xout"] for r in cores])
    y3 = np.stack([r3[r]["yout"] for r in cores])
    nc4 = build_lstm_program()
    im4 = []
    for r in cores:
        xs = np.empty((NSTEP * SEQ2, C), np.float32)
        ys = np.empty((128, 8, NSTEP * SEQ2), y3.dtype)
        for t in range(NSTEP):
            src_r = t // 2
            off = (t % 2) * TB + SEQ2 * r
            xs[SEQ2 * t:SEQ2 * (t + 1)] = x3[src_r][off:off + SEQ2]
            ys[:, :, SEQ2 * t:SEQ2 * (t + 1)] = y3[src_r][:, :, off:off + SEQ2]
        im4.append(dict(xin=xs, yin=np.ascontiguousarray(ys),
                        bcols=pp["bcols"], bo=pp["bo_r"], incc=pp["incc"],
                        **pp["lstm_w"]))
    r4 = run_bass_kernel_spmd(nc4, im4, core_ids=cores).results
    out = np.empty((NSTEP, NCORES * SEQ2, C), np.float32)
    for r in cores:
        out[:, SEQ2 * r:SEQ2 * (r + 1), :] = r4[r]["out"].reshape(
            NSTEP, SEQ2, C)
    return out


# version 11
# speedup vs baseline: 1.2381x; 1.0025x over previous
import contextlib
import numpy as np
import ml_dtypes
import concourse.bass as bass
import concourse.mybir as mybir
from concourse.bass_utils import run_bass_kernel_spmd

F32 = mybir.dt.float32
BF16 = mybir.dt.bfloat16
I32 = mybir.dt.int32
AF = mybir.ActivationFunctionType
ALU = mybir.AluOpType
AX = mybir.AxisListType
BF = ml_dtypes.bfloat16

NCORES = 8
C = 1024
NH = 16
DH = 64
FH = 3072
TB = 2048
BPC = 2
TPC = TB * BPC
SEQ2 = 256
NSTEP = 16
EPS = 1e-5
NEGBIG = -1e9
RCP127 = float(np.float32(1.0 / 127.0))

ENGS = ("sync", "gpsimd", "tensor", "vector", "scalar")
DBG = set()


class Emitter:

    SNAP = {"V": "vector", "S": "scalar", "T": "tensor"}

    def __init__(self, nc):
        self.nc = nc
        self.ops = {e: [] for e in ENGS}
        self.sems = {}
        self.cnt = {}
        self.producer = {}
        self.dma_sem = set()
        self.hw = {e: {} for e in ENGS}
        self.know = {e: {} for e in ENGS}
        self.ck = {s: {} for s in self.SNAP}

    def sem(self, name):
        if name not in self.sems:
            self.sems[name] = self.nc.alloc_semaphore(name)
            self.cnt[name] = 0
        return self.sems[name]

    def emit(self, eng, fn, wait=(), inc=None, amt=1):
        for s, _v in wait:
            self.sem(s)
        if inc is not None:
            self.sem(inc)
            self.cnt[inc] += amt
            if amt >= 16:
                self.dma_sem.add(inc)
            p = self.producer.get(inc)
            if p is None:
                self.producer[inc] = eng
            elif p != eng:
                self.producer[inc] = "*"
        if inc is not None and inc in self.SNAP:
            assert eng == self.SNAP[inc], (inc, eng)
        sems = self.sems
        hw = self.hw[eng]
        know = self.know[eng]
        kept = []
        for s, v in wait:
            if v <= 0:
                continue
            if v <= hw.get(s, 0):
                continue
            if (s not in self.dma_sem and self.producer.get(s) == eng):
                continue
            if know.get(s, 0) >= v:
                continue
            hw[s] = v
            kept.append((s, v))
        for s, v in wait:
            if v <= 0:
                continue
            if know.get(s, 0) < v:
                know[s] = v
            snap = self.ck.get(s)
            if snap is not None:
                ex = snap.get(v)
                if ex:
                    for s2, v2 in ex.items():
                        if know.get(s2, 0) < v2:
                            know[s2] = v2
        if inc is not None and inc in self.SNAP:
            newv = self.cnt[inc]
            kk = dict(know)
            kk[inc] = newv
            self.ck[inc][newv] = kk
            know[inc] = newv

        def g(b, fn=fn, kept=tuple(kept), inc=inc, amt=amt):
            for s, v in kept:
                b.wait_ge(sems[s], v)
            ins = fn(b)
            if inc is not None:
                assert ins is not None, "inc requires an instruction"
                ins.then_inc(sems[inc], amt)
        self.ops[eng].append(g)

    def val(self, sem):
        self.sem(sem)
        return self.cnt[sem]

    def barrier(self, waits, engines=ENGS):
        for e in engines:
            self.emit(e, lambda b: None, wait=waits)

    def finalize(self):
        nc = self.nc
        with nc.Block() as block:
            @block.sync
            def _(b):
                for g in self.ops["sync"]:
                    g(b)

            @block.gpsimd
            def _(b):
                for g in self.ops["gpsimd"]:
                    g(b)

            @block.tensor
            def _(b):
                for g in self.ops["tensor"]:
                    g(b)

            @block.vector
            def _(b):
                for g in self.ops["vector"]:
                    g(b)

            @block.scalar
            def _(b):
                for g in self.ops["scalar"]:
                    g(b)



def make_consts(nc):
    con = {}
    con["ident"] = nc.inline_tensor(np.eye(128, dtype=BF), name="ident_c")
    con["diagmask"] = nc.inline_tensor(
        (np.triu(np.ones((128, 128), np.float32), 1) * NEGBIG), name="diagmask_c")
    con["ones2"] = nc.inline_tensor(np.ones((2, 128), dtype=BF), name="ones2_c")
    return con


def make_alibi_consts(nc):
    kt = np.arange(TB, dtype=np.float64) - (TB - 1)
    rows = np.zeros((48, TB), np.float64)
    for h in range(NH):
        slope = 2.0 ** (-0.5 * (h + 1))
        v = slope * kt
        b1 = v.astype(BF).astype(np.float64)
        b2 = (v - b1).astype(BF).astype(np.float64)
        b3 = (v - b1 - b2).astype(BF).astype(np.float64)
        rows[3 * h + 0] = b1
        rows[3 * h + 1] = b2
        rows[3 * h + 2] = b3
    sel = np.zeros((48, NH * 128), np.float32)
    for h in range(NH):
        sel[3 * h:3 * h + 3, 128 * h:128 * (h + 1)] = 1.0
    b48 = nc.inline_tensor(rows.astype(BF), name="alibi_b48_c")
    selc = nc.inline_tensor(sel.astype(BF), name="alibi_sel_c")
    return b48, selc



def emit_act_warmup(E, funcs, scratch):
    E.emit("vector", lambda b: b.memset(scratch[:, :], 1.0), inc="V")
    mw = [("V", E.val("V"))]
    for k in range(3):
        for f in funcs:
            E.emit("scalar", lambda b, f=f: b.activation(
                scratch[:, 0:256], scratch[:, 0:256], f), wait=mw, inc="S")
            mw = [("S", E.val("S"))]


def emit_ln_pass(E, src_aps, ydram_ap, tag, capture=None):
    nc = E.nc
    nt = len(src_aps)
    st = contextlib.ExitStack()
    xb_all = st.enter_context(nc.sbuf_tensor(f"{tag}_xb", [128, nt, C], BF16))
    stt = st.enter_context(nc.sbuf_tensor(f"{tag}_st", [128, nt, 8], F32))
    scr = st.enter_context(nc.sbuf_tensor(f"{tag}_scr", [128, C], BF16))
    scr2 = st.enter_context(nc.sbuf_tensor(f"{tag}_scr2", [128, C], BF16))
    ytile = st.enter_context(nc.sbuf_tensor(f"{tag}_yt", [128, 2, C], BF16))
    ystage = st.enter_context(nc.sbuf_tensor(f"{tag}_ys", [128, 2, C], BF16))
    warm = st.enter_context(nc.sbuf_tensor(f"{tag}_warm", [128, 256], F32))
    ident_sb = st.enter_context(nc.sbuf_tensor(f"{tag}_id", [128, 128], BF16))
    pst2 = st.enter_context(nc.psum_tensor(f"{tag}_psT", [128, 2, 1024], BF16))

    eps_col = st.enter_context(nc.sbuf_tensor(f"{tag}_eps", [128, 1], F32))
    E.emit("vector", lambda b: b.memset(eps_col[:, :], EPS))
    E.emit("sync", lambda b: b.dma_start(out=ident_sb[:, :],
                                         in_=E.con["ident"].ap()),
           inc="Dc", amt=16)
    idv = E.val("Dc")
    emit_act_warmup(E, [AF.Ln, AF.Square, AF.Copy], warm)
    nt4 = (nt + 3) // 4
    gsem = {}
    for g in range(4):
        lo, hi = g * nt4, min((g + 1) * nt4, nt)
        if lo >= hi:
            continue
        name = f"Gx{g}"
        for i in range(lo, hi):
            E.emit("gpsimd", lambda b, i=i: b.dma_start(
                out=xb_all[:, i, :], in_=src_aps[i]), inc=name, amt=16)
        gv = E.val(name)
        for i in range(lo, hi):
            E.emit("vector", lambda b, i=i: b.tensor_scalar(
                scr[:, :], xb_all[:, i, :], 1.0, 0.0, ALU.mult, ALU.add,
                accum_out=stt[:, i, 0:1]), wait=[(name, gv)], inc="V")
            E.emit("scalar", lambda b, i=i: b.activation(
                scr2[:, :], xb_all[:, i, :], AF.Square,
                accum_out=stt[:, i, 1:2]), wait=[(name, gv)], inc="S")
    v_stats = E.val("V")
    s_stats = E.val("S")
    spacer = lambda b: b.tensor_copy(scr[:, 0:512], scr2[:, 0:512])
    E.emit("vector", lambda b: b.tensor_scalar(
        stt[:, :, 2:3], stt[:, :, 0:1], 1.0 / C, None, ALU.mult),
        wait=[("V", v_stats), ("S", s_stats)])
    E.emit("vector", spacer)
    E.emit("vector", lambda b: b.tensor_tensor(
        stt[:, :, 3:4], stt[:, :, 2:3], stt[:, :, 2:3], ALU.mult))
    E.emit("vector", spacer)
    E.emit("vector", lambda b: b.scalar_tensor_tensor(
        stt[:, :, 4:5], stt[:, :, 1:2], 1.0 / C, stt[:, :, 3:4],
        ALU.mult, ALU.subtract))
    E.emit("vector", spacer)
    E.emit("vector", spacer, inc="V")
    v_var = E.val("V")
    E.emit("scalar", lambda b: b.activation(
        stt[:, :, 5:6], stt[:, :, 4:5], AF.Ln, bias=eps_col[:, 0:1]),
        wait=[("V", v_var)])
    emit_act_warmup(E, [AF.Exp], warm)
    E.emit("scalar", lambda b: b.activation(
        stt[:, :, 6:7], stt[:, :, 5:6], AF.Exp, scale=-0.5))
    E.emit("scalar", lambda b: b.activation(warm[:, 0:256], warm[:, 0:256],
                                            AF.Copy), inc="S")
    s_rstd = E.val("S")
    E.emit("vector", lambda b: b.scalar_tensor_tensor(
        stt[:, :, 7:8], stt[:, :, 2:3], -1.0, stt[:, :, 6:7],
        ALU.mult, ALU.mult), wait=[("S", s_rstd)], inc="V")
    E.emit("vector", spacer)
    E.emit("vector", spacer)
    v_nmr = E.val("V")
    if capture is not None:
        E.emit("vector", lambda b: b.tensor_copy(capture[:, 0:8 * nt],
                                                 stt[:, :, :]), inc="V")
    ev = [0, 0]
    yw = [0, 0]
    sv = [0, 0]
    for i in range(nt):
        p = i % 2
        w = [("V", v_nmr)]
        if yw[p]:
            w.append(("T", yw[p]))
        E.emit("vector", lambda b, i=i, p=p: b.tensor_scalar(
            ytile[:, p, :], xb_all[:, i, :], stt[:, i, 6:7], stt[:, i, 7:8],
            ALU.mult, ALU.add), wait=w, inc="V")
        v_y = E.val("V")
        w = [("V", v_y), ("Dc", idv)]
        if ev[p]:
            w.append(("V", ev[p]))

        def tr(b, p=p):
            ins = None
            for c in range(8):
                ins = b.transpose(pst2[:, p, 128 * c:128 * (c + 1)],
                                  ytile[:, p, 128 * c:128 * (c + 1)],
                                  ident_sb[:, :])
            return ins
        E.emit("tensor", tr, wait=w, inc="T")
        yw[p] = E.val("T")
        ysem = f"{tag}y{p}"
        w = [("T", yw[p])]
        if sv[p]:
            w.append((ysem, sv[p]))
        E.emit("vector", lambda b, p=p: b.tensor_copy(
            ystage[:, p, :], pst2[:, p, :]), wait=w, inc="V")
        ev[p] = E.val("V")
        E.emit("sync", lambda b, p=p, i=i: b.dma_start(
            out=ydram_ap[:, :, 128 * i:128 * (i + 1)],
            in_=ystage[:, p, :].rearrange("x (a c) -> x a c", a=8)),
            wait=[("V", ev[p])], inc=ysem, amt=16)
        sv[p] = E.val(ysem)
    st.close()
    return [(f"{tag}y0", E.val(f"{tag}y0")), (f"{tag}y1", E.val(f"{tag}y1"))]



def emit_ffn_block(E, xin_ap, yin_ap, xout_ap, wg_ap, wu_ap, wd_ap, ctx,
                   ntok=TPC):
    nc = E.nc
    st = contextlib.ExitStack()
    wg_sb = st.enter_context(nc.sbuf_tensor("ffn_wg", [128, 8, FH], BF16))
    wu_sb = st.enter_context(nc.sbuf_tensor("ffn_wu", [128, 8, FH], BF16))
    wd_sb = st.enter_context(nc.sbuf_tensor("ffn_wd", [128, 24, C], BF16))
    warm_sb = st.enter_context(nc.sbuf_tensor("ffn_warm", [128, 256], F32))
    emit_act_warmup(E, [AF.Tanh], warm_sb)

    with nc.sbuf_tensor("ffn_stage", [128, 2, FH], I32) as stage:
        last_dq = [0, 0]
        srcs = [(wg_ap, wg_sb, 8, FH, RCP127), (wu_ap, wu_sb, 8, FH, RCP127),
                (wd_ap, wd_sb, 24, C, float(np.float32(0.5) * np.float32(RCP127)))]
        k = 0
        for (src, dst, nch, wid, dqs) in srcs:
            for cc in range(nch):
                p = k % 2
                w = [("V", last_dq[p])] if last_dq[p] else []
                gsem = f"Gs{p}"
                E.emit("gpsimd", lambda b, src=src, cc=cc, p=p, wid=wid:
                       b.dma_start(out=stage[:, p, 0:wid], in_=src[:, cc, :]),
                       wait=w, inc=gsem, amt=16)
                gv = E.val(gsem)
                E.emit("vector", lambda b, dst=dst, cc=cc, p=p, wid=wid, dqs=dqs:
                       b.tensor_scalar(dst[:, cc, :], stage[:, p, 0:wid],
                                       dqs, None, ALU.mult),
                       wait=[(gsem, gv)], inc="V")
                last_dq[p] = E.val("V")
                k += 1
    dq_done = E.val("V")
    E.barrier([("V", dq_done)])

    y_blk = st.enter_context(nc.sbuf_tensor("ffn_y", [128, 2, 8, 512], BF16))
    hid = st.enter_context(nc.sbuf_tensor("ffn_hid", [128, 24, 512], BF16))
    t_sb = st.enter_context(nc.sbuf_tensor("ffn_t", [128, 2, 512], BF16))
    g_sb = st.enter_context(nc.sbuf_tensor("ffn_g", [128, 2, 512], BF16))
    xres2 = st.enter_context(nc.sbuf_tensor("ffn_xr", [128, 2, C], F32))
    out2 = st.enter_context(nc.sbuf_tensor("ffn_out", [128, 2, C], F32))
    ps_s = st.enter_context(nc.psum_tensor("ffn_psA", [128, 2048], F32))
    ps_m = st.enter_context(nc.psum_tensor("ffn_psM", [128, 1024], F32))

    last_hmul = [0, 0]
    last_gu = 0
    prev_res = 0
    last_store = [0, 0]
    last_xload = [0, 0]
    last_yload = [0, 0]

    nblk = ntok // 512
    for blk in range(nblk):
        bp = blk % 2
        w = [("T", last_yload[bp])] if last_yload[bp] else []
        ybsem = f"Dyb{bp}"
        E.emit("sync", lambda b, blk=blk, bp=bp: b.dma_start(
            out=y_blk[:, bp, :, :],
            in_=yin_ap[:, :, 512 * blk:512 * (blk + 1)]),
            wait=w, inc=ybsem, amt=16)
        yv = (ybsem, E.val(ybsem))
        for ht in range(24):
            pp = ht % 2
            gslc = (1024 * pp, 1024 * pp + 512)
            uslc = (1024 * pp + 512, 1024 * pp + 1024)
            w = [yv]
            if last_hmul[pp]:
                w.append(("V", last_hmul[pp]))

            def gu(b, ht=ht, bp=bp, gslc=gslc, uslc=uslc):
                ins = None
                for cc in range(8):
                    b.matmul(ps_s[:, gslc[0]:gslc[1]],
                             wg_sb[:, cc, 128 * ht:128 * (ht + 1)],
                             y_blk[:, bp, cc, :], start=(cc == 0),
                             stop=(cc == 7))
                for cc in range(8):
                    ins = b.matmul(ps_s[:, uslc[0]:uslc[1]],
                                   wu_sb[:, cc, 128 * ht:128 * (ht + 1)],
                                   y_blk[:, bp, cc, :], start=(cc == 0),
                                   stop=(cc == 7))
                return ins
            E.emit("tensor", gu, wait=w, inc="T")
            last_gu = E.val("T")
            E.emit("scalar", lambda b, pp=pp, gslc=gslc: b.activation(
                t_sb[:, pp, :], ps_s[:, gslc[0]:gslc[1]], AF.Tanh, scale=0.5),
                wait=[("T", last_gu)], inc="S")
            sv = E.val("S")
            E.emit("vector", lambda b, pp=pp, gslc=gslc:
                   b.scalar_tensor_tensor(g_sb[:, pp, :], t_sb[:, pp, :], 1.0,
                                          ps_s[:, gslc[0]:gslc[1]],
                                          ALU.add, ALU.mult),
                   wait=[("S", sv)])
            E.emit("vector", lambda b, ht=ht, pp=pp, uslc=uslc:
                   b.tensor_tensor(hid[:, ht, :], ps_s[:, uslc[0]:uslc[1]],
                                   g_sb[:, pp, :], ALU.mult), inc="V")
            last_hmul[pp] = E.val("V")
        last_yload[bp] = last_gu
        hv = E.val("V")
        for tt in range(4):
            p = tt % 2
            w = [("V", hv)]
            if prev_res:
                w.append(("V", prev_res))

            def down(b, tt=tt):
                ins = None
                for mc in range(2):
                    for hc in range(24):
                        ins = b.matmul(ps_m[:, 512 * mc:512 * (mc + 1)],
                                       hid[:, hc, 128 * tt:128 * (tt + 1)],
                                       wd_sb[:, hc, 512 * mc:512 * (mc + 1)],
                                       start=(hc == 0), stop=(hc == 23))
                return ins
            E.emit("tensor", down, wait=w, inc="T")
            tv = E.val("T")
            row0 = 512 * blk + 128 * tt
            xsem, osem = f"Dx{p}", f"Do{p}"
            wl = [("V", last_xload[p])] if last_xload[p] else []
            E.emit("sync", lambda b, row0=row0, p=p: b.dma_start(
                out=xres2[:, p, :], in_=xin_ap[row0:row0 + 128, :]),
                wait=wl, inc=xsem, amt=16)
            dv = E.val(xsem)
            w = [("T", tv), (xsem, dv)]
            if last_store[p]:
                w.append((osem, last_store[p]))
            E.emit("vector", lambda b, p=p: b.tensor_tensor(
                out2[:, p, :], ps_m[:, :], xres2[:, p, :], ALU.add),
                wait=w, inc="V")
            prev_res = E.val("V")
            last_xload[p] = prev_res
            E.emit("sync", lambda b, row0=row0, p=p: b.dma_start(
                out=xout_ap[row0:row0 + 128, :], in_=out2[:, p, :]),
                wait=[("V", prev_res)], inc=osem, amt=16)
            last_store[p] = E.val(osem)
    st.close()
    return [("Do0", E.val("Do0")), ("Do1", E.val("Do1"))]



def emit_attn_block(E, xin_ap, yin_ap, xout_ap, wq_ap, wk_ap, wv_ap, wo_ap,
                    bvec_ap, bo_ap, layer, tag, dbg_nqt=16, dbg_nb=BPC,
                    out_row_fn=None):
    nc = E.nc
    st = contextlib.ExitStack()
    warm = st.enter_context(nc.sbuf_tensor(f"{tag}_warm", [128, 256], F32))
    Q2 = st.enter_context(nc.sbuf_tensor(f"{tag}_q2", [128, 8, TB], BF16))
    K2 = st.enter_context(nc.sbuf_tensor(f"{tag}_k2", [128, 8, TB], BF16))
    Vt = st.enter_context(nc.sbuf_tensor(f"{tag}_vt", [128, 16, C], BF16))
    bvec = st.enter_context(nc.sbuf_tensor(f"{tag}_bv", [128, 24], F32))
    bvbf = st.enter_context(nc.sbuf_tensor(f"{tag}_bvb", [128, 8], BF16))
    bo_sb = st.enter_context(nc.sbuf_tensor(f"{tag}_bo", [1, C], F32))
    bias2 = st.enter_context(nc.sbuf_tensor(f"{tag}_b2", [2, C], BF16))
    ones2 = st.enter_context(nc.sbuf_tensor(f"{tag}_on2", [2, 128], BF16))
    dmask = st.enter_context(nc.sbuf_tensor(f"{tag}_dm", [128, 128], F32))
    ident_sb = st.enter_context(nc.sbuf_tensor(f"{tag}_id", [128, 128], BF16))
    if layer == 0:
        sel_sb = st.enter_context(nc.sbuf_tensor(f"{tag}_sel", [48, NH * 128], BF16))
        b48_sb = st.enter_context(nc.sbuf_tensor(f"{tag}_b48", [48, TB], BF16))
    ast = st.enter_context(nc.sbuf_tensor(f"{tag}_ast", [128, 2, 8], F32))
    E.dbgP = st.enter_context(nc.sbuf_tensor(f"{tag}_dbgp", [128, 128], BF16))
    E.dbgA = st.enter_context(nc.sbuf_tensor(f"{tag}_dbga", [128, 8], F32))
    P_sb = st.enter_context(nc.sbuf_tensor(f"{tag}_p", [128, 2, TB], BF16))
    PT_sb = st.enter_context(nc.sbuf_tensor(f"{tag}_pt", [128, TB], BF16))
    zn2 = st.enter_context(nc.sbuf_tensor(f"{tag}_zn", [128, 128], BF16))
    qst = st.enter_context(nc.sbuf_tensor(f"{tag}_qst", [128, 2, 128], BF16))
    zT2 = st.enter_context(nc.sbuf_tensor(f"{tag}_zt", [128, 128], BF16))
    ps_s = st.enter_context(nc.psum_tensor(f"{tag}_pss", [128, 2048], F32))
    ps_t = st.enter_context(nc.psum_tensor(f"{tag}_pst", [128, 1024], BF16))
    ps_z = st.enter_context(nc.psum_tensor(f"{tag}_psz", [128, 128], F32))
    ps_m = st.enter_context(nc.psum_tensor(f"{tag}_psm", [128, 1024], F32))

    emit_act_warmup(E, [AF.Exp], warm)
    for dst, srcap in ((bvec, bvec_ap), (bo_sb, bo_ap),
                       (ones2, E.con["ones2"].ap()),
                       (dmask, E.con["diagmask"].ap()),
                       (ident_sb, E.con["ident"].ap())):
        E.emit("sync", lambda b, dst=dst, srcap=srcap: b.dma_start(
            out=dst[:, :], in_=srcap), inc="Dc", amt=16)
    if layer == 0:
        E.emit("sync", lambda b: b.dma_start(out=sel_sb[:, :],
                                             in_=E.con["alibi_sel"].ap()),
               inc="Dc", amt=16)
        E.emit("sync", lambda b: b.dma_start(out=b48_sb[:, :],
                                             in_=E.con["alibi_b48"].ap()),
               inc="Dc", amt=16)
    dc_v = E.val("Dc")
    E.emit("vector", lambda b: b.tensor_copy(bvbf[:, :], bvec[:, 16:24]),
           wait=[("Dc", dc_v)], inc="V")
    E.emit("vector", lambda b: b.memset(qst[:, :, :], 0.0), inc="V")
    E.barrier([("Dc", dc_v), ("V", E.val("V"))])

    osem = [f"{tag}o0", f"{tag}o1"]
    xsem = [f"{tag}x0", f"{tag}x1"]
    last_store = [0, 0]
    last_xload = [0, 0]
    bias_ready = False
    prev_merge_evac = [0]

    for bb in range(dbg_nb):
        tok0 = bb * TB
        E.barrier([("T", E.val("T"))], engines=("vector", "sync", "gpsimd"))
        with contextlib.ExitStack() as stA:
            wq_sb = stA.enter_context(nc.sbuf_tensor(f"{tag}_wq{bb}", [128, 8, C], BF16))
            wk_sb = stA.enter_context(nc.sbuf_tensor(f"{tag}_wk{bb}", [128, 8, C], BF16))
            wv_sb = stA.enter_context(nc.sbuf_tensor(f"{tag}_wv{bb}", [128, 8, C], BF16))
            y_win = stA.enter_context(nc.sbuf_tensor(f"{tag}_yw{bb}", [128, 2, 8, 512], BF16))
            wsems = {}
            for nm, dst, srcap in (("q", wq_sb, wq_ap), ("k", wk_sb, wk_ap),
                                   ("v", wv_sb, wv_ap)):
                sname = f"{tag}w{nm}{bb}"
                E.emit("gpsimd", lambda b, dst=dst, srcap=srcap: b.dma_start(
                    out=dst[:, :, :], in_=srcap), inc=sname, amt=16)
                wsems[nm] = (sname, E.val(sname))
            bank_evac = [0, 0, 0, 0]
            ywin_T = [0, 0]
            kbank = [0]
            for tw in range(4):
                p = tw % 2
                w = [("T", ywin_T[p])] if ywin_T[p] else []
                ysm = f"{tag}ya{p}"
                E.emit("sync", lambda b, tw=tw, p=p, tok0=tok0: b.dma_start(
                    out=y_win[:, p, :, :],
                    in_=yin_ap[:, :, tok0 + 512 * tw: tok0 + 512 * (tw + 1)]),
                    wait=w, inc=ysm, amt=16)
                yv = E.val(ysm)

                def proj(dst, w_sb, wwait, bcol, tw=tw, p=p, yv=yv, ysm=ysm):
                    for hp in range(8):
                        bk = kbank[0] % 4
                        kbank[0] += 1
                        w = [(ysm, yv), wwait]
                        if bank_evac[bk]:
                            w.append(("V", bank_evac[bk]))

                        def mm(b, hp=hp, bk=bk, w_sb=w_sb, p=p):
                            ins = None
                            for cc in range(8):
                                ins = b.matmul(
                                    ps_s[:, 512 * bk:512 * (bk + 1)],
                                    w_sb[:, cc, 128 * hp:128 * (hp + 1)],
                                    y_win[:, p, cc, :],
                                    start=(cc == 0), stop=(cc == 7))
                            return ins
                        E.emit("tensor", mm, wait=w, inc="T")
                        tv = E.val("T")
                        E.emit("vector", lambda b, hp=hp, bk=bk, dst=dst,
                               bcol=bcol, tw=tw: b.tensor_scalar(
                                   dst[:, hp, 512 * tw:512 * (tw + 1)],
                                   ps_s[:, 512 * bk:512 * (bk + 1)],
                                   bvec[:, bcol + hp:bcol + hp + 1], None,
                                   ALU.add), wait=[("T", tv)], inc="V")
                        bank_evac[bk] = E.val("V")
                proj(Q2, wq_sb, wsems["q"], 0)
                proj(K2, wk_sb, wsems["k"], 8)
                for tt in range(4):
                    for vc in range(2):
                        bk = kbank[0] % 4
                        kbank[0] += 1
                        w = [(ysm, yv), wsems["v"]]
                        if bank_evac[bk]:
                            w.append(("V", bank_evac[bk]))

                        def mmv(b, tt=tt, vc=vc, bk=bk, p=p):
                            ins = None
                            for cc in range(8):
                                ins = b.matmul(
                                    ps_s[:, 512 * bk:512 * (bk + 1)],
                                    y_win[:, p, cc, 128 * tt:128 * (tt + 1)],
                                    wv_sb[:, cc, 512 * vc:512 * (vc + 1)],
                                    start=(cc == 0), stop=(cc == 7))
                            return ins
                        E.emit("tensor", mmv, wait=w, inc="T")
                        tv = E.val("T")
                        E.emit("vector", lambda b, tt=tt, vc=vc, bk=bk, tw=tw:
                               b.tensor_copy(
                                   Vt[:, 4 * tw + tt, 512 * vc:512 * (vc + 1)],
                                   ps_s[:, 512 * bk:512 * (bk + 1)]),
                               wait=[("T", tv)], inc="V")
                        bank_evac[bk] = E.val("V")
                ywin_T[p] = E.val("T")
        E.barrier([("V", E.val("V"))])
        with contextlib.ExitStack() as stB:
            wo_sb = stB.enter_context(nc.sbuf_tensor(f"{tag}_wo{bb}", [128, 8, C], BF16))
            comb = stB.enter_context(nc.sbuf_tensor(f"{tag}_cmb{bb}", [1, C], F32))
            hi1 = stB.enter_context(nc.sbuf_tensor(f"{tag}_hi{bb}", [1, C], BF16))
            lo1 = stB.enter_context(nc.sbuf_tensor(f"{tag}_lo{bb}", [1, C], BF16))
            out2 = stB.enter_context(nc.sbuf_tensor(f"{tag}_o2{bb}", [128, 2, C], F32))
            xres2 = stB.enter_context(nc.sbuf_tensor(f"{tag}_xr{bb}", [128, 2, C], F32))
            wosem = f"{tag}wo{bb}"
            E.emit("gpsimd", lambda b: b.dma_start(out=wo_sb[:, :, :],
                                                   in_=wo_ap), inc=wosem, amt=16)
            wo_v = E.val(wosem)
            if not bias_ready:
                bias_ready = True

                def bmm(b):
                    ins = None
                    for j in range(8):
                        for mw in range(2):
                            ins = b.matmul(ps_m[0:1, 512 * mw:512 * (mw + 1)],
                                           bvbf[:, j:j + 1],
                                           wo_sb[:, j, 512 * mw:512 * (mw + 1)],
                                           start=(j == 0), stop=(j == 7))
                    return ins
                E.emit("tensor", bmm, wait=[(wosem, wo_v)], inc="T")
                tv = E.val("T")
                E.emit("vector", lambda b: b.tensor_tensor(
                    comb[:, :], ps_m[0:1, :], bo_sb[:, :], ALU.add),
                    wait=[("T", tv)], inc="V")
                E.emit("vector", lambda b: b.tensor_copy(hi1[:, :],
                                                         comb[:, :]))
                E.emit("vector", lambda b: b.tensor_tensor(
                    lo1[:, :], comb[:, :], hi1[:, :], ALU.subtract),
                    inc="V")
                prev_merge_evac[0] = E.val("V")
                E.emit("gpsimd", lambda b: b.dma_start(
                    out=bias2[0:1, :], in_=hi1[:, :]),
                    wait=[("V", E.val("V"))], inc=f"{tag}b2", amt=16)
                E.emit("gpsimd", lambda b: b.dma_start(
                    out=bias2[1:2, :], in_=lo1[:, :]),
                    inc=f"{tag}b2", amt=16)
                E.barrier([("V", E.val("V")), (f"{tag}b2", E.val(f"{tag}b2"))],
                          engines=("tensor",))

            pend = [None]
            qst_war = [0, 0]
            reg_free = {0: (0, 0), 1: (0, 0)}
            psz_free = [0]
            pt_war = [0]
            psb_war = [0, 0]
            it = [0]

            def flush():
                if pend[0] is None or "noflush" in DBG:
                    pend[0] = None
                    return
                (qt, hp, j, par, W, s_exp) = pend[0]
                pend[0] = None
                h = 2 * hp + j
                nkt = W // 128
                for rnd in range((nkt + 7) // 8):
                    k0, k1 = 8 * rnd, min(8 * rnd + 8, nkt)

                    def tr(b, k0=k0, k1=k1, par=par):
                        ins = None
                        for k in range(k0, k1):
                            ins = b.transpose(
                                ps_t[:, 128 * (k - k0):128 * (k - k0 + 1)],
                                P_sb[:, par, 128 * k:128 * (k + 1)],
                                ident_sb[:, :])
                        return ins
                    w = [("S", s_exp)]
                    if pt_war[0]:
                        w.append(pt_war[0])
                    E.emit("tensor", tr, wait=w, inc="T")
                    tv = E.val("T")
                    E.emit("scalar", lambda b, k0=k0, k1=k1: b.activation(
                        PT_sb[:, 128 * k0:128 * k1],
                        ps_t[:, 0:128 * (k1 - k0)], AF.Copy),
                        wait=[("T", tv)], inc="Sp")
                    pt_war[0] = ("Sp", E.val("Sp"))
                psb_war[par] = E.val("T")
                w = [pt_war[0]]
                if psz_free[0]:
                    w.append(("V", psz_free[0]))

                def pv(b, nkt=nkt, h=h):
                    ins = None
                    for k in range(nkt):
                        ins = b.matmul(ps_z[:, 0:DH],
                                       PT_sb[:, 128 * k:128 * (k + 1)],
                                       Vt[:, k, DH * h:DH * h + DH],
                                       start=(k == 0), stop=(k == nkt - 1))
                    return ins
                E.emit("tensor", pv, wait=w, inc="T")
                tv = E.val("T")
                E.emit("vector", lambda b, j=j, par=par: b.tensor_scalar(
                    zn2[:, DH * j:DH * (j + 1)], ps_z[:, 0:DH],
                    ast[:, par, 1:2], None, ALU.mult),
                    wait=[("T", tv)], inc="V")
                psz_free[0] = E.val("V")
                if j == 1:
                    w = [("V", psz_free[0])]
                    if pt_war[0]:
                        w.append(pt_war[0])
                    E.emit("tensor", lambda b: b.transpose(
                        ps_t[:, 0:128], zn2[:, :], ident_sb[:, :]),
                        wait=w, inc="T")
                    tv = E.val("T")
                    E.emit("vector", lambda b: b.tensor_copy(
                        zT2[:, :], ps_t[:, 0:128]), wait=[("T", tv)], inc="V")
                    pt_war[0] = ("V", E.val("V"))
                    zv = E.val("V")
                    w = [("V", zv), (wosem, wo_v)]
                    if hp == 0 and prev_merge_evac[0]:
                        w.append(("V", prev_merge_evac[0]))

                    def mrg(b, hp=hp):
                        ins = None
                        for mw in range(2):
                            ins = b.matmul(ps_m[:, 512 * mw:512 * (mw + 1)],
                                           zT2[:, :],
                                           wo_sb[:, hp, 512 * mw:512 * (mw + 1)],
                                           start=(hp == 0), stop=False)
                        return ins
                    E.emit("tensor", mrg, wait=w, inc="T")

            for qt in range(dbg_nqt):
                W = 128 * (qt + 1)
                two_chunk = W > 1024
                for hp in range(8):
                    for j in range(2):
                        h = 2 * hp + j
                        it[0] += 1
                        par = it[0] % 2
                        if two_chunk:
                            regs = [0, 1]
                            roff = 0
                        else:
                            regs = [it[0] % 2]
                            roff = 1024 * (it[0] % 2)
                        w = []
                        for r in regs:
                            if reg_free[r][0]:
                                w.append(("S", reg_free[r][0]))
                            if reg_free[r][1]:
                                w.append(("V", reg_free[r][1]))

                        alibi = (layer == 0) and ("noalibi" not in DBG)
                        wq = []
                        if qst_war[j]:
                            wq.append(("T", qst_war[j]))
                        E.emit("scalar", lambda b, qt=qt, hp=hp, j=j:
                               b.activation(
                                   qst[64 * j:64 * (j + 1), j, :],
                                   Q2[64 * j:64 * (j + 1), hp,
                                      128 * qt:128 * (qt + 1)],
                                   AF.Copy, scale=0.125),
                               wait=wq, inc="S")
                        w.append(("S", E.val("S")))

                        def smm(b, hp=hp, j=j, W=W, roff=roff, h=h,
                                alibi=alibi):
                            ins = None
                            for wnd in range(0, W, 512):
                                wid = min(512, W - wnd)
                                ins = b.matmul(
                                    ps_s[:, roff + wnd:roff + wnd + wid],
                                    qst[:, j, :],
                                    K2[:, hp, wnd:wnd + wid],
                                    start=True, stop=not alibi)
                                if alibi:
                                    ins = b.matmul(
                                        ps_s[:, roff + wnd:roff + wnd + wid],
                                        sel_sb[:, 128 * h:128 * (h + 1)],
                                        b48_sb[:, wnd:wnd + wid],
                                        start=False, stop=True)
                            return ins
                        if "nosmm" in DBG:
                            E.emit("tensor", lambda b, roff=roff, W=W: b.matmul(
                                ps_s[:, roff:roff + W], ident_sb[:, 0:128],
                                K2[:, 0, 0:W], start=True, stop=True),
                                wait=w, inc="T")
                        else:
                            E.emit("tensor", smm, wait=w, inc="T")
                        t_s = E.val("T")
                        qst_war[j] = t_s
                        E.emit("vector", lambda b, roff=roff, W=W:
                               b.tensor_tensor(
                                   ps_s[:, roff + W - 128:roff + W],
                                   ps_s[:, roff + W - 128:roff + W],
                                   dmask[:, :], ALU.add),
                               wait=[("T", t_s)], inc="V")
                        if alibi:
                            E.emit("vector", lambda b, roff=roff, W=W,
                                   par=par: b.tensor_reduce(
                                       ast[:, par, 3:4],
                                       ps_s[:, roff:roff + W],
                                       AX.X, ALU.max, negate=True), inc="V")
                        bcol = 3 if alibi else None
                        flush()
                        v_bias = E.val("V")
                        w = [("V", v_bias)]
                        if psb_war[par]:
                            w.append(("T", psb_war[par]))
                        if "noexp" in DBG:
                            E.emit("scalar", lambda b, par=par, roff=roff, W=W:
                                   b.activation(
                                       P_sb[:, par, 0:W], ps_s[:, roff:roff + W],
                                       AF.Copy), wait=w, inc="S")
                        else:
                            E.emit("scalar", lambda b, par=par, roff=roff, W=W,
                                   bcol=bcol: b.activation(
                                       P_sb[:, par, 0:W], ps_s[:, roff:roff + W],
                                       AF.Exp,
                                       **({} if bcol is None else
                                          {"bias": ast[:, par, bcol:bcol + 1]}),
                                       scale=1.0,
                                       accum_out=ast[:, par, 0:1]),
                                   wait=w, inc="S")
                        s_exp = E.val("S")
                        reg_done_v = E.val("V")
                        for r in regs:
                            reg_free[r] = (s_exp, reg_done_v)
                        E.emit("vector", lambda b, par=par: b.reciprocal(
                            ast[:, par, 1:2], ast[:, par, 0:1]),
                            wait=[("S", s_exp)], inc="V")
                        if "dump" in DBG and it[0] == 1 and bb == 0:
                            E.emit("vector", lambda b, par=par: b.tensor_copy(
                                E.dbgP[:, :], P_sb[:, par, 0:128]),
                                wait=[("S", s_exp)], inc="V")
                            E.emit("vector", lambda b, par=par: b.tensor_copy(
                                E.dbgA[:, :], ast[:, par, :]), inc="V")
                            dv = E.val("V")
                            E.emit("sync", lambda b: b.dma_start(
                                out=E.dbgP_ap, in_=E.dbgP[:, :]),
                                wait=[("V", dv)], inc="Dc", amt=16)
                            E.emit("sync", lambda b: b.dma_start(
                                out=E.dbgA_ap, in_=E.dbgA[:, :]),
                                inc="Dc", amt=16)
                        pend[0] = (qt, hp, j, par, W, s_exp)
                flush()
                if "notail" in DBG:
                    continue

                def bmm2(b):
                    ins = None
                    for mw in range(2):
                        ins = b.matmul(ps_m[:, 512 * mw:512 * (mw + 1)],
                                       ones2[:, :],
                                       bias2[0:2, 512 * mw:512 * (mw + 1)],
                                       start=False, stop=True)
                    return ins
                E.emit("tensor", bmm2, inc="T")
                t_m = E.val("T")
                p = qt % 2
                row0 = tok0 + 128 * qt
                orow0 = row0 if out_row_fn is None else out_row_fn(bb, qt)
                w = [("V", last_xload[p])] if last_xload[p] else []
                E.emit("sync", lambda b, row0=row0, p=p: b.dma_start(
                    out=xres2[:, p, :], in_=xin_ap[row0:row0 + 128, :]),
                    wait=w, inc=xsem[p], amt=16)
                dv = E.val(xsem[p])
                w = [("T", t_m), (xsem[p], dv)]
                if last_store[p]:
                    w.append((osem[p], last_store[p]))
                E.emit("vector", lambda b, p=p: b.tensor_tensor(
                    out2[:, p, :], ps_m[:, :], xres2[:, p, :], ALU.add),
                    wait=w, inc="V")
                prev_merge_evac[0] = E.val("V")
                last_xload[p] = prev_merge_evac[0]
                E.emit("sync", lambda b, orow0=orow0, p=p: b.dma_start(
                    out=xout_ap[orow0:orow0 + 128, :], in_=out2[:, p, :]),
                    wait=[("V", prev_merge_evac[0])], inc=osem[p], amt=16)
                last_store[p] = E.val(osem[p])
    st.close()
    return [(osem[0], E.val(osem[0])), (osem[1], E.val(osem[1]))]



def emit_lstm_block(E, xin_ap, yin_ap, out_ap, w_aps, b_aps, ctx):
    nc = E.nc
    S2 = SEQ2
    st = contextlib.ExitStack()
    wsb = {}
    for g in ("f", "i", "c", "o"):
        wsb[g] = (st.enter_context(nc.sbuf_tensor(f"ls_w{g}", [128, 8, C], BF16)),
                  st.enter_context(nc.sbuf_tensor(f"ls_u{g}", [128, 8, C], BF16)))
    bcols = st.enter_context(nc.sbuf_tensor("ls_bc", [128, 24], F32))
    inccol = st.enter_context(nc.sbuf_tensor("ls_inc", [128, 8], F32))
    bo2 = st.enter_context(nc.sbuf_tensor("ls_bo2", [2, C], BF16))
    ones2 = st.enter_context(nc.sbuf_tensor("ls_on2", [2, 128], BF16))
    warm = st.enter_context(nc.sbuf_tensor("ls_warm", [128, 256], F32))
    c_fm = st.enter_context(nc.sbuf_tensor("ls_c", [128, 8, S2], F32))
    c_bf = st.enter_context(nc.sbuf_tensor("ls_cb", [128, 8, S2], BF16))
    f_sb = st.enter_context(nc.sbuf_tensor("ls_f", [128, 8, S2], F32))
    i_sb = st.enter_context(nc.sbuf_tensor("ls_i", [128, 8, S2], F32))
    g_sb = st.enter_context(nc.sbuf_tensor("ls_g", [128, 8, S2], F32))
    y2 = st.enter_context(nc.sbuf_tensor("ls_y", [128, 2, 8, S2], BF16))
    cfuo = st.enter_context(nc.sbuf_tensor("ls_cfuo", [128, 2, C], F32))
    xr2 = st.enter_context(nc.sbuf_tensor("ls_xr", [128, 2, C], F32))
    o2 = st.enter_context(nc.sbuf_tensor("ls_o2", [128, 2, C], F32))
    ps_g = st.enter_context(nc.psum_tensor("ls_psg", [128, 2, 512], F32))
    ps_h = st.enter_context(nc.psum_tensor("ls_psh", [128, 2, 512], F32))

    emit_act_warmup(E, [AF.Sigmoid, AF.Tanh], warm)
    for g in ("f", "i", "c", "o"):
        for k in range(2):
            E.emit("gpsimd", lambda b, g=g, k=k: b.dma_start(
                out=wsb[g][k][:, :, :], in_=w_aps[g][k]), inc=f"lw{g}{k}", amt=16)
    E.emit("sync", lambda b: b.dma_start(out=bcols[:, :], in_=b_aps["bcols"]),
           inc="Dc", amt=16)
    E.emit("sync", lambda b: b.dma_start(out=inccol[:, :], in_=b_aps["inc"]),
           inc="Dc", amt=16)
    E.emit("sync", lambda b: b.dma_start(out=ones2[:, :],
                                         in_=E.con["ones2"].ap()), inc="Dc", amt=16)
    with (nc.sbuf_tensor("ls_bo", [1, C], F32) as bo_sb,
          nc.sbuf_tensor("ls_hi", [1, C], BF16) as hi1,
          nc.sbuf_tensor("ls_lo", [1, C], BF16) as lo1):
        E.emit("sync", lambda b: b.dma_start(out=bo_sb[:, :], in_=b_aps["bo"]),
               inc="Dc", amt=16)
        dc = E.val("Dc")
        E.emit("vector", lambda b: b.memset(warm[:, :], 0.0),
               wait=[("Dc", dc), ("S", E.val("S"))])
        E.emit("vector", lambda b: b.tensor_copy(hi1[:, :], bo_sb[:, :]),
               inc="V")
        E.emit("vector", lambda b: b.tensor_tensor(
            lo1[:, :], bo_sb[:, :], hi1[:, :], ALU.subtract), inc="V")
        bv = E.val("V")
        E.emit("gpsimd", lambda b: b.dma_start(out=bo2[0:1, :], in_=hi1[:, :]),
               wait=[("V", bv)], inc="lsb2", amt=16)
        E.emit("gpsimd", lambda b: b.dma_start(out=bo2[1:2, :], in_=lo1[:, :]),
               inc="lsb2", amt=16)
        E.barrier([("lsb2", E.val("lsb2"))], engines=("vector", "scalar"))
    dc = E.val("Dc")
    wws = [(f"lw{g}{k}", E.val(f"lw{g}{k}")) for g in "fico" for k in range(2)]
    zz = warm
    for cc in range(8):
        E.emit("vector", lambda b, cc=cc: b.tensor_scalar(
            c_fm[:, cc, :], zz[:, :], inccol[:, cc:cc + 1], None, ALU.add))
    E.emit("vector", lambda b: b.tensor_copy(
        c_bf[:, :, :].rearrange("p a s -> p (a s)"),
        c_fm[:, :, :].rearrange("p a s -> p (a s)")), inc="V")
    E.barrier([("V", E.val("V")), ("Dc", dc), ("lsb2", E.val("lsb2"))] + wws)

    nstep = NSTEP
    ylast = [0, 0]
    for t in range(nstep):
        p = t % 2
        ysm = f"lsy{p}"
        w = [("T", ylast[p])] if ylast[p] else []
        E.emit("sync", lambda b, t=t, p=p: b.dma_start(
            out=y2[:, p, :, :], in_=yin_ap[:, :, S2 * t:S2 * (t + 1)]),
            wait=w, inc=ysm, amt=16)
        yv = E.val(ysm)
        gates = (("f", f_sb, AF.Sigmoid, 0), ("i", i_sb, AF.Sigmoid, 8),
                 ("c", g_sb, AF.Tanh, 16))
        bank_free = [0, 0]
        for (gn, dst, func, bcol) in gates:
            for mt in range(8):
                bk = mt % 2
                wl = [(ysm, yv)]
                if bank_free[bk]:
                    wl.append(("S", bank_free[bk]))

                def mm(b, gn=gn, mt=mt, bk=bk, p=p):
                    ins = None
                    for cc in range(8):
                        b.matmul(ps_g[:, bk, 0:S2],
                                 wsb[gn][0][:, cc, 128 * mt:128 * (mt + 1)],
                                 y2[:, p, cc, :], start=(cc == 0), stop=False)
                    for cc in range(8):
                        ins = b.matmul(
                            ps_g[:, bk, 0:S2],
                            wsb[gn][1][:, cc, 128 * mt:128 * (mt + 1)],
                            c_bf[:, cc, :], start=False, stop=(cc == 7))
                    return ins
                E.emit("tensor", mm, wait=wl, inc="T")
                tv = E.val("T")
                E.emit("scalar", lambda b, gn=gn, mt=mt, bk=bk, dst=dst,
                       func=func, bcol=bcol: b.activation(
                           dst[:, mt, :], ps_g[:, bk, 0:S2], func,
                           bias=bcols[:, bcol + mt:bcol + mt + 1]),
                       wait=[("T", tv)], inc="S")
                bank_free[bk] = E.val("S")
        ylast[p] = E.val("T")
        sg = E.val("S")
        fl = f_sb[:, :, :].rearrange("p a s -> p (a s)")
        il = i_sb[:, :, :].rearrange("p a s -> p (a s)")
        gl = g_sb[:, :, :].rearrange("p a s -> p (a s)")
        cl = c_fm[:, :, :].rearrange("p a s -> p (a s)")
        cbl = c_bf[:, :, :].rearrange("p a s -> p (a s)")
        E.emit("vector", lambda b, fl=fl, cl=cl: b.tensor_tensor(
            fl, fl, cl, ALU.mult), wait=[("S", sg)])
        E.emit("vector", lambda b, il=il, gl=gl: b.tensor_tensor(
            il, il, gl, ALU.mult))
        E.emit("vector", lambda b, cl=cl, fl=fl, il=il: b.tensor_tensor(
            cl, fl, il, ALU.add))
        E.emit("vector", lambda b, cbl=cbl, cl=cl: b.tensor_copy(cbl, cl),
               inc="V")
        cv = E.val("V")
        E.emit("tensor", lambda b: None, wait=[("V", cv)])
    if "noh" in DBG:
        st.close()
        return [("lsy0", E.val("lsy0")), ("lsy1", E.val("lsy1"))]
    cfin = E.val("V")
    for st2 in range(2):
        pst = ps_h if st2 == 0 else ps_g

        def mmc(b, st2=st2, pst=pst):
            ins = None
            for mw in range(2):
                for cc in range(8):
                    b.matmul(pst[:, mw, :],
                             c_bf[:, cc, 128 * st2:128 * (st2 + 1)],
                             wsb["o"][1][:, cc, 512 * mw:512 * (mw + 1)],
                             start=(cc == 0), stop=False)
                ins = b.matmul(pst[:, mw, :], ones2[:, :],
                               bo2[0:2, 512 * mw:512 * (mw + 1)],
                               start=False, stop=True)
            return ins
        E.emit("tensor", mmc, wait=[("V", cfin)], inc="T")
        tv = E.val("T")
        E.emit("vector", lambda b, st2=st2, pst=pst: b.tensor_copy(
            cfuo[:, st2, :], pst[:, :, :].rearrange("p a s -> p (a s)")),
            wait=[("T", tv)], inc="V")
    cfv = E.val("V")
    E.barrier([("V", cfv)], engines=("tensor",))
    if "nohh" in DBG:
        st.close()
        return [("lsy0", E.val("lsy0")), ("lsy1", E.val("lsy1"))]
    xlast = [0, 0]
    olast = [0, 0]
    hlast = [0, 0]
    for t in range(nstep):
        p = t % 2
        ysm = f"lsy{p}"
        w = [("T", ylast[p])] if ylast[p] else []
        E.emit("sync", lambda b, t=t, p=p: b.dma_start(
            out=y2[:, p, :, :], in_=yin_ap[:, :, S2 * t:S2 * (t + 1)]),
            wait=w, inc=ysm, amt=16)
        yv = E.val(ysm)
        for st2 in range(2):
            pp = st2
            row0 = S2 * t + 128 * st2
            sp = (t * 2 + st2) % 2
            wl = [(ysm, yv)]
            if hlast[pp]:
                wl.append(("S", hlast[pp]))

            def mmh(b, st2=st2, p=p, pp=pp):
                ins = None
                for mw in range(2):
                    for cc in range(8):
                        ins = b.matmul(
                            ps_h[:, pp, :] if mw == 0 else ps_h[:, pp, :],
                            y2[:, p, cc, 128 * st2:128 * (st2 + 1)],
                            wsb["o"][0][:, cc, 512 * mw:512 * (mw + 1)],
                            start=(cc == 0), stop=(cc == 7))
                return ins
            def mmh2(b, st2=st2, p=p, pp=pp):
                ins = None
                for cc in range(8):
                    b.matmul(ps_h[:, pp, :],
                             y2[:, p, cc, 128 * st2:128 * (st2 + 1)],
                             wsb["o"][0][:, cc, 0:512],
                             start=(cc == 0), stop=(cc == 7))
                for cc in range(8):
                    ins = b.matmul(ps_g[:, pp, :],
                                   y2[:, p, cc, 128 * st2:128 * (st2 + 1)],
                                   wsb["o"][0][:, cc, 512:1024],
                                   start=(cc == 0), stop=(cc == 7))
                return ins
            E.emit("tensor", mmh2, wait=wl, inc="T")
            tv = E.val("T")
            ylast[p] = tv
            E.emit("vector", lambda b, st2=st2, pp=pp: b.tensor_tensor(
                ps_h[:, pp, :], ps_h[:, pp, :], cfuo[:, st2, 0:512], ALU.add),
                wait=[("T", tv)])
            E.emit("vector", lambda b, st2=st2, pp=pp: b.tensor_tensor(
                ps_g[:, pp, :], ps_g[:, pp, :], cfuo[:, st2, 512:1024],
                ALU.add), inc="V")
            av = E.val("V")
            xsm, osm = f"lsx{sp}", f"lso{sp}"
            wx = [("V", xlast[sp])] if xlast[sp] else []
            E.emit("sync", lambda b, row0=row0, sp=sp: b.dma_start(
                out=xr2[:, sp, :], in_=xin_ap[row0:row0 + 128, :]),
                wait=wx, inc=xsm, amt=16)
            xv = E.val(xsm)
            E.emit("scalar", lambda b, sp=sp, pp=pp: b.activation(
                o2[:, sp, 0:512], ps_h[:, pp, :], AF.Tanh),
                wait=[("V", av)], inc="S")
            E.emit("scalar", lambda b, sp=sp, pp=pp: b.activation(
                o2[:, sp, 512:1024], ps_g[:, pp, :], AF.Tanh), inc="S")
            sv = E.val("S")
            hlast[pp] = sv
            wl = [("S", sv), (xsm, xv)]
            if olast[sp]:
                wl.append((osm, olast[sp]))
            E.emit("vector", lambda b, sp=sp: b.tensor_tensor(
                o2[:, sp, :], o2[:, sp, :], xr2[:, sp, :], ALU.add),
                wait=wl, inc="V")
            ov = E.val("V")
            xlast[sp] = ov
            E.emit("sync", lambda b, row0=row0, sp=sp: b.dma_start(
                out=out_ap[row0:row0 + 128, :], in_=o2[:, sp, :]),
                wait=[("V", ov)], inc=osm, amt=16)
            olast[sp] = E.val(osm)
    st.close()
    return [("lso0", E.val("lso0")), ("lso1", E.val("lso1"))]


def build_lstm_program():
    nc = bass.Bass("TRN2", target_bir_lowering=False)
    xin = nc.dram_tensor("xin", [NSTEP * SEQ2, C], F32, kind="ExternalInput").ap()
    yin = nc.dram_tensor("yin", [128, 8, NSTEP * SEQ2], BF16,
                         kind="ExternalInput").ap()
    w_aps = {}
    for g in ("f", "i", "c", "o"):
        w = nc.dram_tensor(f"w{g}", [128, 8, C], F32, kind="ExternalInput").ap()
        u = nc.dram_tensor(f"u{g}", [128, 8, C], F32, kind="ExternalInput").ap()
        w_aps[g] = (w, u)
    b_aps = {
        "bcols": nc.dram_tensor("bcols", [128, 24], F32, kind="ExternalInput").ap(),
        "bo": nc.dram_tensor("bo", [1, C], F32, kind="ExternalInput").ap(),
        "inc": nc.dram_tensor("incc", [128, 8], F32, kind="ExternalInput").ap(),
    }
    out = nc.dram_tensor("out", [NSTEP * SEQ2, C], F32, kind="ExternalOutput").ap()
    E = Emitter(nc)
    E.con = make_consts(nc)
    done = emit_lstm_block(E, xin, yin, out, w_aps, b_aps, {})
    E.emit("sync", lambda b: None, wait=done)
    E.finalize()
    return nc


def prep_lstm_weights(wf, uf, bf, wi, ui, bi, wc, uc, bc, wo, uo, bo, in_c):
    w_aps = {}
    for g, (w, u) in (("f", (wf, uf)), ("i", (wi, ui)), ("c", (wc, uc)),
                      ("o", (wo, uo))):
        w_aps[f"w{g}"] = _chunk_lhst(np.ascontiguousarray(np.asarray(w).T))
        w_aps[f"u{g}"] = _chunk_lhst(np.ascontiguousarray(np.asarray(u).T))
    bcols = np.concatenate([_col_t(np.asarray(bf)), _col_t(np.asarray(bi)),
                            _col_t(np.asarray(bc))], axis=1)
    bo_r = np.ascontiguousarray(np.asarray(bo).reshape(1, C))
    incc = _col_t(np.asarray(in_c))
    return w_aps, bcols, bo_r, incc



def _chunk_lhst(a, nch=8):
    n = a.shape[1]
    return np.ascontiguousarray(a.reshape(nch, 128, n).transpose(1, 0, 2))


def _col_t(v):
    k = v.shape[0] // 128
    return np.ascontiguousarray(v.reshape(k, 128).T)


def prep_attn0_weights(w_attn, b_attn, w_proj, b_proj):
    wqkv = _chunk_lhst(np.asarray(w_attn))
    bvec = _col_t(np.asarray(b_attn))
    wo = _chunk_lhst(np.asarray(w_proj))
    bo = np.ascontiguousarray(np.asarray(b_proj).reshape(1, C))
    return wqkv, bvec, wo, bo


def prep_attn1_weights(W_Q, b_Q, W_K, b_K, W_V, b_V, W_O, b_O):
    wq = _chunk_lhst(np.ascontiguousarray(
        np.asarray(W_Q).transpose(1, 0, 2).reshape(C, C)))
    wk = _chunk_lhst(np.ascontiguousarray(
        np.asarray(W_K).transpose(1, 0, 2).reshape(C, C)))
    wv = _chunk_lhst(np.ascontiguousarray(
        np.asarray(W_V).transpose(1, 0, 2).reshape(C, C)))
    wo = _chunk_lhst(np.ascontiguousarray(np.asarray(W_O).reshape(C, C)))
    bvec = np.concatenate([_col_t(np.asarray(b_Q).reshape(-1)),
                           _col_t(np.asarray(b_K).reshape(-1)),
                           _col_t(np.asarray(b_V).reshape(-1))], axis=1)
    bo = np.ascontiguousarray(np.asarray(b_O).reshape(1, C))
    return wq, wk, wv, wo, bvec, bo


def prep_ffn_weights(w_gate, w_up, w_down):
    wg = _chunk_lhst(np.ascontiguousarray(w_gate.T))
    wu = _chunk_lhst(np.ascontiguousarray(w_up.T))
    wd = _chunk_lhst(np.ascontiguousarray(w_down.T), 24)
    return wg, wu, wd



def build_ffn_program(in_ln=True, out_ln=False):
    nc = bass.Bass("TRN2", target_bir_lowering=False)
    xin = nc.dram_tensor("xin", [TPC, C], F32, kind="ExternalInput").ap()
    wg = nc.dram_tensor("wg", [128, 8, FH], I32, kind="ExternalInput").ap()
    wu = nc.dram_tensor("wu", [128, 8, FH], I32, kind="ExternalInput").ap()
    wd = nc.dram_tensor("wd", [128, 24, C], I32, kind="ExternalInput").ap()
    xout = nc.dram_tensor("xout", [TPC, C], F32, kind="ExternalOutput").ap()
    E = Emitter(nc)
    E.con = make_consts(nc)
    if in_ln:
        ydram = nc.dram_tensor("ffn_ydram", [128, 8, TPC], BF16, kind="Internal")
        ln_done = emit_ln_pass(
            E, [xin[128 * i:128 * (i + 1), :] for i in range(TPC // 128)],
            ydram.ap(), "ln0")
        E.barrier(ln_done)
    else:
        ydram = nc.dram_tensor("yin", [128, 8, TPC], BF16, kind="ExternalInput")
    ffn_done = emit_ffn_block(E, xin, ydram.ap(), xout, wg, wu, wd, {})
    if out_ln:
        yout = nc.dram_tensor("yout", [128, 8, TPC], BF16,
                              kind="ExternalOutput").ap()
        E.barrier(ffn_done)
        ln2 = emit_ln_pass(
            E, [xout[128 * i:128 * (i + 1), :] for i in range(TPC // 128)],
            yout, "ln2")
        E.emit("sync", lambda b: None, wait=ln2 + ffn_done)
    else:
        E.emit("sync", lambda b: None, wait=ffn_done)
    E.finalize()
    return nc


def build_attn_program(layer, out_ln=True, in_ln=True, dbg_nqt=16, dbg_nb=BPC):
    nc = bass.Bass("TRN2", target_bir_lowering=False)
    xin = nc.dram_tensor("xin", [TPC, C], F32, kind="ExternalInput").ap()
    if layer == 0:
        wqkv = nc.dram_tensor("wqkv", [128, 8, 3 * C], F32,
                              kind="ExternalInput").ap()
        wq_ap, wk_ap, wv_ap = (wqkv[:, :, 0:C], wqkv[:, :, C:2 * C],
                               wqkv[:, :, 2 * C:3 * C])
    else:
        wq_ap = nc.dram_tensor("wq", [128, 8, C], F32, kind="ExternalInput").ap()
        wk_ap = nc.dram_tensor("wk", [128, 8, C], F32, kind="ExternalInput").ap()
        wv_ap = nc.dram_tensor("wv", [128, 8, C], F32, kind="ExternalInput").ap()
    wo = nc.dram_tensor("wo", [128, 8, C], F32, kind="ExternalInput").ap()
    bvec = nc.dram_tensor("bvec", [128, 24], F32, kind="ExternalInput").ap()
    bo = nc.dram_tensor("bo", [1, C], F32, kind="ExternalInput").ap()
    xout = nc.dram_tensor("xout", [TPC, C], F32, kind="ExternalOutput").ap()
    yout = nc.dram_tensor("yout", [128, 8, TPC], BF16,
                          kind="ExternalOutput").ap()
    if in_ln:
        ydram = nc.dram_tensor("att_ydram", [128, 8, TPC], BF16, kind="Internal")
    else:
        ydram = nc.dram_tensor("yin", [128, 8, TPC], BF16, kind="ExternalInput")
    E = Emitter(nc)
    E.con = make_consts(nc)
    E.dbgP_ap = nc.dram_tensor("dbgP", [128, 128], BF16, kind="ExternalOutput").ap()
    E.dbgA_ap = nc.dram_tensor("dbgA", [128, 8], F32, kind="ExternalOutput").ap()
    if layer == 0:
        b48, selc = make_alibi_consts(nc)
        E.con["alibi_b48"] = b48
        E.con["alibi_sel"] = selc
    if in_ln:
        ln_done = emit_ln_pass(
            E, [xin[128 * i:128 * (i + 1), :] for i in range(TPC // 128)],
            ydram.ap(), "lni")
        E.barrier(ln_done)
    at_done = emit_attn_block(E, xin, ydram.ap(), xout, wq_ap, wk_ap, wv_ap,
                              wo, bvec, bo, layer, "att", dbg_nqt=dbg_nqt,
                              dbg_nb=dbg_nb)
    E.barrier(at_done)
    if out_ln:
        ln2 = emit_ln_pass(
            E, [xout[128 * i:128 * (i + 1), :] for i in range(TPC // 128)],
            yout, "lno")
        E.emit("sync", lambda b: None, wait=ln2 + at_done)
    else:
        E.emit("sync", lambda b: None, wait=at_done)
    E.finalize()
    return nc


def _core_inmaps(inputs):
    x = np.ascontiguousarray(np.asarray(inputs["L0_attn_x"], np.float32))
    wqkv, bvec0, wo0, bo0 = prep_attn0_weights(
        inputs["L0_attn_w_attn"], inputs["L0_attn_b_attn"],
        inputs["L0_attn_w_proj"], inputs["L0_attn_b_proj"])
    wg, wu, wd = prep_ffn_weights(
        np.asarray(inputs["L0_ffn_w_gate"], np.int32),
        np.asarray(inputs["L0_ffn_w_up"], np.int32),
        np.asarray(inputs["L0_ffn_w_down"], np.int32))
    wq1, wk1, wv1, wo1, bvec1, bo1 = prep_attn1_weights(
        inputs["L1_attn_W_Q"], inputs["L1_attn_b_Q"],
        inputs["L1_attn_W_K"], inputs["L1_attn_b_K"],
        inputs["L1_attn_W_V"], inputs["L1_attn_b_V"],
        inputs["L1_attn_W_O"], inputs["L1_attn_b_O"])
    w_aps, bcols, bo_r, incc = prep_lstm_weights(
        inputs["L1_ffn_wf"], inputs["L1_ffn_uf"], inputs["L1_ffn_bf"],
        inputs["L1_ffn_wi"], inputs["L1_ffn_ui"], inputs["L1_ffn_bi"],
        inputs["L1_ffn_wc"], inputs["L1_ffn_uc"], inputs["L1_ffn_bc"],
        inputs["L1_ffn_wo"], inputs["L1_ffn_uo"], inputs["L1_ffn_bo"],
        inputs["L1_ffn_in_c"])
    return dict(x=x, wqkv=wqkv, bvec0=bvec0, wo0=wo0, bo0=bo0,
                wg=wg, wu=wu, wd=wd, wq1=wq1, wk1=wk1, wv1=wv1, wo1=wo1,
                bvec1=bvec1, bo1=bo1, lstm_w=w_aps, bcols=bcols, bo_r=bo_r,
                incc=incc)


def build_fused_program():
    nc = bass.Bass("TRN2", target_bir_lowering=False)
    xin = nc.dram_tensor("xin", [TPC, C], F32, kind="ExternalInput").ap()
    wqkv = nc.dram_tensor("wqkv", [128, 8, 3 * C], F32, kind="ExternalInput").ap()
    wo0 = nc.dram_tensor("wo0", [128, 8, C], F32, kind="ExternalInput").ap()
    bvec0 = nc.dram_tensor("bvec0", [128, 24], F32, kind="ExternalInput").ap()
    bo0 = nc.dram_tensor("bo0", [1, C], F32, kind="ExternalInput").ap()
    wg = nc.dram_tensor("wg", [128, 8, FH], I32, kind="ExternalInput").ap()
    wu = nc.dram_tensor("wu", [128, 8, FH], I32, kind="ExternalInput").ap()
    wd = nc.dram_tensor("wd", [128, 24, C], I32, kind="ExternalInput").ap()
    wq1 = nc.dram_tensor("wq1", [128, 8, C], F32, kind="ExternalInput").ap()
    wk1 = nc.dram_tensor("wk1", [128, 8, C], F32, kind="ExternalInput").ap()
    wv1 = nc.dram_tensor("wv1", [128, 8, C], F32, kind="ExternalInput").ap()
    wo1 = nc.dram_tensor("wo1", [128, 8, C], F32, kind="ExternalInput").ap()
    bvec1 = nc.dram_tensor("bvec1", [128, 24], F32, kind="ExternalInput").ap()
    bo1 = nc.dram_tensor("bo1", [1, C], F32, kind="ExternalInput").ap()
    w_aps = {}
    for g in ("f", "i", "c", "o"):
        w = nc.dram_tensor(f"w{g}", [128, 8, C], F32, kind="ExternalInput").ap()
        u = nc.dram_tensor(f"u{g}", [128, 8, C], F32, kind="ExternalInput").ap()
        w_aps[g] = (w, u)
    b_aps = {
        "bcols": nc.dram_tensor("bcols", [128, 24], F32, kind="ExternalInput").ap(),
        "bo": nc.dram_tensor("bo", [1, C], F32, kind="ExternalInput").ap(),
        "inc": nc.dram_tensor("incc", [128, 8], F32, kind="ExternalInput").ap(),
    }
    out = nc.dram_tensor("out", [NSTEP * SEQ2, C], F32, kind="ExternalOutput").ap()
    x1 = nc.dram_tensor("x1s", [TPC, C], F32, kind="Internal").ap()
    x2 = nc.dram_tensor("x2s", [TPC, C], F32, kind="Internal").ap()
    x3 = nc.dram_tensor("x3s", [TPC, C], F32, kind="Internal")
    x4 = nc.dram_tensor("x4s", [TPC, C], F32, kind="Internal")
    ya = nc.dram_tensor("yas", [128, 8, TPC], BF16, kind="Internal").ap()
    yb = nc.dram_tensor("ybs", [128, 8, TPC], BF16, kind="Internal").ap()

    E = Emitter(nc)
    E.con = make_consts(nc)
    b48, selc = make_alibi_consts(nc)
    E.con["alibi_b48"] = b48
    E.con["alibi_sel"] = selc

    tiles = lambda ap: [ap[128 * i:128 * (i + 1), :] for i in range(TPC // 128)]
    d = emit_ln_pass(E, tiles(xin), ya, "ln0")
    E.barrier(d)
    d = emit_attn_block(E, xin, ya, x1, wqkv[:, :, 0:C], wqkv[:, :, C:2 * C],
                        wqkv[:, :, 2 * C:3 * C], wo0, bvec0, bo0, 0, "a0")
    E.barrier(d)
    d = emit_ln_pass(E, tiles(x1), yb, "ln1")
    E.barrier(d)
    d = emit_ffn_block(E, x1, yb, x2, wg, wu, wd, {})
    E.barrier(d)
    d = emit_ln_pass(E, tiles(x2), ya, "ln2")
    E.barrier(d)

    def a2a_row(bb, qt):
        t0 = 128 * qt
        return (t0 // SEQ2) * 2 * SEQ2 + bb * SEQ2 + (t0 % SEQ2)
    d = emit_attn_block(E, x2, ya, x3.ap(), wq1, wk1, wv1, wo1, bvec1, bo1,
                        1, "a1", out_row_fn=a2a_row)
    E.barrier(d)
    E.emit("gpsimd", lambda b: b.collective_compute(
        "AllToAll", ALU.bypass, replica_groups=[list(range(NCORES))],
        ins=[x3.ap().opt()], outs=[x4.ap().opt()]), inc="CC", amt=1)
    E.barrier([("CC", 1)])
    d = emit_ln_pass(E, tiles(x4.ap()), yb, "ln3")
    E.barrier(d)
    d = emit_lstm_block(E, x4.ap(), yb, out, w_aps, b_aps, {})
    E.emit("sync", lambda b: None, wait=d)
    E.finalize()
    return nc


def kernel_fused(**inputs):
    from concourse.aot_env import allow_ici_all_to_all
    pp = _core_inmaps(inputs)
    x = pp["x"]
    cores = list(range(NCORES))
    with allow_ici_all_to_all():
        nc = build_fused_program()
        im = []
        for r in cores:
            m = {"xin": np.ascontiguousarray(x[2 * r:2 * r + 2].reshape(TPC, C)),
                 "wqkv": pp["wqkv"], "wo0": pp["wo0"], "bvec0": pp["bvec0"],
                 "bo0": pp["bo0"], "wg": pp["wg"], "wu": pp["wu"], "wd": pp["wd"],
                 "wq1": pp["wq1"], "wk1": pp["wk1"], "wv1": pp["wv1"],
                 "wo1": pp["wo1"], "bvec1": pp["bvec1"], "bo1": pp["bo1"],
                 "bcols": pp["bcols"], "bo": pp["bo_r"], "incc": pp["incc"]}
            m.update(pp["lstm_w"])
            im.append(m)
        res = run_bass_kernel_spmd(nc, im, core_ids=cores).results
    out = np.empty((NSTEP, NCORES * SEQ2, C), np.float32)
    for r in cores:
        out[:, SEQ2 * r:SEQ2 * (r + 1), :] = res[r]["out"].reshape(
            NSTEP, SEQ2, C)
    return out


def kernel(**inputs):
    _ = inputs.get("L1_attn_pos_embed")
    return kernel_fused(**inputs)


def kernel_unfused(**inputs):
    pp = _core_inmaps(inputs)
    x = pp["x"]
    cores = list(range(NCORES))
    nc1 = build_attn_program(0, out_ln=True, in_ln=True)
    im1 = [{"xin": np.ascontiguousarray(x[2 * r:2 * r + 2].reshape(TPC, C)),
            "wqkv": pp["wqkv"], "wo": pp["wo0"], "bvec": pp["bvec0"],
            "bo": pp["bo0"]} for r in cores]
    r1 = run_bass_kernel_spmd(nc1, im1, core_ids=cores).results
    nc2 = build_ffn_program(in_ln=False, out_ln=True)
    im2 = [{"xin": r1[r]["xout"], "yin": r1[r]["yout"], "wg": pp["wg"],
            "wu": pp["wu"], "wd": pp["wd"]} for r in cores]
    r2 = run_bass_kernel_spmd(nc2, im2, core_ids=cores).results
    nc3 = build_attn_program(1, out_ln=True, in_ln=False)
    im3 = [{"xin": r2[r]["xout"], "yin": r2[r]["yout"], "wq": pp["wq1"],
            "wk": pp["wk1"], "wv": pp["wv1"], "wo": pp["wo1"],
            "bvec": pp["bvec1"], "bo": pp["bo1"]} for r in cores]
    r3 = run_bass_kernel_spmd(nc3, im3, core_ids=cores).results
    x3 = np.stack([r3[r]["xout"] for r in cores])
    y3 = np.stack([r3[r]["yout"] for r in cores])
    nc4 = build_lstm_program()
    im4 = []
    for r in cores:
        xs = np.empty((NSTEP * SEQ2, C), np.float32)
        ys = np.empty((128, 8, NSTEP * SEQ2), y3.dtype)
        for t in range(NSTEP):
            src_r = t // 2
            off = (t % 2) * TB + SEQ2 * r
            xs[SEQ2 * t:SEQ2 * (t + 1)] = x3[src_r][off:off + SEQ2]
            ys[:, :, SEQ2 * t:SEQ2 * (t + 1)] = y3[src_r][:, :, off:off + SEQ2]
        im4.append(dict(xin=xs, yin=np.ascontiguousarray(ys),
                        bcols=pp["bcols"], bo=pp["bo_r"], incc=pp["incc"],
                        **pp["lstm_w"]))
    r4 = run_bass_kernel_spmd(nc4, im4, core_ids=cores).results
    out = np.empty((NSTEP, NCORES * SEQ2, C), np.float32)
    for r in cores:
        out[:, SEQ2 * r:SEQ2 * (r + 1), :] = r4[r]["out"].reshape(
            NSTEP, SEQ2, C)
    return out


# version 12
# speedup vs baseline: 1.2395x; 1.0011x over previous
import contextlib
import numpy as np
import ml_dtypes
import concourse.bass as bass
import concourse.mybir as mybir
from concourse.bass_utils import run_bass_kernel_spmd

F32 = mybir.dt.float32
BF16 = mybir.dt.bfloat16
I32 = mybir.dt.int32
AF = mybir.ActivationFunctionType
ALU = mybir.AluOpType
AX = mybir.AxisListType
BF = ml_dtypes.bfloat16

NCORES = 8
C = 1024
NH = 16
DH = 64
FH = 3072
TB = 2048
BPC = 2
TPC = TB * BPC
SEQ2 = 256
NSTEP = 16
EPS = 1e-5
NEGBIG = -1e9
RCP127 = float(np.float32(1.0 / 127.0))

ENGS = ("sync", "gpsimd", "tensor", "vector", "scalar")
DBG = set()


class Emitter:

    SNAP = {"V": "vector", "S": "scalar", "T": "tensor"}

    def __init__(self, nc):
        self.nc = nc
        self.ops = {e: [] for e in ENGS}
        self.sems = {}
        self.cnt = {}
        self.producer = {}
        self.dma_sem = set()
        self.hw = {e: {} for e in ENGS}
        self.know = {e: {} for e in ENGS}
        self.ck = {s: {} for s in self.SNAP}

    def sem(self, name):
        if name not in self.sems:
            self.sems[name] = self.nc.alloc_semaphore(name)
            self.cnt[name] = 0
        return self.sems[name]

    def emit(self, eng, fn, wait=(), inc=None, amt=1):
        for s, _v in wait:
            self.sem(s)
        if inc is not None:
            self.sem(inc)
            self.cnt[inc] += amt
            if amt >= 16:
                self.dma_sem.add(inc)
            p = self.producer.get(inc)
            if p is None:
                self.producer[inc] = eng
            elif p != eng:
                self.producer[inc] = "*"
        if inc is not None and inc in self.SNAP:
            assert eng == self.SNAP[inc], (inc, eng)
        sems = self.sems
        hw = self.hw[eng]
        know = self.know[eng]
        kept = []
        for s, v in wait:
            if v <= 0:
                continue
            if v <= hw.get(s, 0):
                continue
            if (s not in self.dma_sem and self.producer.get(s) == eng):
                continue
            if know.get(s, 0) >= v:
                continue
            hw[s] = v
            kept.append((s, v))
        for s, v in wait:
            if v <= 0:
                continue
            if know.get(s, 0) < v:
                know[s] = v
            snap = self.ck.get(s)
            if snap is not None:
                ex = snap.get(v)
                if ex:
                    for s2, v2 in ex.items():
                        if know.get(s2, 0) < v2:
                            know[s2] = v2
        if inc is not None and inc in self.SNAP:
            newv = self.cnt[inc]
            kk = dict(know)
            kk[inc] = newv
            self.ck[inc][newv] = kk
            know[inc] = newv

        def g(b, fn=fn, kept=tuple(kept), inc=inc, amt=amt):
            for s, v in kept:
                b.wait_ge(sems[s], v)
            ins = fn(b)
            if inc is not None:
                assert ins is not None, "inc requires an instruction"
                ins.then_inc(sems[inc], amt)
        self.ops[eng].append(g)

    def val(self, sem):
        self.sem(sem)
        return self.cnt[sem]

    def barrier(self, waits, engines=ENGS):
        for e in engines:
            self.emit(e, lambda b: None, wait=waits)

    def finalize(self):
        nc = self.nc
        with nc.Block() as block:
            @block.sync
            def _(b):
                for g in self.ops["sync"]:
                    g(b)

            @block.gpsimd
            def _(b):
                for g in self.ops["gpsimd"]:
                    g(b)

            @block.tensor
            def _(b):
                for g in self.ops["tensor"]:
                    g(b)

            @block.vector
            def _(b):
                for g in self.ops["vector"]:
                    g(b)

            @block.scalar
            def _(b):
                for g in self.ops["scalar"]:
                    g(b)



def make_consts(nc):
    con = {}
    con["ident"] = nc.inline_tensor(np.eye(128, dtype=BF), name="ident_c")
    con["diagmask"] = nc.inline_tensor(
        (np.triu(np.ones((128, 128), np.float32), 1) * NEGBIG), name="diagmask_c")
    con["ones2"] = nc.inline_tensor(np.ones((2, 128), dtype=BF), name="ones2_c")
    return con


def make_alibi_consts(nc):
    kt = np.arange(TB, dtype=np.float64) - (TB - 1)
    rows = np.zeros((48, TB), np.float64)
    for h in range(NH):
        slope = 2.0 ** (-0.5 * (h + 1))
        v = slope * kt
        b1 = v.astype(BF).astype(np.float64)
        b2 = (v - b1).astype(BF).astype(np.float64)
        b3 = (v - b1 - b2).astype(BF).astype(np.float64)
        rows[3 * h + 0] = b1
        rows[3 * h + 1] = b2
        rows[3 * h + 2] = b3
    sel = np.zeros((48, NH * 128), np.float32)
    for h in range(NH):
        sel[3 * h:3 * h + 3, 128 * h:128 * (h + 1)] = 1.0
    b48 = nc.inline_tensor(rows.astype(BF), name="alibi_b48_c")
    selc = nc.inline_tensor(sel.astype(BF), name="alibi_sel_c")
    btab = np.zeros((128, (TB // 128) * NH), np.float32)
    pp = np.arange(128)
    for qt in range(TB // 128):
        for h in range(NH):
            slope = 2.0 ** (-0.5 * (h + 1))
            btab[:, NH * qt + h] = slope * ((TB - 1) - (128 * qt + pp))
    bt = nc.inline_tensor(btab, name="alibi_btab_c")
    return b48, selc, bt



def emit_act_warmup(E, funcs, scratch):
    E.emit("vector", lambda b: b.memset(scratch[:, :], 1.0), inc="V")
    mw = [("V", E.val("V"))]
    for k in range(3):
        for f in funcs:
            E.emit("scalar", lambda b, f=f: b.activation(
                scratch[:, 0:256], scratch[:, 0:256], f), wait=mw, inc="S")
            mw = [("S", E.val("S"))]


def emit_ln_pass(E, src_aps, ydram_ap, tag, capture=None):
    nc = E.nc
    nt = len(src_aps)
    st = contextlib.ExitStack()
    xb_all = st.enter_context(nc.sbuf_tensor(f"{tag}_xb", [128, nt, C], BF16))
    stt = st.enter_context(nc.sbuf_tensor(f"{tag}_st", [128, nt, 8], F32))
    scr = st.enter_context(nc.sbuf_tensor(f"{tag}_scr", [128, C], BF16))
    scr2 = st.enter_context(nc.sbuf_tensor(f"{tag}_scr2", [128, C], BF16))
    ytile = st.enter_context(nc.sbuf_tensor(f"{tag}_yt", [128, 2, C], BF16))
    ystage = st.enter_context(nc.sbuf_tensor(f"{tag}_ys", [128, 2, C], BF16))
    warm = st.enter_context(nc.sbuf_tensor(f"{tag}_warm", [128, 256], F32))
    ident_sb = st.enter_context(nc.sbuf_tensor(f"{tag}_id", [128, 128], BF16))
    pst2 = st.enter_context(nc.psum_tensor(f"{tag}_psT", [128, 2, 1024], BF16))

    eps_col = st.enter_context(nc.sbuf_tensor(f"{tag}_eps", [128, 1], F32))
    E.emit("vector", lambda b: b.memset(eps_col[:, :], EPS))
    E.emit("sync", lambda b: b.dma_start(out=ident_sb[:, :],
                                         in_=E.con["ident"].ap()),
           inc="Dc", amt=16)
    idv = E.val("Dc")
    emit_act_warmup(E, [AF.Ln, AF.Square, AF.Copy], warm)
    nt4 = (nt + 3) // 4
    gsem = {}
    for g in range(4):
        lo, hi = g * nt4, min((g + 1) * nt4, nt)
        if lo >= hi:
            continue
        name = f"Gx{g}"
        for i in range(lo, hi):
            E.emit("gpsimd", lambda b, i=i: b.dma_start(
                out=xb_all[:, i, :], in_=src_aps[i]), inc=name, amt=16)
        gv = E.val(name)
        for i in range(lo, hi):
            E.emit("vector", lambda b, i=i: b.tensor_scalar(
                scr[:, :], xb_all[:, i, :], 1.0, 0.0, ALU.mult, ALU.add,
                accum_out=stt[:, i, 0:1]), wait=[(name, gv)], inc="V")
            E.emit("scalar", lambda b, i=i: b.activation(
                scr2[:, :], xb_all[:, i, :], AF.Square,
                accum_out=stt[:, i, 1:2]), wait=[(name, gv)], inc="S")
    v_stats = E.val("V")
    s_stats = E.val("S")
    spacer = lambda b: b.tensor_copy(scr[:, 0:512], scr2[:, 0:512])
    E.emit("vector", lambda b: b.tensor_scalar(
        stt[:, :, 2:3], stt[:, :, 0:1], 1.0 / C, None, ALU.mult),
        wait=[("V", v_stats), ("S", s_stats)])
    E.emit("vector", spacer)
    E.emit("vector", lambda b: b.tensor_tensor(
        stt[:, :, 3:4], stt[:, :, 2:3], stt[:, :, 2:3], ALU.mult))
    E.emit("vector", spacer)
    E.emit("vector", lambda b: b.scalar_tensor_tensor(
        stt[:, :, 4:5], stt[:, :, 1:2], 1.0 / C, stt[:, :, 3:4],
        ALU.mult, ALU.subtract))
    E.emit("vector", spacer)
    E.emit("vector", spacer, inc="V")
    v_var = E.val("V")
    E.emit("scalar", lambda b: b.activation(
        stt[:, :, 5:6], stt[:, :, 4:5], AF.Ln, bias=eps_col[:, 0:1]),
        wait=[("V", v_var)])
    emit_act_warmup(E, [AF.Exp], warm)
    E.emit("scalar", lambda b: b.activation(
        stt[:, :, 6:7], stt[:, :, 5:6], AF.Exp, scale=-0.5))
    E.emit("scalar", lambda b: b.activation(warm[:, 0:256], warm[:, 0:256],
                                            AF.Copy), inc="S")
    s_rstd = E.val("S")
    E.emit("vector", lambda b: b.scalar_tensor_tensor(
        stt[:, :, 7:8], stt[:, :, 2:3], -1.0, stt[:, :, 6:7],
        ALU.mult, ALU.mult), wait=[("S", s_rstd)], inc="V")
    E.emit("vector", spacer)
    E.emit("vector", spacer)
    v_nmr = E.val("V")
    if capture is not None:
        E.emit("vector", lambda b: b.tensor_copy(capture[:, 0:8 * nt],
                                                 stt[:, :, :]), inc="V")
    ev = [0, 0]
    yw = [0, 0]
    sv = [0, 0]
    for i in range(nt):
        p = i % 2
        w = [("V", v_nmr)]
        if yw[p]:
            w.append(("T", yw[p]))
        E.emit("vector", lambda b, i=i, p=p: b.tensor_scalar(
            ytile[:, p, :], xb_all[:, i, :], stt[:, i, 6:7], stt[:, i, 7:8],
            ALU.mult, ALU.add), wait=w, inc="V")
        v_y = E.val("V")
        w = [("V", v_y), ("Dc", idv)]
        if ev[p]:
            w.append(("V", ev[p]))

        def tr(b, p=p):
            ins = None
            for c in range(8):
                ins = b.transpose(pst2[:, p, 128 * c:128 * (c + 1)],
                                  ytile[:, p, 128 * c:128 * (c + 1)],
                                  ident_sb[:, :])
            return ins
        E.emit("tensor", tr, wait=w, inc="T")
        yw[p] = E.val("T")
        ysem = f"{tag}y{p}"
        w = [("T", yw[p])]
        if sv[p]:
            w.append((ysem, sv[p]))
        E.emit("vector", lambda b, p=p: b.tensor_copy(
            ystage[:, p, :], pst2[:, p, :]), wait=w, inc="V")
        ev[p] = E.val("V")
        E.emit("sync", lambda b, p=p, i=i: b.dma_start(
            out=ydram_ap[:, :, 128 * i:128 * (i + 1)],
            in_=ystage[:, p, :].rearrange("x (a c) -> x a c", a=8)),
            wait=[("V", ev[p])], inc=ysem, amt=16)
        sv[p] = E.val(ysem)
    st.close()
    return [(f"{tag}y0", E.val(f"{tag}y0")), (f"{tag}y1", E.val(f"{tag}y1"))]



def emit_ffn_block(E, xin_ap, yin_ap, xout_ap, wg_ap, wu_ap, wd_ap, ctx,
                   ntok=TPC):
    nc = E.nc
    st = contextlib.ExitStack()
    wg_sb = st.enter_context(nc.sbuf_tensor("ffn_wg", [128, 8, FH], BF16))
    wu_sb = st.enter_context(nc.sbuf_tensor("ffn_wu", [128, 8, FH], BF16))
    wd_sb = st.enter_context(nc.sbuf_tensor("ffn_wd", [128, 24, C], BF16))
    warm_sb = st.enter_context(nc.sbuf_tensor("ffn_warm", [128, 256], F32))
    emit_act_warmup(E, [AF.Tanh], warm_sb)

    with nc.sbuf_tensor("ffn_stage", [128, 2, FH], I32) as stage:
        last_dq = [0, 0]
        srcs = [(wg_ap, wg_sb, 8, FH, RCP127), (wu_ap, wu_sb, 8, FH, RCP127),
                (wd_ap, wd_sb, 24, C, float(np.float32(0.5) * np.float32(RCP127)))]
        k = 0
        for (src, dst, nch, wid, dqs) in srcs:
            for cc in range(nch):
                p = k % 2
                w = [("V", last_dq[p])] if last_dq[p] else []
                gsem = f"Gs{p}"
                E.emit("gpsimd", lambda b, src=src, cc=cc, p=p, wid=wid:
                       b.dma_start(out=stage[:, p, 0:wid], in_=src[:, cc, :]),
                       wait=w, inc=gsem, amt=16)
                gv = E.val(gsem)
                E.emit("vector", lambda b, dst=dst, cc=cc, p=p, wid=wid, dqs=dqs:
                       b.tensor_scalar(dst[:, cc, :], stage[:, p, 0:wid],
                                       dqs, None, ALU.mult),
                       wait=[(gsem, gv)], inc="V")
                last_dq[p] = E.val("V")
                k += 1
    dq_done = E.val("V")
    E.barrier([("V", dq_done)])

    y_blk = st.enter_context(nc.sbuf_tensor("ffn_y", [128, 2, 8, 512], BF16))
    hid = st.enter_context(nc.sbuf_tensor("ffn_hid", [128, 24, 512], BF16))
    t_sb = st.enter_context(nc.sbuf_tensor("ffn_t", [128, 2, 512], BF16))
    g_sb = st.enter_context(nc.sbuf_tensor("ffn_g", [128, 2, 512], BF16))
    xres2 = st.enter_context(nc.sbuf_tensor("ffn_xr", [128, 2, C], F32))
    out2 = st.enter_context(nc.sbuf_tensor("ffn_out", [128, 2, C], F32))
    ps_s = st.enter_context(nc.psum_tensor("ffn_psA", [128, 2048], F32))
    ps_m = st.enter_context(nc.psum_tensor("ffn_psM", [128, 1024], F32))

    last_hmul = [0, 0]
    last_gu = 0
    prev_res = 0
    last_store = [0, 0]
    last_xload = [0, 0]
    last_yload = [0, 0]

    nblk = ntok // 512
    for blk in range(nblk):
        bp = blk % 2
        w = [("T", last_yload[bp])] if last_yload[bp] else []
        ybsem = f"Dyb{bp}"
        E.emit("sync", lambda b, blk=blk, bp=bp: b.dma_start(
            out=y_blk[:, bp, :, :],
            in_=yin_ap[:, :, 512 * blk:512 * (blk + 1)]),
            wait=w, inc=ybsem, amt=16)
        yv = (ybsem, E.val(ybsem))
        for ht in range(24):
            pp = ht % 2
            gslc = (1024 * pp, 1024 * pp + 512)
            uslc = (1024 * pp + 512, 1024 * pp + 1024)
            w = [yv]
            if last_hmul[pp]:
                w.append(("V", last_hmul[pp]))

            def gu(b, ht=ht, bp=bp, gslc=gslc, uslc=uslc):
                ins = None
                for cc in range(8):
                    b.matmul(ps_s[:, gslc[0]:gslc[1]],
                             wg_sb[:, cc, 128 * ht:128 * (ht + 1)],
                             y_blk[:, bp, cc, :], start=(cc == 0),
                             stop=(cc == 7))
                for cc in range(8):
                    ins = b.matmul(ps_s[:, uslc[0]:uslc[1]],
                                   wu_sb[:, cc, 128 * ht:128 * (ht + 1)],
                                   y_blk[:, bp, cc, :], start=(cc == 0),
                                   stop=(cc == 7))
                return ins
            E.emit("tensor", gu, wait=w, inc="T")
            last_gu = E.val("T")
            E.emit("scalar", lambda b, pp=pp, gslc=gslc: b.activation(
                t_sb[:, pp, :], ps_s[:, gslc[0]:gslc[1]], AF.Tanh, scale=0.5),
                wait=[("T", last_gu)], inc="S")
            sv = E.val("S")
            E.emit("vector", lambda b, pp=pp, gslc=gslc:
                   b.scalar_tensor_tensor(g_sb[:, pp, :], t_sb[:, pp, :], 1.0,
                                          ps_s[:, gslc[0]:gslc[1]],
                                          ALU.add, ALU.mult),
                   wait=[("S", sv)])
            E.emit("vector", lambda b, ht=ht, pp=pp, uslc=uslc:
                   b.tensor_tensor(hid[:, ht, :], ps_s[:, uslc[0]:uslc[1]],
                                   g_sb[:, pp, :], ALU.mult), inc="V")
            last_hmul[pp] = E.val("V")
        last_yload[bp] = last_gu
        hv = E.val("V")
        for tt in range(4):
            p = tt % 2
            w = [("V", hv)]
            if prev_res:
                w.append(("V", prev_res))

            def down(b, tt=tt):
                ins = None
                for mc in range(2):
                    for hc in range(24):
                        ins = b.matmul(ps_m[:, 512 * mc:512 * (mc + 1)],
                                       hid[:, hc, 128 * tt:128 * (tt + 1)],
                                       wd_sb[:, hc, 512 * mc:512 * (mc + 1)],
                                       start=(hc == 0), stop=(hc == 23))
                return ins
            E.emit("tensor", down, wait=w, inc="T")
            tv = E.val("T")
            row0 = 512 * blk + 128 * tt
            xsem, osem = f"Dx{p}", f"Do{p}"
            wl = [("V", last_xload[p])] if last_xload[p] else []
            E.emit("sync", lambda b, row0=row0, p=p: b.dma_start(
                out=xres2[:, p, :], in_=xin_ap[row0:row0 + 128, :]),
                wait=wl, inc=xsem, amt=16)
            dv = E.val(xsem)
            w = [("T", tv), (xsem, dv)]
            if last_store[p]:
                w.append((osem, last_store[p]))
            E.emit("vector", lambda b, p=p: b.tensor_tensor(
                out2[:, p, :], ps_m[:, :], xres2[:, p, :], ALU.add),
                wait=w, inc="V")
            prev_res = E.val("V")
            last_xload[p] = prev_res
            E.emit("sync", lambda b, row0=row0, p=p: b.dma_start(
                out=xout_ap[row0:row0 + 128, :], in_=out2[:, p, :]),
                wait=[("V", prev_res)], inc=osem, amt=16)
            last_store[p] = E.val(osem)
    st.close()
    return [("Do0", E.val("Do0")), ("Do1", E.val("Do1"))]



def emit_attn_block(E, xin_ap, yin_ap, xout_ap, wq_ap, wk_ap, wv_ap, wo_ap,
                    bvec_ap, bo_ap, layer, tag, dbg_nqt=16, dbg_nb=BPC,
                    out_row_fn=None):
    nc = E.nc
    st = contextlib.ExitStack()
    warm = st.enter_context(nc.sbuf_tensor(f"{tag}_warm", [128, 256], F32))
    Q2 = st.enter_context(nc.sbuf_tensor(f"{tag}_q2", [128, 8, TB], BF16))
    K2 = st.enter_context(nc.sbuf_tensor(f"{tag}_k2", [128, 8, TB], BF16))
    Vt = st.enter_context(nc.sbuf_tensor(f"{tag}_vt", [128, 16, C], BF16))
    bvec = st.enter_context(nc.sbuf_tensor(f"{tag}_bv", [128, 24], F32))
    bvbf = st.enter_context(nc.sbuf_tensor(f"{tag}_bvb", [128, 8], BF16))
    bo_sb = st.enter_context(nc.sbuf_tensor(f"{tag}_bo", [1, C], F32))
    bias2 = st.enter_context(nc.sbuf_tensor(f"{tag}_b2", [2, C], BF16))
    ones2 = st.enter_context(nc.sbuf_tensor(f"{tag}_on2", [2, 128], BF16))
    dmask = st.enter_context(nc.sbuf_tensor(f"{tag}_dm", [128, 128], F32))
    ident_sb = st.enter_context(nc.sbuf_tensor(f"{tag}_id", [128, 128], BF16))
    if layer == 0:
        sel_sb = st.enter_context(nc.sbuf_tensor(f"{tag}_sel", [48, NH * 128], BF16))
        b48_sb = st.enter_context(nc.sbuf_tensor(f"{tag}_b48", [48, TB], BF16))
        btab_sb = st.enter_context(
            nc.sbuf_tensor(f"{tag}_btab", [128, (TB // 128) * NH], F32))
    ast = st.enter_context(nc.sbuf_tensor(f"{tag}_ast", [128, 2, 8], F32))
    E.dbgP = st.enter_context(nc.sbuf_tensor(f"{tag}_dbgp", [128, 128], BF16))
    E.dbgA = st.enter_context(nc.sbuf_tensor(f"{tag}_dbga", [128, 8], F32))
    P_sb = st.enter_context(nc.sbuf_tensor(f"{tag}_p", [128, 2, TB], BF16))
    PT_sb = st.enter_context(nc.sbuf_tensor(f"{tag}_pt", [128, TB], BF16))
    zn2 = st.enter_context(nc.sbuf_tensor(f"{tag}_zn", [128, 128], BF16))
    qst = st.enter_context(nc.sbuf_tensor(f"{tag}_qst", [128, 2, 128], BF16))
    zT2 = st.enter_context(nc.sbuf_tensor(f"{tag}_zt", [128, 128], BF16))
    ps_s = st.enter_context(nc.psum_tensor(f"{tag}_pss", [128, 2048], F32))
    ps_t = st.enter_context(nc.psum_tensor(f"{tag}_pst", [128, 1024], BF16))
    ps_z = st.enter_context(nc.psum_tensor(f"{tag}_psz", [128, 128], F32))
    ps_m = st.enter_context(nc.psum_tensor(f"{tag}_psm", [128, 1024], F32))

    emit_act_warmup(E, [AF.Exp], warm)
    for dst, srcap in ((bvec, bvec_ap), (bo_sb, bo_ap),
                       (ones2, E.con["ones2"].ap()),
                       (dmask, E.con["diagmask"].ap()),
                       (ident_sb, E.con["ident"].ap())):
        E.emit("sync", lambda b, dst=dst, srcap=srcap: b.dma_start(
            out=dst[:, :], in_=srcap), inc="Dc", amt=16)
    if layer == 0:
        E.emit("sync", lambda b: b.dma_start(out=sel_sb[:, :],
                                             in_=E.con["alibi_sel"].ap()),
               inc="Dc", amt=16)
        E.emit("sync", lambda b: b.dma_start(out=b48_sb[:, :],
                                             in_=E.con["alibi_b48"].ap()),
               inc="Dc", amt=16)
        E.emit("sync", lambda b: b.dma_start(out=btab_sb[:, :],
                                             in_=E.con["alibi_btab"].ap()),
               inc="Dc", amt=16)
    dc_v = E.val("Dc")
    E.emit("vector", lambda b: b.tensor_copy(bvbf[:, :], bvec[:, 16:24]),
           wait=[("Dc", dc_v)], inc="V")
    E.emit("vector", lambda b: b.memset(qst[:, :, :], 0.0), inc="V")
    E.barrier([("Dc", dc_v), ("V", E.val("V"))])

    osem = [f"{tag}o0", f"{tag}o1"]
    xsem = [f"{tag}x0", f"{tag}x1"]
    last_store = [0, 0]
    last_xload = [0, 0]
    bias_ready = False
    prev_merge_evac = [0]

    for bb in range(dbg_nb):
        tok0 = bb * TB
        E.barrier([("T", E.val("T"))], engines=("vector", "sync", "gpsimd"))
        with contextlib.ExitStack() as stA:
            wq_sb = stA.enter_context(nc.sbuf_tensor(f"{tag}_wq{bb}", [128, 8, C], BF16))
            wk_sb = stA.enter_context(nc.sbuf_tensor(f"{tag}_wk{bb}", [128, 8, C], BF16))
            wv_sb = stA.enter_context(nc.sbuf_tensor(f"{tag}_wv{bb}", [128, 8, C], BF16))
            y_win = stA.enter_context(nc.sbuf_tensor(f"{tag}_yw{bb}", [128, 2, 8, 512], BF16))
            wsems = {}
            for nm, dst, srcap in (("q", wq_sb, wq_ap), ("k", wk_sb, wk_ap),
                                   ("v", wv_sb, wv_ap)):
                sname = f"{tag}w{nm}{bb}"
                E.emit("gpsimd", lambda b, dst=dst, srcap=srcap: b.dma_start(
                    out=dst[:, :, :], in_=srcap), inc=sname, amt=16)
                wsems[nm] = (sname, E.val(sname))
            bank_evac = [0, 0, 0, 0]
            ywin_T = [0, 0]
            kbank = [0]
            for tw in range(4):
                p = tw % 2
                w = [("T", ywin_T[p])] if ywin_T[p] else []
                ysm = f"{tag}ya{p}"
                E.emit("sync", lambda b, tw=tw, p=p, tok0=tok0: b.dma_start(
                    out=y_win[:, p, :, :],
                    in_=yin_ap[:, :, tok0 + 512 * tw: tok0 + 512 * (tw + 1)]),
                    wait=w, inc=ysm, amt=16)
                yv = E.val(ysm)

                def proj(dst, w_sb, wwait, bcol, tw=tw, p=p, yv=yv, ysm=ysm):
                    for hp in range(8):
                        bk = kbank[0] % 4
                        kbank[0] += 1
                        w = [(ysm, yv), wwait]
                        if bank_evac[bk]:
                            w.append(("V", bank_evac[bk]))

                        def mm(b, hp=hp, bk=bk, w_sb=w_sb, p=p):
                            ins = None
                            for cc in range(8):
                                ins = b.matmul(
                                    ps_s[:, 512 * bk:512 * (bk + 1)],
                                    w_sb[:, cc, 128 * hp:128 * (hp + 1)],
                                    y_win[:, p, cc, :],
                                    start=(cc == 0), stop=(cc == 7))
                            return ins
                        E.emit("tensor", mm, wait=w, inc="T")
                        tv = E.val("T")
                        E.emit("vector", lambda b, hp=hp, bk=bk, dst=dst,
                               bcol=bcol, tw=tw: b.tensor_scalar(
                                   dst[:, hp, 512 * tw:512 * (tw + 1)],
                                   ps_s[:, 512 * bk:512 * (bk + 1)],
                                   bvec[:, bcol + hp:bcol + hp + 1], None,
                                   ALU.add), wait=[("T", tv)], inc="V")
                        bank_evac[bk] = E.val("V")
                proj(Q2, wq_sb, wsems["q"], 0)
                proj(K2, wk_sb, wsems["k"], 8)
                for tt in range(4):
                    for vc in range(2):
                        bk = kbank[0] % 4
                        kbank[0] += 1
                        w = [(ysm, yv), wsems["v"]]
                        if bank_evac[bk]:
                            w.append(("V", bank_evac[bk]))

                        def mmv(b, tt=tt, vc=vc, bk=bk, p=p):
                            ins = None
                            for cc in range(8):
                                ins = b.matmul(
                                    ps_s[:, 512 * bk:512 * (bk + 1)],
                                    y_win[:, p, cc, 128 * tt:128 * (tt + 1)],
                                    wv_sb[:, cc, 512 * vc:512 * (vc + 1)],
                                    start=(cc == 0), stop=(cc == 7))
                            return ins
                        E.emit("tensor", mmv, wait=w, inc="T")
                        tv = E.val("T")
                        E.emit("vector", lambda b, tt=tt, vc=vc, bk=bk, tw=tw:
                               b.tensor_copy(
                                   Vt[:, 4 * tw + tt, 512 * vc:512 * (vc + 1)],
                                   ps_s[:, 512 * bk:512 * (bk + 1)]),
                               wait=[("T", tv)], inc="V")
                        bank_evac[bk] = E.val("V")
                ywin_T[p] = E.val("T")
        E.barrier([("V", E.val("V"))])
        with contextlib.ExitStack() as stB:
            wo_sb = stB.enter_context(nc.sbuf_tensor(f"{tag}_wo{bb}", [128, 8, C], BF16))
            comb = stB.enter_context(nc.sbuf_tensor(f"{tag}_cmb{bb}", [1, C], F32))
            hi1 = stB.enter_context(nc.sbuf_tensor(f"{tag}_hi{bb}", [1, C], BF16))
            lo1 = stB.enter_context(nc.sbuf_tensor(f"{tag}_lo{bb}", [1, C], BF16))
            out2 = stB.enter_context(nc.sbuf_tensor(f"{tag}_o2{bb}", [128, 2, C], F32))
            xres2 = stB.enter_context(nc.sbuf_tensor(f"{tag}_xr{bb}", [128, 2, C], F32))
            wosem = f"{tag}wo{bb}"
            E.emit("gpsimd", lambda b: b.dma_start(out=wo_sb[:, :, :],
                                                   in_=wo_ap), inc=wosem, amt=16)
            wo_v = E.val(wosem)
            if not bias_ready:
                bias_ready = True

                def bmm(b):
                    ins = None
                    for j in range(8):
                        for mw in range(2):
                            ins = b.matmul(ps_m[0:1, 512 * mw:512 * (mw + 1)],
                                           bvbf[:, j:j + 1],
                                           wo_sb[:, j, 512 * mw:512 * (mw + 1)],
                                           start=(j == 0), stop=(j == 7))
                    return ins
                E.emit("tensor", bmm, wait=[(wosem, wo_v)], inc="T")
                tv = E.val("T")
                E.emit("vector", lambda b: b.tensor_tensor(
                    comb[:, :], ps_m[0:1, :], bo_sb[:, :], ALU.add),
                    wait=[("T", tv)], inc="V")
                E.emit("vector", lambda b: b.tensor_copy(hi1[:, :],
                                                         comb[:, :]))
                E.emit("vector", lambda b: b.tensor_tensor(
                    lo1[:, :], comb[:, :], hi1[:, :], ALU.subtract),
                    inc="V")
                prev_merge_evac[0] = E.val("V")
                E.emit("gpsimd", lambda b: b.dma_start(
                    out=bias2[0:1, :], in_=hi1[:, :]),
                    wait=[("V", E.val("V"))], inc=f"{tag}b2", amt=16)
                E.emit("gpsimd", lambda b: b.dma_start(
                    out=bias2[1:2, :], in_=lo1[:, :]),
                    inc=f"{tag}b2", amt=16)
                E.barrier([("V", E.val("V")), (f"{tag}b2", E.val(f"{tag}b2"))],
                          engines=("tensor",))

            pend = [None]
            qst_war = [0, 0]
            reg_free = {0: (0, 0), 1: (0, 0)}
            psz_free = [0]
            pt_war = [0]
            psb_war = [0, 0]
            it = [0]

            def flush():
                if pend[0] is None or "noflush" in DBG:
                    pend[0] = None
                    return
                (qt, hp, j, par, W, s_exp) = pend[0]
                pend[0] = None
                h = 2 * hp + j
                nkt = W // 128
                for rnd in range((nkt + 7) // 8):
                    k0, k1 = 8 * rnd, min(8 * rnd + 8, nkt)

                    def tr(b, k0=k0, k1=k1, par=par):
                        ins = None
                        for k in range(k0, k1):
                            ins = b.transpose(
                                ps_t[:, 128 * (k - k0):128 * (k - k0 + 1)],
                                P_sb[:, par, 128 * k:128 * (k + 1)],
                                ident_sb[:, :])
                        return ins
                    w = [("S", s_exp)]
                    if pt_war[0]:
                        w.append(pt_war[0])
                    E.emit("tensor", tr, wait=w, inc="T")
                    tv = E.val("T")
                    E.emit("scalar", lambda b, k0=k0, k1=k1: b.activation(
                        PT_sb[:, 128 * k0:128 * k1],
                        ps_t[:, 0:128 * (k1 - k0)], AF.Copy),
                        wait=[("T", tv)], inc="Sp")
                    pt_war[0] = ("Sp", E.val("Sp"))
                psb_war[par] = E.val("T")
                w = [pt_war[0]]
                if psz_free[0]:
                    w.append(("V", psz_free[0]))

                def pv(b, nkt=nkt, h=h):
                    ins = None
                    for k in range(nkt):
                        ins = b.matmul(ps_z[:, 0:DH],
                                       PT_sb[:, 128 * k:128 * (k + 1)],
                                       Vt[:, k, DH * h:DH * h + DH],
                                       start=(k == 0), stop=(k == nkt - 1))
                    return ins
                E.emit("tensor", pv, wait=w, inc="T")
                tv = E.val("T")
                E.emit("vector", lambda b, j=j, par=par: b.tensor_scalar(
                    zn2[:, DH * j:DH * (j + 1)], ps_z[:, 0:DH],
                    ast[:, par, 1:2], None, ALU.mult),
                    wait=[("T", tv)], inc="V")
                psz_free[0] = E.val("V")
                if j == 1:
                    w = [("V", psz_free[0])]
                    if pt_war[0]:
                        w.append(pt_war[0])
                    E.emit("tensor", lambda b: b.transpose(
                        ps_t[:, 0:128], zn2[:, :], ident_sb[:, :]),
                        wait=w, inc="T")
                    tv = E.val("T")
                    E.emit("vector", lambda b: b.tensor_copy(
                        zT2[:, :], ps_t[:, 0:128]), wait=[("T", tv)], inc="V")
                    pt_war[0] = ("V", E.val("V"))
                    zv = E.val("V")
                    w = [("V", zv), (wosem, wo_v)]
                    if hp == 0 and prev_merge_evac[0]:
                        w.append(("V", prev_merge_evac[0]))

                    def mrg(b, hp=hp):
                        ins = None
                        for mw in range(2):
                            ins = b.matmul(ps_m[:, 512 * mw:512 * (mw + 1)],
                                           zT2[:, :],
                                           wo_sb[:, hp, 512 * mw:512 * (mw + 1)],
                                           start=(hp == 0), stop=False)
                        return ins
                    E.emit("tensor", mrg, wait=w, inc="T")

            for qt in range(dbg_nqt):
                W = 128 * (qt + 1)
                two_chunk = W > 1024
                for hp in range(8):
                    for j in range(2):
                        h = 2 * hp + j
                        it[0] += 1
                        par = it[0] % 2
                        if two_chunk:
                            regs = [0, 1]
                            roff = 0
                        else:
                            regs = [it[0] % 2]
                            roff = 1024 * (it[0] % 2)
                        w = []
                        for r in regs:
                            if reg_free[r][0]:
                                w.append(("S", reg_free[r][0]))
                            if reg_free[r][1]:
                                w.append(("V", reg_free[r][1]))

                        alibi = (layer == 0) and ("noalibi" not in DBG)
                        wq = []
                        if qst_war[j]:
                            wq.append(("T", qst_war[j]))
                        E.emit("scalar", lambda b, qt=qt, hp=hp, j=j:
                               b.activation(
                                   qst[64 * j:64 * (j + 1), j, :],
                                   Q2[64 * j:64 * (j + 1), hp,
                                      128 * qt:128 * (qt + 1)],
                                   AF.Copy, scale=0.125),
                               wait=wq, inc="S")
                        w.append(("S", E.val("S")))

                        def smm(b, hp=hp, j=j, W=W, roff=roff, h=h,
                                alibi=alibi):
                            ins = None
                            for wnd in range(0, W, 512):
                                wid = min(512, W - wnd)
                                ins = b.matmul(
                                    ps_s[:, roff + wnd:roff + wnd + wid],
                                    qst[:, j, :],
                                    K2[:, hp, wnd:wnd + wid],
                                    start=True, stop=not alibi)
                                if alibi:
                                    ins = b.matmul(
                                        ps_s[:, roff + wnd:roff + wnd + wid],
                                        sel_sb[:, 128 * h:128 * (h + 1)],
                                        b48_sb[:, wnd:wnd + wid],
                                        start=False, stop=True)
                            return ins
                        if "nosmm" in DBG:
                            E.emit("tensor", lambda b, roff=roff, W=W: b.matmul(
                                ps_s[:, roff:roff + W], ident_sb[:, 0:128],
                                K2[:, 0, 0:W], start=True, stop=True),
                                wait=w, inc="T")
                        else:
                            E.emit("tensor", smm, wait=w, inc="T")
                        t_s = E.val("T")
                        qst_war[j] = t_s
                        E.emit("vector", lambda b, roff=roff, W=W:
                               b.tensor_tensor(
                                   ps_s[:, roff + W - 128:roff + W],
                                   ps_s[:, roff + W - 128:roff + W],
                                   dmask[:, :], ALU.add),
                               wait=[("T", t_s)], inc="V")
                        bap = (btab_sb[:, NH * qt + h:NH * qt + h + 1]
                               if alibi else None)
                        flush()
                        v_bias = E.val("V")
                        w = [("V", v_bias), ("Dc", dc_v)]
                        if psb_war[par]:
                            w.append(("T", psb_war[par]))
                        if "noexp" in DBG:
                            E.emit("scalar", lambda b, par=par, roff=roff, W=W:
                                   b.activation(
                                       P_sb[:, par, 0:W], ps_s[:, roff:roff + W],
                                       AF.Copy), wait=w, inc="S")
                        else:
                            E.emit("scalar", lambda b, par=par, roff=roff, W=W,
                                   bap=bap: b.activation(
                                       P_sb[:, par, 0:W], ps_s[:, roff:roff + W],
                                       AF.Exp,
                                       **({} if bap is None else {"bias": bap}),
                                       scale=1.0,
                                       accum_out=ast[:, par, 0:1]),
                                   wait=w, inc="S")
                        s_exp = E.val("S")
                        reg_done_v = E.val("V")
                        for r in regs:
                            reg_free[r] = (s_exp, reg_done_v)
                        E.emit("vector", lambda b, par=par: b.reciprocal(
                            ast[:, par, 1:2], ast[:, par, 0:1]),
                            wait=[("S", s_exp)], inc="V")
                        if "dump" in DBG and it[0] == 1 and bb == 0:
                            E.emit("vector", lambda b, par=par: b.tensor_copy(
                                E.dbgP[:, :], P_sb[:, par, 0:128]),
                                wait=[("S", s_exp)], inc="V")
                            E.emit("vector", lambda b, par=par: b.tensor_copy(
                                E.dbgA[:, :], ast[:, par, :]), inc="V")
                            dv = E.val("V")
                            E.emit("sync", lambda b: b.dma_start(
                                out=E.dbgP_ap, in_=E.dbgP[:, :]),
                                wait=[("V", dv)], inc="Dc", amt=16)
                            E.emit("sync", lambda b: b.dma_start(
                                out=E.dbgA_ap, in_=E.dbgA[:, :]),
                                inc="Dc", amt=16)
                        pend[0] = (qt, hp, j, par, W, s_exp)
                flush()
                if "notail" in DBG:
                    continue

                def bmm2(b):
                    ins = None
                    for mw in range(2):
                        ins = b.matmul(ps_m[:, 512 * mw:512 * (mw + 1)],
                                       ones2[:, :],
                                       bias2[0:2, 512 * mw:512 * (mw + 1)],
                                       start=False, stop=True)
                    return ins
                E.emit("tensor", bmm2, inc="T")
                t_m = E.val("T")
                p = qt % 2
                row0 = tok0 + 128 * qt
                orow0 = row0 if out_row_fn is None else out_row_fn(bb, qt)
                w = [("V", last_xload[p])] if last_xload[p] else []
                E.emit("sync", lambda b, row0=row0, p=p: b.dma_start(
                    out=xres2[:, p, :], in_=xin_ap[row0:row0 + 128, :]),
                    wait=w, inc=xsem[p], amt=16)
                dv = E.val(xsem[p])
                w = [("T", t_m), (xsem[p], dv)]
                if last_store[p]:
                    w.append((osem[p], last_store[p]))
                E.emit("vector", lambda b, p=p: b.tensor_tensor(
                    out2[:, p, :], ps_m[:, :], xres2[:, p, :], ALU.add),
                    wait=w, inc="V")
                prev_merge_evac[0] = E.val("V")
                last_xload[p] = prev_merge_evac[0]
                E.emit("sync", lambda b, orow0=orow0, p=p: b.dma_start(
                    out=xout_ap[orow0:orow0 + 128, :], in_=out2[:, p, :]),
                    wait=[("V", prev_merge_evac[0])], inc=osem[p], amt=16)
                last_store[p] = E.val(osem[p])
    st.close()
    return [(osem[0], E.val(osem[0])), (osem[1], E.val(osem[1]))]



def emit_lstm_block(E, xin_ap, yin_ap, out_ap, w_aps, b_aps, ctx):
    nc = E.nc
    S2 = SEQ2
    st = contextlib.ExitStack()
    wsb = {}
    for g in ("f", "i", "c", "o"):
        wsb[g] = (st.enter_context(nc.sbuf_tensor(f"ls_w{g}", [128, 8, C], BF16)),
                  st.enter_context(nc.sbuf_tensor(f"ls_u{g}", [128, 8, C], BF16)))
    bcols = st.enter_context(nc.sbuf_tensor("ls_bc", [128, 24], F32))
    inccol = st.enter_context(nc.sbuf_tensor("ls_inc", [128, 8], F32))
    bo2 = st.enter_context(nc.sbuf_tensor("ls_bo2", [2, C], BF16))
    ones2 = st.enter_context(nc.sbuf_tensor("ls_on2", [2, 128], BF16))
    warm = st.enter_context(nc.sbuf_tensor("ls_warm", [128, 256], F32))
    c_fm = st.enter_context(nc.sbuf_tensor("ls_c", [128, 8, S2], F32))
    c_bf = st.enter_context(nc.sbuf_tensor("ls_cb", [128, 8, S2], BF16))
    f_sb = st.enter_context(nc.sbuf_tensor("ls_f", [128, 8, S2], F32))
    i_sb = st.enter_context(nc.sbuf_tensor("ls_i", [128, 8, S2], F32))
    g_sb = st.enter_context(nc.sbuf_tensor("ls_g", [128, 8, S2], F32))
    y2 = st.enter_context(nc.sbuf_tensor("ls_y", [128, 2, 8, S2], BF16))
    cfuo = st.enter_context(nc.sbuf_tensor("ls_cfuo", [128, 2, C], F32))
    xr2 = st.enter_context(nc.sbuf_tensor("ls_xr", [128, 2, C], F32))
    o2 = st.enter_context(nc.sbuf_tensor("ls_o2", [128, 2, C], F32))
    ps_g = st.enter_context(nc.psum_tensor("ls_psg", [128, 2, 512], F32))
    ps_h = st.enter_context(nc.psum_tensor("ls_psh", [128, 2, 512], F32))

    emit_act_warmup(E, [AF.Sigmoid, AF.Tanh], warm)
    for g in ("f", "i", "c", "o"):
        for k in range(2):
            E.emit("gpsimd", lambda b, g=g, k=k: b.dma_start(
                out=wsb[g][k][:, :, :], in_=w_aps[g][k]), inc=f"lw{g}{k}", amt=16)
    E.emit("sync", lambda b: b.dma_start(out=bcols[:, :], in_=b_aps["bcols"]),
           inc="Dc", amt=16)
    E.emit("sync", lambda b: b.dma_start(out=inccol[:, :], in_=b_aps["inc"]),
           inc="Dc", amt=16)
    E.emit("sync", lambda b: b.dma_start(out=ones2[:, :],
                                         in_=E.con["ones2"].ap()), inc="Dc", amt=16)
    with (nc.sbuf_tensor("ls_bo", [1, C], F32) as bo_sb,
          nc.sbuf_tensor("ls_hi", [1, C], BF16) as hi1,
          nc.sbuf_tensor("ls_lo", [1, C], BF16) as lo1):
        E.emit("sync", lambda b: b.dma_start(out=bo_sb[:, :], in_=b_aps["bo"]),
               inc="Dc", amt=16)
        dc = E.val("Dc")
        E.emit("vector", lambda b: b.memset(warm[:, :], 0.0),
               wait=[("Dc", dc), ("S", E.val("S"))])
        E.emit("vector", lambda b: b.tensor_copy(hi1[:, :], bo_sb[:, :]),
               inc="V")
        E.emit("vector", lambda b: b.tensor_tensor(
            lo1[:, :], bo_sb[:, :], hi1[:, :], ALU.subtract), inc="V")
        bv = E.val("V")
        E.emit("gpsimd", lambda b: b.dma_start(out=bo2[0:1, :], in_=hi1[:, :]),
               wait=[("V", bv)], inc="lsb2", amt=16)
        E.emit("gpsimd", lambda b: b.dma_start(out=bo2[1:2, :], in_=lo1[:, :]),
               inc="lsb2", amt=16)
        E.barrier([("lsb2", E.val("lsb2"))], engines=("vector", "scalar"))
    dc = E.val("Dc")
    wws = [(f"lw{g}{k}", E.val(f"lw{g}{k}")) for g in "fico" for k in range(2)]
    zz = warm
    for cc in range(8):
        E.emit("vector", lambda b, cc=cc: b.tensor_scalar(
            c_fm[:, cc, :], zz[:, :], inccol[:, cc:cc + 1], None, ALU.add))
    E.emit("vector", lambda b: b.tensor_copy(
        c_bf[:, :, :].rearrange("p a s -> p (a s)"),
        c_fm[:, :, :].rearrange("p a s -> p (a s)")), inc="V")
    E.barrier([("V", E.val("V")), ("Dc", dc), ("lsb2", E.val("lsb2"))] + wws)

    nstep = NSTEP
    ylast = [0, 0]
    for t in range(nstep):
        p = t % 2
        ysm = f"lsy{p}"
        w = [("T", ylast[p])] if ylast[p] else []
        E.emit("sync", lambda b, t=t, p=p: b.dma_start(
            out=y2[:, p, :, :], in_=yin_ap[:, :, S2 * t:S2 * (t + 1)]),
            wait=w, inc=ysm, amt=16)
        yv = E.val(ysm)
        gates = (("f", f_sb, AF.Sigmoid, 0), ("i", i_sb, AF.Sigmoid, 8),
                 ("c", g_sb, AF.Tanh, 16))
        bank_free = [0, 0]
        for (gn, dst, func, bcol) in gates:
            for mt in range(8):
                bk = mt % 2
                wl = [(ysm, yv)]
                if bank_free[bk]:
                    wl.append(("S", bank_free[bk]))

                def mm(b, gn=gn, mt=mt, bk=bk, p=p):
                    ins = None
                    for cc in range(8):
                        b.matmul(ps_g[:, bk, 0:S2],
                                 wsb[gn][0][:, cc, 128 * mt:128 * (mt + 1)],
                                 y2[:, p, cc, :], start=(cc == 0), stop=False)
                    for cc in range(8):
                        ins = b.matmul(
                            ps_g[:, bk, 0:S2],
                            wsb[gn][1][:, cc, 128 * mt:128 * (mt + 1)],
                            c_bf[:, cc, :], start=False, stop=(cc == 7))
                    return ins
                E.emit("tensor", mm, wait=wl, inc="T")
                tv = E.val("T")
                E.emit("scalar", lambda b, gn=gn, mt=mt, bk=bk, dst=dst,
                       func=func, bcol=bcol: b.activation(
                           dst[:, mt, :], ps_g[:, bk, 0:S2], func,
                           bias=bcols[:, bcol + mt:bcol + mt + 1]),
                       wait=[("T", tv)], inc="S")
                bank_free[bk] = E.val("S")
        ylast[p] = E.val("T")
        sg = E.val("S")
        fl = f_sb[:, :, :].rearrange("p a s -> p (a s)")
        il = i_sb[:, :, :].rearrange("p a s -> p (a s)")
        gl = g_sb[:, :, :].rearrange("p a s -> p (a s)")
        cl = c_fm[:, :, :].rearrange("p a s -> p (a s)")
        cbl = c_bf[:, :, :].rearrange("p a s -> p (a s)")
        E.emit("vector", lambda b, fl=fl, cl=cl: b.tensor_tensor(
            fl, fl, cl, ALU.mult), wait=[("S", sg)])
        E.emit("vector", lambda b, il=il, gl=gl: b.tensor_tensor(
            il, il, gl, ALU.mult))
        E.emit("vector", lambda b, cl=cl, fl=fl, il=il: b.tensor_tensor(
            cl, fl, il, ALU.add))
        E.emit("vector", lambda b, cbl=cbl, cl=cl: b.tensor_copy(cbl, cl),
               inc="V")
        cv = E.val("V")
        E.emit("tensor", lambda b: None, wait=[("V", cv)])
    if "noh" in DBG:
        st.close()
        return [("lsy0", E.val("lsy0")), ("lsy1", E.val("lsy1"))]
    cfin = E.val("V")
    for st2 in range(2):
        pst = ps_h if st2 == 0 else ps_g

        def mmc(b, st2=st2, pst=pst):
            ins = None
            for mw in range(2):
                for cc in range(8):
                    b.matmul(pst[:, mw, :],
                             c_bf[:, cc, 128 * st2:128 * (st2 + 1)],
                             wsb["o"][1][:, cc, 512 * mw:512 * (mw + 1)],
                             start=(cc == 0), stop=False)
                ins = b.matmul(pst[:, mw, :], ones2[:, :],
                               bo2[0:2, 512 * mw:512 * (mw + 1)],
                               start=False, stop=True)
            return ins
        E.emit("tensor", mmc, wait=[("V", cfin)], inc="T")
        tv = E.val("T")
        E.emit("vector", lambda b, st2=st2, pst=pst: b.tensor_copy(
            cfuo[:, st2, :], pst[:, :, :].rearrange("p a s -> p (a s)")),
            wait=[("T", tv)], inc="V")
    cfv = E.val("V")
    E.barrier([("V", cfv)], engines=("tensor",))
    if "nohh" in DBG:
        st.close()
        return [("lsy0", E.val("lsy0")), ("lsy1", E.val("lsy1"))]
    xlast = [0, 0]
    olast = [0, 0]
    hlast = [0, 0]
    for t in range(nstep):
        p = t % 2
        ysm = f"lsy{p}"
        w = [("T", ylast[p])] if ylast[p] else []
        E.emit("sync", lambda b, t=t, p=p: b.dma_start(
            out=y2[:, p, :, :], in_=yin_ap[:, :, S2 * t:S2 * (t + 1)]),
            wait=w, inc=ysm, amt=16)
        yv = E.val(ysm)
        for st2 in range(2):
            pp = st2
            row0 = S2 * t + 128 * st2
            sp = (t * 2 + st2) % 2
            wl = [(ysm, yv)]
            if hlast[pp]:
                wl.append(("S", hlast[pp]))

            def mmh(b, st2=st2, p=p, pp=pp):
                ins = None
                for mw in range(2):
                    for cc in range(8):
                        ins = b.matmul(
                            ps_h[:, pp, :] if mw == 0 else ps_h[:, pp, :],
                            y2[:, p, cc, 128 * st2:128 * (st2 + 1)],
                            wsb["o"][0][:, cc, 512 * mw:512 * (mw + 1)],
                            start=(cc == 0), stop=(cc == 7))
                return ins
            def mmh2(b, st2=st2, p=p, pp=pp):
                ins = None
                for cc in range(8):
                    b.matmul(ps_h[:, pp, :],
                             y2[:, p, cc, 128 * st2:128 * (st2 + 1)],
                             wsb["o"][0][:, cc, 0:512],
                             start=(cc == 0), stop=(cc == 7))
                for cc in range(8):
                    ins = b.matmul(ps_g[:, pp, :],
                                   y2[:, p, cc, 128 * st2:128 * (st2 + 1)],
                                   wsb["o"][0][:, cc, 512:1024],
                                   start=(cc == 0), stop=(cc == 7))
                return ins
            E.emit("tensor", mmh2, wait=wl, inc="T")
            tv = E.val("T")
            ylast[p] = tv
            E.emit("vector", lambda b, st2=st2, pp=pp: b.tensor_tensor(
                ps_h[:, pp, :], ps_h[:, pp, :], cfuo[:, st2, 0:512], ALU.add),
                wait=[("T", tv)])
            E.emit("vector", lambda b, st2=st2, pp=pp: b.tensor_tensor(
                ps_g[:, pp, :], ps_g[:, pp, :], cfuo[:, st2, 512:1024],
                ALU.add), inc="V")
            av = E.val("V")
            xsm, osm = f"lsx{sp}", f"lso{sp}"
            wx = [("V", xlast[sp])] if xlast[sp] else []
            E.emit("sync", lambda b, row0=row0, sp=sp: b.dma_start(
                out=xr2[:, sp, :], in_=xin_ap[row0:row0 + 128, :]),
                wait=wx, inc=xsm, amt=16)
            xv = E.val(xsm)
            E.emit("scalar", lambda b, sp=sp, pp=pp: b.activation(
                o2[:, sp, 0:512], ps_h[:, pp, :], AF.Tanh),
                wait=[("V", av)], inc="S")
            E.emit("scalar", lambda b, sp=sp, pp=pp: b.activation(
                o2[:, sp, 512:1024], ps_g[:, pp, :], AF.Tanh), inc="S")
            sv = E.val("S")
            hlast[pp] = sv
            wl = [("S", sv), (xsm, xv)]
            if olast[sp]:
                wl.append((osm, olast[sp]))
            E.emit("vector", lambda b, sp=sp: b.tensor_tensor(
                o2[:, sp, :], o2[:, sp, :], xr2[:, sp, :], ALU.add),
                wait=wl, inc="V")
            ov = E.val("V")
            xlast[sp] = ov
            E.emit("sync", lambda b, row0=row0, sp=sp: b.dma_start(
                out=out_ap[row0:row0 + 128, :], in_=o2[:, sp, :]),
                wait=[("V", ov)], inc=osm, amt=16)
            olast[sp] = E.val(osm)
    st.close()
    return [("lso0", E.val("lso0")), ("lso1", E.val("lso1"))]


def build_lstm_program():
    nc = bass.Bass("TRN2", target_bir_lowering=False)
    xin = nc.dram_tensor("xin", [NSTEP * SEQ2, C], F32, kind="ExternalInput").ap()
    yin = nc.dram_tensor("yin", [128, 8, NSTEP * SEQ2], BF16,
                         kind="ExternalInput").ap()
    w_aps = {}
    for g in ("f", "i", "c", "o"):
        w = nc.dram_tensor(f"w{g}", [128, 8, C], F32, kind="ExternalInput").ap()
        u = nc.dram_tensor(f"u{g}", [128, 8, C], F32, kind="ExternalInput").ap()
        w_aps[g] = (w, u)
    b_aps = {
        "bcols": nc.dram_tensor("bcols", [128, 24], F32, kind="ExternalInput").ap(),
        "bo": nc.dram_tensor("bo", [1, C], F32, kind="ExternalInput").ap(),
        "inc": nc.dram_tensor("incc", [128, 8], F32, kind="ExternalInput").ap(),
    }
    out = nc.dram_tensor("out", [NSTEP * SEQ2, C], F32, kind="ExternalOutput").ap()
    E = Emitter(nc)
    E.con = make_consts(nc)
    done = emit_lstm_block(E, xin, yin, out, w_aps, b_aps, {})
    E.emit("sync", lambda b: None, wait=done)
    E.finalize()
    return nc


def prep_lstm_weights(wf, uf, bf, wi, ui, bi, wc, uc, bc, wo, uo, bo, in_c):
    w_aps = {}
    for g, (w, u) in (("f", (wf, uf)), ("i", (wi, ui)), ("c", (wc, uc)),
                      ("o", (wo, uo))):
        w_aps[f"w{g}"] = _chunk_lhst(np.ascontiguousarray(np.asarray(w).T))
        w_aps[f"u{g}"] = _chunk_lhst(np.ascontiguousarray(np.asarray(u).T))
    bcols = np.concatenate([_col_t(np.asarray(bf)), _col_t(np.asarray(bi)),
                            _col_t(np.asarray(bc))], axis=1)
    bo_r = np.ascontiguousarray(np.asarray(bo).reshape(1, C))
    incc = _col_t(np.asarray(in_c))
    return w_aps, bcols, bo_r, incc



def _chunk_lhst(a, nch=8):
    n = a.shape[1]
    return np.ascontiguousarray(a.reshape(nch, 128, n).transpose(1, 0, 2))


def _col_t(v):
    k = v.shape[0] // 128
    return np.ascontiguousarray(v.reshape(k, 128).T)


def prep_attn0_weights(w_attn, b_attn, w_proj, b_proj):
    wqkv = _chunk_lhst(np.asarray(w_attn))
    bvec = _col_t(np.asarray(b_attn))
    wo = _chunk_lhst(np.asarray(w_proj))
    bo = np.ascontiguousarray(np.asarray(b_proj).reshape(1, C))
    return wqkv, bvec, wo, bo


def prep_attn1_weights(W_Q, b_Q, W_K, b_K, W_V, b_V, W_O, b_O):
    wq = _chunk_lhst(np.ascontiguousarray(
        np.asarray(W_Q).transpose(1, 0, 2).reshape(C, C)))
    wk = _chunk_lhst(np.ascontiguousarray(
        np.asarray(W_K).transpose(1, 0, 2).reshape(C, C)))
    wv = _chunk_lhst(np.ascontiguousarray(
        np.asarray(W_V).transpose(1, 0, 2).reshape(C, C)))
    wo = _chunk_lhst(np.ascontiguousarray(np.asarray(W_O).reshape(C, C)))
    bvec = np.concatenate([_col_t(np.asarray(b_Q).reshape(-1)),
                           _col_t(np.asarray(b_K).reshape(-1)),
                           _col_t(np.asarray(b_V).reshape(-1))], axis=1)
    bo = np.ascontiguousarray(np.asarray(b_O).reshape(1, C))
    return wq, wk, wv, wo, bvec, bo


def prep_ffn_weights(w_gate, w_up, w_down):
    wg = _chunk_lhst(np.ascontiguousarray(w_gate.T))
    wu = _chunk_lhst(np.ascontiguousarray(w_up.T))
    wd = _chunk_lhst(np.ascontiguousarray(w_down.T), 24)
    return wg, wu, wd



def build_ffn_program(in_ln=True, out_ln=False):
    nc = bass.Bass("TRN2", target_bir_lowering=False)
    xin = nc.dram_tensor("xin", [TPC, C], F32, kind="ExternalInput").ap()
    wg = nc.dram_tensor("wg", [128, 8, FH], I32, kind="ExternalInput").ap()
    wu = nc.dram_tensor("wu", [128, 8, FH], I32, kind="ExternalInput").ap()
    wd = nc.dram_tensor("wd", [128, 24, C], I32, kind="ExternalInput").ap()
    xout = nc.dram_tensor("xout", [TPC, C], F32, kind="ExternalOutput").ap()
    E = Emitter(nc)
    E.con = make_consts(nc)
    if in_ln:
        ydram = nc.dram_tensor("ffn_ydram", [128, 8, TPC], BF16, kind="Internal")
        ln_done = emit_ln_pass(
            E, [xin[128 * i:128 * (i + 1), :] for i in range(TPC // 128)],
            ydram.ap(), "ln0")
        E.barrier(ln_done)
    else:
        ydram = nc.dram_tensor("yin", [128, 8, TPC], BF16, kind="ExternalInput")
    ffn_done = emit_ffn_block(E, xin, ydram.ap(), xout, wg, wu, wd, {})
    if out_ln:
        yout = nc.dram_tensor("yout", [128, 8, TPC], BF16,
                              kind="ExternalOutput").ap()
        E.barrier(ffn_done)
        ln2 = emit_ln_pass(
            E, [xout[128 * i:128 * (i + 1), :] for i in range(TPC // 128)],
            yout, "ln2")
        E.emit("sync", lambda b: None, wait=ln2 + ffn_done)
    else:
        E.emit("sync", lambda b: None, wait=ffn_done)
    E.finalize()
    return nc


def build_attn_program(layer, out_ln=True, in_ln=True, dbg_nqt=16, dbg_nb=BPC):
    nc = bass.Bass("TRN2", target_bir_lowering=False)
    xin = nc.dram_tensor("xin", [TPC, C], F32, kind="ExternalInput").ap()
    if layer == 0:
        wqkv = nc.dram_tensor("wqkv", [128, 8, 3 * C], F32,
                              kind="ExternalInput").ap()
        wq_ap, wk_ap, wv_ap = (wqkv[:, :, 0:C], wqkv[:, :, C:2 * C],
                               wqkv[:, :, 2 * C:3 * C])
    else:
        wq_ap = nc.dram_tensor("wq", [128, 8, C], F32, kind="ExternalInput").ap()
        wk_ap = nc.dram_tensor("wk", [128, 8, C], F32, kind="ExternalInput").ap()
        wv_ap = nc.dram_tensor("wv", [128, 8, C], F32, kind="ExternalInput").ap()
    wo = nc.dram_tensor("wo", [128, 8, C], F32, kind="ExternalInput").ap()
    bvec = nc.dram_tensor("bvec", [128, 24], F32, kind="ExternalInput").ap()
    bo = nc.dram_tensor("bo", [1, C], F32, kind="ExternalInput").ap()
    xout = nc.dram_tensor("xout", [TPC, C], F32, kind="ExternalOutput").ap()
    yout = nc.dram_tensor("yout", [128, 8, TPC], BF16,
                          kind="ExternalOutput").ap()
    if in_ln:
        ydram = nc.dram_tensor("att_ydram", [128, 8, TPC], BF16, kind="Internal")
    else:
        ydram = nc.dram_tensor("yin", [128, 8, TPC], BF16, kind="ExternalInput")
    E = Emitter(nc)
    E.con = make_consts(nc)
    E.dbgP_ap = nc.dram_tensor("dbgP", [128, 128], BF16, kind="ExternalOutput").ap()
    E.dbgA_ap = nc.dram_tensor("dbgA", [128, 8], F32, kind="ExternalOutput").ap()
    if layer == 0:
        b48, selc, btabc = make_alibi_consts(nc)
        E.con["alibi_b48"] = b48
        E.con["alibi_sel"] = selc
        E.con["alibi_btab"] = btabc
    if in_ln:
        ln_done = emit_ln_pass(
            E, [xin[128 * i:128 * (i + 1), :] for i in range(TPC // 128)],
            ydram.ap(), "lni")
        E.barrier(ln_done)
    at_done = emit_attn_block(E, xin, ydram.ap(), xout, wq_ap, wk_ap, wv_ap,
                              wo, bvec, bo, layer, "att", dbg_nqt=dbg_nqt,
                              dbg_nb=dbg_nb)
    E.barrier(at_done)
    if out_ln:
        ln2 = emit_ln_pass(
            E, [xout[128 * i:128 * (i + 1), :] for i in range(TPC // 128)],
            yout, "lno")
        E.emit("sync", lambda b: None, wait=ln2 + at_done)
    else:
        E.emit("sync", lambda b: None, wait=at_done)
    E.finalize()
    return nc


def _core_inmaps(inputs):
    x = np.ascontiguousarray(np.asarray(inputs["L0_attn_x"], np.float32))
    wqkv, bvec0, wo0, bo0 = prep_attn0_weights(
        inputs["L0_attn_w_attn"], inputs["L0_attn_b_attn"],
        inputs["L0_attn_w_proj"], inputs["L0_attn_b_proj"])
    wg, wu, wd = prep_ffn_weights(
        np.asarray(inputs["L0_ffn_w_gate"], np.int32),
        np.asarray(inputs["L0_ffn_w_up"], np.int32),
        np.asarray(inputs["L0_ffn_w_down"], np.int32))
    wq1, wk1, wv1, wo1, bvec1, bo1 = prep_attn1_weights(
        inputs["L1_attn_W_Q"], inputs["L1_attn_b_Q"],
        inputs["L1_attn_W_K"], inputs["L1_attn_b_K"],
        inputs["L1_attn_W_V"], inputs["L1_attn_b_V"],
        inputs["L1_attn_W_O"], inputs["L1_attn_b_O"])
    w_aps, bcols, bo_r, incc = prep_lstm_weights(
        inputs["L1_ffn_wf"], inputs["L1_ffn_uf"], inputs["L1_ffn_bf"],
        inputs["L1_ffn_wi"], inputs["L1_ffn_ui"], inputs["L1_ffn_bi"],
        inputs["L1_ffn_wc"], inputs["L1_ffn_uc"], inputs["L1_ffn_bc"],
        inputs["L1_ffn_wo"], inputs["L1_ffn_uo"], inputs["L1_ffn_bo"],
        inputs["L1_ffn_in_c"])
    return dict(x=x, wqkv=wqkv, bvec0=bvec0, wo0=wo0, bo0=bo0,
                wg=wg, wu=wu, wd=wd, wq1=wq1, wk1=wk1, wv1=wv1, wo1=wo1,
                bvec1=bvec1, bo1=bo1, lstm_w=w_aps, bcols=bcols, bo_r=bo_r,
                incc=incc)


def build_fused_program():
    nc = bass.Bass("TRN2", target_bir_lowering=False)
    xin = nc.dram_tensor("xin", [TPC, C], F32, kind="ExternalInput").ap()
    wqkv = nc.dram_tensor("wqkv", [128, 8, 3 * C], F32, kind="ExternalInput").ap()
    wo0 = nc.dram_tensor("wo0", [128, 8, C], F32, kind="ExternalInput").ap()
    bvec0 = nc.dram_tensor("bvec0", [128, 24], F32, kind="ExternalInput").ap()
    bo0 = nc.dram_tensor("bo0", [1, C], F32, kind="ExternalInput").ap()
    wg = nc.dram_tensor("wg", [128, 8, FH], I32, kind="ExternalInput").ap()
    wu = nc.dram_tensor("wu", [128, 8, FH], I32, kind="ExternalInput").ap()
    wd = nc.dram_tensor("wd", [128, 24, C], I32, kind="ExternalInput").ap()
    wq1 = nc.dram_tensor("wq1", [128, 8, C], F32, kind="ExternalInput").ap()
    wk1 = nc.dram_tensor("wk1", [128, 8, C], F32, kind="ExternalInput").ap()
    wv1 = nc.dram_tensor("wv1", [128, 8, C], F32, kind="ExternalInput").ap()
    wo1 = nc.dram_tensor("wo1", [128, 8, C], F32, kind="ExternalInput").ap()
    bvec1 = nc.dram_tensor("bvec1", [128, 24], F32, kind="ExternalInput").ap()
    bo1 = nc.dram_tensor("bo1", [1, C], F32, kind="ExternalInput").ap()
    w_aps = {}
    for g in ("f", "i", "c", "o"):
        w = nc.dram_tensor(f"w{g}", [128, 8, C], F32, kind="ExternalInput").ap()
        u = nc.dram_tensor(f"u{g}", [128, 8, C], F32, kind="ExternalInput").ap()
        w_aps[g] = (w, u)
    b_aps = {
        "bcols": nc.dram_tensor("bcols", [128, 24], F32, kind="ExternalInput").ap(),
        "bo": nc.dram_tensor("bo", [1, C], F32, kind="ExternalInput").ap(),
        "inc": nc.dram_tensor("incc", [128, 8], F32, kind="ExternalInput").ap(),
    }
    out = nc.dram_tensor("out", [NSTEP * SEQ2, C], F32, kind="ExternalOutput").ap()
    x1 = nc.dram_tensor("x1s", [TPC, C], F32, kind="Internal").ap()
    x2 = nc.dram_tensor("x2s", [TPC, C], F32, kind="Internal").ap()
    x3 = nc.dram_tensor("x3s", [TPC, C], F32, kind="Internal")
    x4 = nc.dram_tensor("x4s", [TPC, C], F32, kind="Internal")
    ya = nc.dram_tensor("yas", [128, 8, TPC], BF16, kind="Internal").ap()
    yb = nc.dram_tensor("ybs", [128, 8, TPC], BF16, kind="Internal").ap()

    E = Emitter(nc)
    E.con = make_consts(nc)
    b48, selc, btabc = make_alibi_consts(nc)
    E.con["alibi_b48"] = b48
    E.con["alibi_sel"] = selc
    E.con["alibi_btab"] = btabc

    tiles = lambda ap: [ap[128 * i:128 * (i + 1), :] for i in range(TPC // 128)]
    d = emit_ln_pass(E, tiles(xin), ya, "ln0")
    E.barrier(d)
    d = emit_attn_block(E, xin, ya, x1, wqkv[:, :, 0:C], wqkv[:, :, C:2 * C],
                        wqkv[:, :, 2 * C:3 * C], wo0, bvec0, bo0, 0, "a0")
    E.barrier(d)
    d = emit_ln_pass(E, tiles(x1), yb, "ln1")
    E.barrier(d)
    d = emit_ffn_block(E, x1, yb, x2, wg, wu, wd, {})
    E.barrier(d)
    d = emit_ln_pass(E, tiles(x2), ya, "ln2")
    E.barrier(d)

    def a2a_row(bb, qt):
        t0 = 128 * qt
        return (t0 // SEQ2) * 2 * SEQ2 + bb * SEQ2 + (t0 % SEQ2)
    d = emit_attn_block(E, x2, ya, x3.ap(), wq1, wk1, wv1, wo1, bvec1, bo1,
                        1, "a1", out_row_fn=a2a_row)
    E.barrier(d)
    E.emit("gpsimd", lambda b: b.collective_compute(
        "AllToAll", ALU.bypass, replica_groups=[list(range(NCORES))],
        ins=[x3.ap().opt()], outs=[x4.ap().opt()]), inc="CC", amt=1)
    E.barrier([("CC", 1)])
    d = emit_ln_pass(E, tiles(x4.ap()), yb, "ln3")
    E.barrier(d)
    d = emit_lstm_block(E, x4.ap(), yb, out, w_aps, b_aps, {})
    E.emit("sync", lambda b: None, wait=d)
    E.finalize()
    return nc


def kernel_fused(**inputs):
    from concourse.aot_env import allow_ici_all_to_all
    pp = _core_inmaps(inputs)
    x = pp["x"]
    cores = list(range(NCORES))
    with allow_ici_all_to_all():
        nc = build_fused_program()
        im = []
        for r in cores:
            m = {"xin": np.ascontiguousarray(x[2 * r:2 * r + 2].reshape(TPC, C)),
                 "wqkv": pp["wqkv"], "wo0": pp["wo0"], "bvec0": pp["bvec0"],
                 "bo0": pp["bo0"], "wg": pp["wg"], "wu": pp["wu"], "wd": pp["wd"],
                 "wq1": pp["wq1"], "wk1": pp["wk1"], "wv1": pp["wv1"],
                 "wo1": pp["wo1"], "bvec1": pp["bvec1"], "bo1": pp["bo1"],
                 "bcols": pp["bcols"], "bo": pp["bo_r"], "incc": pp["incc"]}
            m.update(pp["lstm_w"])
            im.append(m)
        res = run_bass_kernel_spmd(nc, im, core_ids=cores).results
    out = np.empty((NSTEP, NCORES * SEQ2, C), np.float32)
    for r in cores:
        out[:, SEQ2 * r:SEQ2 * (r + 1), :] = res[r]["out"].reshape(
            NSTEP, SEQ2, C)
    return out


def kernel(**inputs):
    _ = inputs.get("L1_attn_pos_embed")
    return kernel_fused(**inputs)


def kernel_unfused(**inputs):
    pp = _core_inmaps(inputs)
    x = pp["x"]
    cores = list(range(NCORES))
    nc1 = build_attn_program(0, out_ln=True, in_ln=True)
    im1 = [{"xin": np.ascontiguousarray(x[2 * r:2 * r + 2].reshape(TPC, C)),
            "wqkv": pp["wqkv"], "wo": pp["wo0"], "bvec": pp["bvec0"],
            "bo": pp["bo0"]} for r in cores]
    r1 = run_bass_kernel_spmd(nc1, im1, core_ids=cores).results
    nc2 = build_ffn_program(in_ln=False, out_ln=True)
    im2 = [{"xin": r1[r]["xout"], "yin": r1[r]["yout"], "wg": pp["wg"],
            "wu": pp["wu"], "wd": pp["wd"]} for r in cores]
    r2 = run_bass_kernel_spmd(nc2, im2, core_ids=cores).results
    nc3 = build_attn_program(1, out_ln=True, in_ln=False)
    im3 = [{"xin": r2[r]["xout"], "yin": r2[r]["yout"], "wq": pp["wq1"],
            "wk": pp["wk1"], "wv": pp["wv1"], "wo": pp["wo1"],
            "bvec": pp["bvec1"], "bo": pp["bo1"]} for r in cores]
    r3 = run_bass_kernel_spmd(nc3, im3, core_ids=cores).results
    x3 = np.stack([r3[r]["xout"] for r in cores])
    y3 = np.stack([r3[r]["yout"] for r in cores])
    nc4 = build_lstm_program()
    im4 = []
    for r in cores:
        xs = np.empty((NSTEP * SEQ2, C), np.float32)
        ys = np.empty((128, 8, NSTEP * SEQ2), y3.dtype)
        for t in range(NSTEP):
            src_r = t // 2
            off = (t % 2) * TB + SEQ2 * r
            xs[SEQ2 * t:SEQ2 * (t + 1)] = x3[src_r][off:off + SEQ2]
            ys[:, :, SEQ2 * t:SEQ2 * (t + 1)] = y3[src_r][:, :, off:off + SEQ2]
        im4.append(dict(xin=xs, yin=np.ascontiguousarray(ys),
                        bcols=pp["bcols"], bo=pp["bo_r"], incc=pp["incc"],
                        **pp["lstm_w"]))
    r4 = run_bass_kernel_spmd(nc4, im4, core_ids=cores).results
    out = np.empty((NSTEP, NCORES * SEQ2, C), np.float32)
    for r in cores:
        out[:, SEQ2 * r:SEQ2 * (r + 1), :] = r4[r]["out"].reshape(
            NSTEP, SEQ2, C)
    return out
